# version 11
# baseline (speedup 1.0000x reference)
import numpy as np
from contextlib import ExitStack

import concourse.bass as bass
import concourse.mybir as mybir
from concourse.bass_utils import run_bass_kernel_spmd

F32 = mybir.dt.float32
BF16 = mybir.dt.bfloat16
I32 = mybir.dt.int32
ALU = mybir.AluOpType
ACT = mybir.ActivationFunctionType

B, L, D = 2, 4096, 4096
H, PH, N, G, KC = 64, 64, 128, 8, 4
I = H * PH
PROJ = 10304
TOK = 1024
NT = 32
Q = 128
NCH = 8
FI = 11008
FIP = 11264
NTC = 11
EPS = 1e-5
NEG = -1e30

ENGINES = ("tensor", "vector", "scalar", "gpsimd", "sync")


class Sched:
    def __init__(self, nc, ctx):
        self.nc = nc
        self.ctx = ctx
        self.plan = True
        self.eng = None
        self.h = None
        self.cnt = {}
        self.marks = {}
        self.last_wait = {}
        self.used = set()
        self.sems = {}
        self.sem_eng = {}

    def init_sems(self, ctx):
        pass

    def _sem(self, name):
        if name not in self.sems:
            self.sems[name] = self.ctx.enter_context(self.nc.semaphore(f"s_{name}"))
            self.cnt[name] = 0
        return self.sems[name]

    def drain(self, snap):
        for sname, eng2 in list(self.sem_eng.items()):
            if eng2 != self.eng:
                continue
            v = self.cnt.get(sname, 0)
            if v > snap.get(sname, 0):
                lw = self.last_wait.get((self.eng, sname), 0)
                if v > lw:
                    self.h.wait_ge(self.sems[sname], v)
                    self.last_wait[(self.eng, sname)] = v

    def op(self, eng, fn, key=None, deps=(), step=1, async_=False, sem=None):
        isasync = async_ or step == 16
        semname = sem if sem is not None else eng
        if step == 16 or async_:
            assert sem is not None, (key, "DMA/collective ops must name a sem")
        if self.plan:
            self.used.add(eng)
            if key is not None or isasync:
                self._sem(semname)
                self.cnt[semname] += step
                if isasync:
                    self.sem_eng[semname] = eng
                if key is not None:
                    assert key not in self.marks, key
                    self.marks[key] = (eng, semname, self.cnt[semname], isasync)
            return
        if eng != self.eng:
            return
        want = {}
        for d in deps:
            if d is None or d not in self.marks:
                continue
            peng, psem, pval, pasync = self.marks[d]
            if pval > want.get(psem, 0):
                want[psem] = pval
        for psem, pval in want.items():
            lw = self.last_wait.get((eng, psem), 0)
            if pval > lw:
                self.h.wait_ge(self.sems[psem], pval)
                self.last_wait[(eng, psem)] = pval
        inst = fn(self.h)
        if (key is not None or isasync) and inst is not None:
            inst.then_inc(self.sems[semname], step)

    def run_phase(self, name, body):
        self.plan = True
        self.used = set()
        snap = dict(self.cnt)
        body()
        self.plan = False
        used = set(self.used)
        with self.nc.Block(name) as blk:
            for e in ENGINES:
                if e not in used:
                    continue

                def make(e2):
                    def f(h):
                        self.eng, self.h = e2, h
                        body()
                        self.drain(snap)
                    return f
                getattr(blk, e)(make(e))


UPTO = 99
P2D_CHUNKS = NCH
P2D_HEADS = H
P2D_PRE = 15
P2D_OPS = 1023


def build_nc():
    nc = bass.Bass("TRN2", target_bir_lowering=False, debug=False)
    dram, sb, ps = {}, {}, {}

    def din(name, shape, dt):
        dram[name] = nc.dram_tensor(name, list(shape), dt, kind="ExternalInput").ap()

    def dint(name, shape, dt):
        dram[name] = nc.dram_tensor(name, list(shape), dt).ap()

    din("xT", (D, TOK), F32)
    WIN_W = [128] * 80 + [64]
    din("w_in_t", (128, 32 * PROJ), F32)
    din("w_out_t", (128, 32 * D), F32)
    din("conv_w", (128, 48 * 4), F32)
    din("conv_b", (128, 48), F32)
    din("dt_bias", (64, 1), F32)
    din("a_log", (64, 1), F32)
    din("d_exp", (128, 32), F32)
    din("norm_w", (128, 32), F32)
    din("m_prev", (128, 1), F32)
    din("prevsel", (128, 8), F32)
    din("msel", (64, 8), F32)
    din("macc", (128, 8), F32)
    din("gq", (128, 32 * 176), I32)
    din("uq", (128, 32 * 176), I32)
    din("gsc", (32, NTC * 128), F32)
    din("usc", (32, NTC * 128), F32)
    din("gzq", (32, 176), I32)
    din("uzq", (32, 176), I32)
    din("dq", (128, NTC * 512), I32)
    din("dsc", (NTC, D), F32)
    din("dzq", (NTC, 512), I32)
    dram["y_out"] = nc.dram_tensor("y_out", [D, TOK], F32, kind="ExternalOutput").ap()
    dint("gate_spill", (32, 128, TOK), BF16)
    dint("x_spill", (32, NCH, 128, Q), BF16)
    dint("bd_in", (128, 144), F32)
    dint("bd_out", (8 * 128, 144), F32)
    dint("a_c_d", (64, TOK), F32)
    dint("a_g_d", (64, TOK), F32)
    dint("ags_in", (128, 4097), F32)
    dint("ags_out", (8 * 128, 4097), F32)
    for j in range(4):
        dint(f"agx_in{j}", (128, NT * 256), BF16)
        dint(f"agx_out{j}", (8 * 128, NT * 256), BF16)
    dint("xl1_spill", (128, NT * TOK), BF16)
    dint("resid_spill", (32, 128, TOK), BF16)
    dint("zs_gu_d", (32, NTC * 128), F32)
    dint("zs_d_d", (NTC, D), F32)
    dint("g_spill", (16, 128, NTC * 512), BF16)
    dint("h_spill", (16, 128, NTC * 512), BF16)
    for j in range(4):
        dint(f"rs_in{j}", (8 * 1024, TOK), BF16)
        dint(f"rs_out{j}", (1024, TOK), BF16)

    ctx = ExitStack()
    S = Sched(nc, ctx)

    def sbuf(name, shape, dt, st=None):
        sb[name] = (st or ctx).enter_context(nc.sbuf_tensor(name, list(shape), dt))
        return sb[name]

    def psum(name, shape, dt, st=None):
        ps[name] = (st or ctx).enter_context(nc.psum_tensor(name, list(shape), dt))
        return ps[name]

    with ctx:
        S.init_sems(ctx)

        sbuf("eyef", (128, 128), F32)
        sbuf("eyeb", (128, 128), BF16)
        sbuf("trineg", (128, 128), F32)
        sbuf("onesrf", (1, 128), F32)
        sbuf("onescb", (128, 1), BF16)
        sbuf("ones64", (64, 128), F32)
        sbuf("cw", (128, 192), F32)
        sbuf("cb", (128, 48), F32)
        sbuf("dtb", (64, 1), F32)
        sbuf("alog", (64, 1), F32)
        sbuf("aneg", (64, 1), F32)
        sbuf("dexp", (128, 32), F32)
        sbuf("nw", (128, 32), F32)
        sbuf("mprev", (128, 1), F32)
        sbuf("s_prevsel", (128, 8), F32)
        sbuf("s_msel", (64, 8), F32)
        sbuf("s_macc", (128, 8), F32)
        sbuf("epsb", (128, 1), F32)
        nc.const_aps.register(F32, EPS, sb["epsb"][:, 0:1])

        stA = ExitStack()
        sbuf("BC", (128, 16 * TOK), BF16, stA)
        sbuf("dtp", (64, TOK), F32, stA)
        sbuf("a_c", (64, TOK), F32, stA)
        sbuf("a_g", (64, TOK), F32, stA)
        sbuf("expa", (64, TOK), F32, stA)
        sbuf("aT", (128, NCH * 64), F32, stA)
        sbuf("dtT", (128, NCH * 64), F32, stA)
        sbuf("hbchead", (128, 48 * 4), F32, stA)
        sbuf("hbctail", (128, 48 * 3), F32, stA)
        sbuf("atot", (64, 1), F32, stA)
        sbuf("arow0", (1, 8 * Q), F32, stA)
        sbuf("arow1", (1, 8 * Q), F32, stA)
        sb["dtraw"] = sb["dtp"]
        sb["expag"] = sb["a_g"]

        def p0():
            for nm, src in (("cw", "conv_w"), ("cb", "conv_b"), ("dtb", "dt_bias"),
                            ("alog", "a_log"), ("dexp", "d_exp"), ("nw", "norm_w"),
                            ("mprev", "m_prev"), ("s_prevsel", "prevsel"),
                            ("s_msel", "msel"), ("s_macc", "macc")):
                S.op("gpsimd", lambda g, nm=nm, src=src: g.dma_start(out=sb[nm][:, :], in_=dram[src]),
                     key=("ld", nm), step=16, sem="c0")
            S.op("gpsimd", lambda g: g.iota(sb["eyef"][:, :], [[-1, 128]], base=0,
                                            channel_multiplier=1, allow_small_or_imprecise_dtypes=True),
                 key=("iota", 0))
            S.op("gpsimd", lambda g: g.iota(sb["trineg"][:, :], [[-1, 128]], base=0,
                                            channel_multiplier=1, allow_small_or_imprecise_dtypes=True),
                 key=("iota", 1))
            S.op("gpsimd", lambda g: g.memset(sb["onesrf"][:, :], 1.0), key=("ms", 0))
            S.op("gpsimd", lambda g: g.memset(sb["onescb"][:, :], 1.0), key=("ms", 1))
            S.op("gpsimd", lambda g: g.memset(sb["ones64"][:, :], 1.0), key=("ms", 2))
            S.op("gpsimd", lambda g: g.memset(sb["epsb"][:, :], EPS), key=("ms", 7))
            S.op("vector", lambda v: v.tensor_single_scalar(sb["eyef"][:, :], sb["eyef"][:, :], 0.0, ALU.is_equal),
                 key=("eyef",), deps=[("iota", 0)])
            S.op("vector", lambda v: v.tensor_copy(sb["eyeb"][:, :], sb["eyef"][:, :]),
                 key=("eyeb",), deps=[("eyef",)])
            S.op("vector", lambda v: v.tensor_single_scalar(sb["trineg"][:, :], sb["trineg"][:, :], 0.0, ALU.is_le),
                 key=("tri0",), deps=[("iota", 1)])
            S.op("vector", lambda v: v.tensor_scalar(sb["trineg"][:, :], sb["trineg"][:, :], -NEG, -NEG,
                                                     ALU.mult, ALU.subtract),
                 key=("tri1",), deps=[("tri0",)])
            S.op("scalar", lambda s: s.activation(sb["aneg"][:, :], sb["alog"][:, :], ACT.Exp),
                 key=("an0",), deps=[("ld", "s_macc")])
            S.op("vector", lambda v: v.tensor_single_scalar(sb["aneg"][:, :], sb["aneg"][:, :], -1.0, ALU.mult),
                 key=("an1",), deps=[("an0",)])

        if UPTO >= 0:
            S.run_phase("p0", p0)

        st1 = ExitStack()
        sbuf("xTb", (128, NT * TOK), BF16, st1)
        sbuf("wbuf0", (128, 32 * 128), BF16, st1)
        sbuf("wbuf1", (128, 32 * 128), BF16, st1)
        for i in range(2):
            sbuf(f"hbcf{i}", (128, TOK), F32, st1)
            sbuf(f"ctmp{i}", (128, TOK), F32, st1)
            sbuf(f"gstage{i}", (128, TOK), BF16, st1)
        for i in range(2):
            sbuf(f"xstage{i}", (128, TOK), BF16, st1)
        psum("pin0", (128, 512), F32, st1)
        psum("pin1", (128, 512), F32, st1)

        win_off = np.concatenate([[0], np.cumsum([32 * w for w in WIN_W])]).astype(int)
        p1_order = list(range(32, 80)) + [80] + list(range(0, 32))

        def p1():
            for k in range(NT):
                S.op("gpsimd", lambda g, k=k: g.dma_start(
                    out=sb["xTb"][:, k * TOK:(k + 1) * TOK],
                    in_=dram["xT"][k * 128:(k + 1) * 128, :]), key=("xld", k), step=16, sem="xld")
            for idx, pi in enumerate(p1_order):
                pw = WIN_W[pi]
                buf = sb[f"wbuf{idx % 2}"]
                dep = [("p1mm", p1_order[idx - 2], 1)] if idx >= 2 else []
                S.op("gpsimd", lambda g, pi=pi, buf=buf: g.dma_start(
                    out=buf[:, 0:32 * WIN_W[pi]],
                    in_=dram["w_in_t"][:, int(win_off[pi]):int(win_off[pi + 1])]),
                    key=("wld", pi), deps=dep, step=16, sem=f"wld{idx % 2}")
                for half in range(2):
                    pst = ps[f"pin{half}"]
                    mmdeps = [("wld", pi)] + ([("xld", k) for k in range(NT)] if idx == 0 else [])
                    if idx >= 1:
                        mmdeps.append(("p1ev", p1_order[idx - 1], half))
                    for k in range(NT):
                        last = k == NT - 1
                        S.op("tensor", lambda t, pst=pst, buf=buf, k=k, pw=pw, half=half: t.matmul(
                            pst[0:pw, :], buf[:, k * pw:(k + 1) * pw],
                            sb["xTb"][:, k * TOK + half * 512: k * TOK + half * 512 + 512],
                            start=(k == 0), stop=last),
                            key=(("p1mm", pi, half) if last else None),
                            deps=(mmdeps if k == 0 else []))
                if 32 <= pi < 80:
                    T = pi - 32
                    hf = sb[f"hbcf{T % 2}"]
                    tmp = sb[f"ctmp{T % 2}"]
                    guard = [("p1silu", T - 2)] if T >= 2 else []
                    for half in range(2):
                        S.op("scalar", lambda s, hf=hf, half=half: s.copy(
                            hf[:, half * 512:half * 512 + 512], ps[f"pin{half}"][:, :]),
                            key=("p1ev", pi, half), deps=[("p1mm", pi, half)] + guard)
                    S.op("vector", lambda v, hf=hf, T=T: v.tensor_copy(sb["hbchead"][:, T * 4:T * 4 + 4], hf[:, 0:4]),
                         key=("p1hd", T), deps=[("p1ev", pi, 0)])
                    S.op("vector", lambda v, hf=hf, T=T: v.tensor_copy(sb["hbctail"][:, T * 3:T * 3 + 3], hf[:, 1021:1024]),
                         key=("p1tl", T), deps=[("p1ev", pi, 1)])
                    S.op("vector", lambda v, tmp=tmp, hf=hf, T=T: v.scalar_tensor_tensor(
                        tmp[:, :], hf[:, :], sb["cw"][:, T * 4 + 3:T * 4 + 4],
                        sb["cb"][:, T:T + 1].broadcast_to((128, TOK)), ALU.mult, ALU.add),
                        key=("p1c0", T), deps=[("p1ev", pi, 0), ("p1ev", pi, 1)])
                    for kk, wi in ((1, 2), (2, 1), (3, 0)):
                        S.op("vector", lambda v, tmp=tmp, hf=hf, T=T, kk=kk, wi=wi: v.scalar_tensor_tensor(
                            tmp[:, kk:TOK], hf[:, 0:TOK - kk], sb["cw"][:, T * 4 + wi:T * 4 + wi + 1],
                            tmp[:, kk:TOK], ALU.mult, ALU.add),
                            key=("p1c", T, kk), deps=[("p1c0", T) if kk == 1 else ("p1c", T, kk - 1)])
                    if T < 32:
                        xst = sb[f"xstage{T % 2}"]
                        sdep = [("p1c", T, 3), ("p1hd", T), ("p1tl", T)]
                        if T >= 2:
                            sdep.append(("p1xs", T - 2, NCH - 1))
                        S.op("scalar", lambda s, xst=xst, tmp=tmp: s.activation(xst[:, :], tmp[:, :], ACT.Silu),
                             key=("p1silu", T), deps=sdep)
                        for c in range(NCH):
                            S.op("sync", lambda s, xst=xst, T=T, c=c: s.dma_start(
                                out=dram["x_spill"][T, c], in_=xst[:, c * Q:(c + 1) * Q]),
                                key=("p1xs", T, c), deps=[("p1silu", T)], step=16, sem=f"xsp{T % 2}")
                    else:
                        bc = T - 32
                        S.op("scalar", lambda s, tmp=tmp, bc=bc: s.activation(
                            sb["BC"][:, bc * TOK:(bc + 1) * TOK], tmp[:, :], ACT.Silu),
                            key=("p1silu", T), deps=[("p1c", T, 3), ("p1hd", T), ("p1tl", T)])
                elif pi == 80:
                    for half in range(2):
                        S.op("scalar", lambda s, half=half: s.copy(
                            sb["dtraw"][:, half * 512:half * 512 + 512], ps[f"pin{half}"][0:64, :]),
                            key=("p1ev", pi, half), deps=[("p1mm", pi, half)])
                else:
                    gst = sb[f"gstage{pi % 2}"]
                    guard = [("p1gs", pi - 2)] if pi >= 2 else []
                    for half in range(2):
                        S.op("scalar", lambda s, gst=gst, half=half: s.copy(
                            gst[:, half * 512:half * 512 + 512], ps[f"pin{half}"][:, :]),
                            key=("p1ev", pi, half), deps=[("p1mm", pi, half)] + guard)
                    S.op("sync", lambda s, gst=gst, pi=pi: s.dma_start(
                        out=dram["gate_spill"][pi], in_=gst[:, :]),
                        key=("p1gs", pi), deps=[("p1ev", pi, 0), ("p1ev", pi, 1)], step=16, sem=f"gsp{pi % 2}")

        if UPTO >= 1:
            S.run_phase("p1", p1)
        st1.close()

        stY = ExitStack()
        sbuf("y", (128, NT * TOK), BF16, stY)
        st2 = ExitStack()
        sbuf("Scur", (128, H * 64), F32, st2)
        sbuf("Sbf0", (128, H * 64), BF16, st2)
        sbuf("Sbf1", (128, H * 64), BF16, st2)
        for i in range(2):
            sbuf(f"evt{i}", (128, 128), F32, st2)
            sbuf(f"Ep{i}", (128, 128), F32, st2)
        sbuf("erep", (128, 64), F32, st2)
        sbuf("erow", (1, 64), F32, st2)
        sbuf("aerep", (128, 64), F32, st2)
        sbuf("expfac", (128, 1), F32, st2)
        psum("ptrf0", (128, 512), F32, st2)
        psum("ptrf1", (128, 512), F32, st2)
        sb["Sin"] = sb["Scur"]
        sb["Sinb"] = sb["Sbf0"]
        st2b = ExitStack()
        sbuf("bdall", (128, 8 * 144), F32, st2b)
        sbuf("bdprev", (128, 144), F32, st2b)
        sbuf("fixt", (128, 3), F32, st2b)
        sbuf("fixs", (128, 3), BF16, st2b)

        def p2a():
            S.op("gpsimd", lambda g: g.dma_start(out=dram["bd_in"], in_=sb["hbctail"][:, :]),
                 key=("bdst",), step=16, sem="bdio")
            S.op("gpsimd", lambda g: g.collective_compute(
                "AllGather", ALU.bypass, replica_groups=[list(range(8))],
                ins=[dram["bd_in"].opt()], outs=[dram["bd_out"].opt()]),
                key=("bdcc",), deps=[("bdst",)], async_=True, sem="ccbd")
            S.op("gpsimd", lambda g: g.dma_start(
                out=sb["bdall"][:, :].rearrange("p (r w) -> p r w", r=8),
                in_=dram["bd_out"].rearrange("(r p) w -> p r w", p=128)),
                key=("bdld",), deps=[("bdcc",)], step=16, sem="bdio")
            S.op("vector", lambda v: v.memset(sb["bdprev"][:, :], 0.0), key=("bd0",))
            for j in range(8):
                S.op("vector", lambda v, j=j: v.scalar_tensor_tensor(
                    sb["bdprev"][:, :], sb["bdall"][:, j * 144:(j + 1) * 144],
                    sb["s_prevsel"][:, j:j + 1], sb["bdprev"][:, :], ALU.mult, ALU.add),
                    key=("bdacc", j), deps=[("bdld",), ("bd0",) if j == 0 else ("bdacc", j - 1)])
            S.op("vector", lambda v: v.tensor_single_scalar(sb["bdprev"][:, :], sb["bdprev"][:, :],
                                                            sb["mprev"][:, 0:1], ALU.mult),
                 key=("bdm",), deps=[("bdacc", 7)])

        if UPTO >= 2:
            S.run_phase("p2a", p2a)

        def p2b():
            for T in range(48):
                bd = (sb["bdprev"], T * 3)
                hh = (sb["hbchead"], T * 4)
                S.op("vector", lambda v, T=T, hh=hh: v.scalar_tensor_tensor(
                    sb["fixt"][:, :], hh[0][:, hh[1]:hh[1] + 3], sb["cw"][:, T * 4 + 3:T * 4 + 4],
                    sb["cb"][:, T:T + 1].broadcast_to((128, 3)), ALU.mult, ALU.add),
                    key=("f0", T), deps=([("bdm",)] if T == 0 else [("fxw", T - 1)]))
                steps = [
                    (2, [(bd, 2, 0, 1), (hh, 0, 1, 2)]),
                    (1, [(bd, 1, 0, 2), (hh, 0, 2, 1)]),
                    (0, [(bd, 0, 0, 3)]),
                ]
                pk = ("f0", T)
                for si, (wi, parts) in enumerate(steps):
                    for pj, (src, so, do, cnt) in enumerate(parts):
                        S.op("vector", lambda v, T=T, wi=wi, src=src, so=so, do=do, cnt=cnt: v.scalar_tensor_tensor(
                            sb["fixt"][:, do:do + cnt], src[0][:, src[1] + so:src[1] + so + cnt],
                            sb["cw"][:, T * 4 + wi:T * 4 + wi + 1], sb["fixt"][:, do:do + cnt],
                            ALU.mult, ALU.add),
                            key=("fx", T, si, pj), deps=[pk])
                        pk = ("fx", T, si, pj)
                S.op("scalar", lambda s: s.activation(sb["fixs"][:, :], sb["fixt"][:, :], ACT.Silu),
                     key=("fsl", T), deps=[pk])
                if T < 32:
                    S.op("sync", lambda s, T=T: s.dma_start(out=dram["x_spill"][T, 0][:, 0:3],
                                                            in_=sb["fixs"][:, :],
                                                            allow_slow_non_contiguous=True),
                         key=("fxw", T), deps=[("fsl", T)], step=16, sem="fxio")
                else:
                    S.op("vector", lambda v, T=T: v.tensor_copy(
                        sb["BC"][:, (T - 32) * TOK:(T - 32) * TOK + 3], sb["fixs"][:, :]),
                        key=("fxw", T), deps=[("fsl", T)])
            S.op("scalar", lambda s: s.activation(sb["dtp"][:, :], sb["dtraw"][:, :], ACT.Exp,
                                                  bias=sb["dtb"][:, 0:1]), key=("dtp0",))
            S.op("vector", lambda v: v.tensor_single_scalar(sb["dtp"][:, :], sb["dtp"][:, :], 1.0, ALU.add),
                 key=("dtp1",), deps=[("dtp0",)])
            S.op("scalar", lambda s: s.activation(sb["dtp"][:, :], sb["dtp"][:, :], ACT.Ln),
                 key=("dtp",), deps=[("dtp1",)])
            S.op("vector", lambda v: v.tensor_single_scalar(sb["a_c"][:, :], sb["dtp"][:, :],
                                                            sb["aneg"][:, 0:1], ALU.mult),
                 key=("dta",), deps=[("dtp",)])
            for c in range(NCH):
                S.op("vector", lambda v, c=c: v.tensor_tensor_scan(
                    sb["a_c"][:, c * Q:(c + 1) * Q], sb["ones64"][:, :], sb["a_c"][:, c * Q:(c + 1) * Q],
                    0.0, ALU.mult, ALU.add), key=("scan", c), deps=[("dta",)])
            S.op("vector", lambda v: v.memset(sb["atot"][:, :], 0.0), key=("off0",))
            for c in range(NCH):
                S.op("vector", lambda v, c=c: v.tensor_tensor(
                    sb["a_g"][:, c * Q:(c + 1) * Q], sb["a_c"][:, c * Q:(c + 1) * Q],
                    sb["atot"][:, 0:1].broadcast_to((64, Q)), ALU.add),
                    key=("ago", c), deps=[("scan", c), ("off0",) if c == 0 else ("offu", c - 1)])
                S.op("vector", lambda v, c=c: v.tensor_tensor(
                    sb["atot"][:, 0:1], sb["atot"][:, 0:1], sb["a_c"][:, (c + 1) * Q - 1:(c + 1) * Q], ALU.add),
                    key=("offu", c), deps=[("ago", c)])
            S.op("sync", lambda s: s.dma_start(out=dram["a_c_d"], in_=sb["a_c"][:, :]),
                 key=("acd",), deps=[("scan", NCH - 1)], step=16, sem="acd")
            S.op("sync", lambda s: s.dma_start(out=dram["a_g_d"], in_=sb["a_g"][:, :]),
                 key=("agd",), deps=[("ago", NCH - 1)], step=16, sem="agd")
            S.op("scalar", lambda s: s.activation(sb["expa"][:, :], sb["a_c"][:, :], ACT.Exp),
                 key=("expa",), deps=[("scan", NCH - 1)])
            S.op("scalar", lambda s: s.activation(sb["expag"][:, :], sb["a_g"][:, :], ACT.Exp),
                 key=("expag",), deps=[("ago", NCH - 1), ("agd",)])
            for c in range(NCH):
                pt = ps[f"ptrf{c % 2}"]
                dep = [("scan", c)] + ([("aTe", c - 2), ("dtTe", c - 2)] if c >= 2 else [])
                S.op("tensor", lambda t, c=c, pt=pt: t.transpose(
                    pt[:, 0:64], sb["a_c"][:, c * Q:(c + 1) * Q], sb["eyef"][0:64, 0:64]),
                    key=("aTt", c), deps=dep)
                S.op("vector", lambda v, c=c, pt=pt: v.tensor_copy(sb["aT"][:, c * 64:(c + 1) * 64], pt[:, 0:64]),
                     key=("aTe", c), deps=[("aTt", c)])
                S.op("tensor", lambda t, c=c, pt=pt: t.transpose(
                    pt[:, 64:128], sb["dtp"][:, c * Q:(c + 1) * Q], sb["eyef"][0:64, 0:64]),
                    key=("dtTt", c), deps=[("dtp",), ("aTe", c)])
                S.op("vector", lambda v, c=c, pt=pt: v.tensor_copy(sb["dtT"][:, c * 64:(c + 1) * 64], pt[:, 64:128]),
                     key=("dtTe", c), deps=[("dtTt", c)])

        if UPTO >= 3:
            S.run_phase("p2b", p2b)

        st2b.close()
        st2d = ExitStack()
        for i in range(2):
            sbuf(f"xc{i}", (128, 32 * Q), BF16, st2d)
        for i in range(4):
            sbuf(f"dbuf{i}", (128, 128), F32, st2d)
            sbuf(f"m2b{i}", (128, 128), BF16, st2d)
            sbuf(f"x1t{i}", (128, 64), BF16, st2d)
            sbuf(f"x2t{i}", (128, 64), BF16, st2d)
            sbuf(f"btb{i}", (128, 128), BF16, st2d)
        sbuf("btg", (128, 8 * 128), BF16, st2d)
        psum("paux", (128, 512), F32, st2d)
        psum("pyA0", (128, 512), F32, st2d)
        psum("pyA1", (128, 512), F32, st2d)
        psum("pyB0", (128, 512), F32, st2d)
        psum("pyB1", (128, 512), F32, st2d)
        psum("ptb", (128, 1024), BF16, st2d)

        def p2d():
            S.op("vector", lambda v: v.memset(sb["Scur"][:, :], 0.0), key=("sc0",))
            S.op("scalar", lambda s: s.copy(sb["Sbf0"][:, :], sb["Scur"][:, :]), key=("sb0",), deps=[("sc0",)])
            for c in range(P2D_CHUNKS):
                if not (P2D_PRE & 1):
                    pass
                for T in (range(32) if (P2D_PRE & 1) else []):
                    dep = [("p1xs", T, c)] + ([("fxw", 31)] if c == 0 else [])
                    if c >= 2:
                        dep.append(("yev", c - 2, 31))
                    S.op("sync", lambda s, T=T, c=c: s.dma_start(
                        out=sb[f"xc{c % 2}"][:, T * Q:(T + 1) * Q], in_=dram["x_spill"][T, c]),
                        key=("xcl", c, T), deps=dep, step=16, sem=f"xc{c % 2}")
                for gg8 in (range(8) if (P2D_PRE & 2) else []):
                    adep = [("acd",)]
                    pgi = (c * 8 + gg8) - 2
                    if pgi >= 0:
                        adep.append(("arp", pgi // 8, (pgi % 8) * 2 + 1))
                    S.op("sync", lambda s, c=c, gg8=gg8: s.dma_start(
                        out=sb[f"arow{gg8 % 2}"][:, :].rearrange("o (h t) -> o h t", h=8),
                        in_=dram["a_c_d"][gg8 * 8:(gg8 + 1) * 8, c * Q:(c + 1) * Q]),
                        key=("arw", c, gg8), deps=adep, step=16, sem=f"ar{gg8 % 2}")
                if not (P2D_PRE & 4):
                    continue
                S.op("tensor", lambda t, c=c: t.transpose(
                    ps["ptrf0"][0:1, 0:64], sb["a_c"][:, (c + 1) * Q - 1:(c + 1) * Q],
                    sb["eyef"][0:64, 0:64]),
                    key=("aet", c), deps=[("scan", c)] + ([("dst", c - 1, 63), ("epx", c - 1, 63)] if c >= 1 else []))
                S.op("vector", lambda v: v.tensor_copy(sb["erow"][:, :], ps["ptrf0"][0:1, 0:64]),
                     key=("aee", c), deps=[("aet", c)])
                S.op("tensor", lambda t: t.matmul(
                    ps["ptrf1"][:, 0:64], sb["onesrf"][0:1, :], sb["erow"][0:1, 0:64],
                    start=True, stop=True),
                    key=("aert", c), deps=[("aee", c)])
                S.op("vector", lambda v: v.tensor_copy(sb["aerep"][:, :], ps["ptrf1"][:, 0:64]),
                     key=("aerE", c), deps=[("aert", c)] + ([("xf", c - 1, 63)] if c >= 1 else []))
                S.op("scalar", lambda s: s.activation(sb["erep"][:, :], sb["aerep"][:, :], ACT.Exp),
                     key=("erepx", c), deps=[("aerE", c)])
                for g in (range(8) if (P2D_PRE & 8) else []):
                    btdep = [("bte", c, g - 1)] if g >= 1 else ([("bte", c - 1, 7), ("x1", c - 1, 63)] if c >= 1 else [])
                    if c == 0:
                        btdep += [("fxw", 32 + g)]
                    S.op("tensor", lambda t, c=c, g=g: t.transpose(
                        ps["ptb"][:, 0:128], sb["BC"][:, g * TOK + c * Q: g * TOK + (c + 1) * Q],
                        sb["eyeb"][:, :]),
                        key=("btt", c, g), deps=btdep)
                    S.op("scalar", lambda s, g=g: s.copy(sb["btg"][:, g * 128:(g + 1) * 128], ps["ptb"][:, 0:128]),
                         key=("bte", c, g), deps=[("btt", c, g)])
                for h in range(P2D_HEADS):
                    g = h // 8
                    pp = h // 2
                    hr = (h % 2) * 64
                    i2 = h % 4
                    ybank = (pp // 4) % 2
                    ycol = (pp % 4) * 128
                    sbf_r = sb[f"Sbf{c % 2}"]
                    sbf_w = sb[f"Sbf{(c + 1) % 2}"]
                    if (P2D_OPS & 64) and h % 8 == 0:
                        g2dep = []
                        if c == 0:
                            g2dep += [("fxw", 32 + g), ("fxw", 40 + g)]
                        if g >= 1:
                            g2dep.append(("m2", c, g * 8 - 1))
                        elif c >= 1:
                            g2dep.append(("m2", c - 1, 63))
                        if g >= 1:
                            g2dep.append(("sch", c, g * 8 - 1))
                        elif c >= 1:
                            g2dep.append(("sch", c - 1, 63))
                        S.op("tensor", lambda t, c=c, g=g: t.matmul(
                            ps["paux"][:, 0:128], sb["BC"][:, g * TOK + c * Q:g * TOK + (c + 1) * Q],
                            sb["BC"][:, (8 + g) * TOK + c * Q:(8 + g) * TOK + (c + 1) * Q],
                            start=True, stop=True),
                            key=("g2", c, g), deps=g2dep)
                    if (P2D_OPS & 1) and h % 4 == 0:
                        S.op("tensor", lambda t, c=c, h=h: t.matmul(
                            ps["ptrf0"][:, 0:512], sb["onesrf"][0:1, :],
                            sb[f"arow{(h // 8) % 2}"][0:1, (h % 8) * Q:(h % 8 + 4) * Q],
                            start=True, stop=True),
                            key=("arp", c, h // 4), deps=[("arw", c, h // 8)] +
                            ([("dst", c, h - 1), ("epx", c, h - 1)] if h >= 1 else []))
                    if P2D_OPS & 1:
                        S.op("vector", lambda v, c=c, h=h, i2=i2: v.scalar_tensor_tensor(
                            sb[f"dbuf{i2}"][:, :], ps["ptrf0"][:, (h % 4) * 128:(h % 4) * 128 + 128],
                            sb["aT"][:, c * 64 + h:c * 64 + h + 1],
                            sb["trineg"][:, :], ALU.subtract, ALU.add),
                            key=("dst", c, h), deps=[("arp", c, h // 4), ("aTe", c),
                                                     ("m2", c, h - 4) if h >= 4 else None,
                                                     ("epx", c, h - 1) if h >= 1 else None]),
                    if P2D_OPS & 512:
                        epdep = [("arp", c, h // 4), ("dst", c, h)]
                        if pp >= 2:
                            epdep.append(("ye1", c, pp - 2))
                        elif c >= 1:
                            epdep.append(("ye1", c - 1, 30 + pp))
                        S.op("scalar", lambda s, hr=hr, pp=pp, h=h: s.activation(
                            sb[f"Ep{pp % 2}"][hr:hr + 64, :],
                            ps["ptrf0"][hr:hr + 64, (h % 4) * 128:(h % 4) * 128 + 128], ACT.Exp),
                            key=("epx", c, h), deps=epdep)
                    if P2D_OPS & 128:
                        S.op("scalar", lambda s, i2=i2: s.activation(sb[f"dbuf{i2}"][:, :], sb[f"dbuf{i2}"][:, :], ACT.Exp),
                             key=("xpd", c, h), deps=[("dst", c, h)])
                    if P2D_OPS & 256:
                        S.op("vector", lambda v, i2=i2: v.tensor_tensor(
                            sb[f"m2b{i2}"][:, :], sb[f"dbuf{i2}"][:, :], ps["paux"][:, 0:128], ALU.mult),
                            key=("m2", c, h), deps=[("xpd", c, h), ("g2", c, g), ("mmA", c, h - 4) if h >= 4 else None,
                                                    ("mmD", c, h - 1) if h >= 1 else (("mmD", c - 1, 63) if c >= 1 else None)])
                    if P2D_OPS & 2:
                        S.op("tensor", lambda t, c=c, pp=pp, hr=hr, h=h: t.transpose(
                            ps["ptb"][:, 128 + 64 * (h % 2):192 + 64 * (h % 2)],
                            sb[f"xc{c % 2}"][hr:hr + 64, pp * Q:(pp + 1) * Q],
                            sb["eyeb"][hr:hr + 64, hr:hr + 64]),
                            key=("xtt", c, h), deps=[("xcl", c, 31), ("bte", c, 7)] +
                            ([("x2", c, h - 2)] if h >= 2 else ([("x2", c - 1, 62 + h)] if c >= 1 else [])))
                    if P2D_OPS & 2:
                        S.op("vector", lambda v, c=c, h=h, i2=i2: v.tensor_single_scalar(
                            sb[f"x1t{i2}"][:, :], ps["ptb"][:, 128 + 64 * (h % 2):192 + 64 * (h % 2)],
                            sb["dtT"][:, c * 64 + h:c * 64 + h + 1], ALU.mult),
                            key=("x1", c, h), deps=[("xtt", c, h), ("dtTe", c)])
                    if P2D_OPS & 2:
                        S.op("scalar", lambda s, c=c, h=h: s.activation(
                            sb["expfac"][:, :], sb["aT"][:, c * 64 + h:c * 64 + h + 1], ACT.Exp,
                            bias=sb["aerep"][:, h:h + 1], scale=-1.0),
                            key=("xf", c, h), deps=[("aerE", c)] + ([("bt", c, h - 1)] if h >= 1 else [])),
                    if P2D_OPS & 2:
                        S.op("vector", lambda v, i2=i2: v.tensor_single_scalar(
                            sb[f"x2t{i2}"][:, :], sb[f"x1t{i2}"][:, :], sb["expfac"][:, 0:1], ALU.mult),
                            key=("x2", c, h), deps=[("x1", c, h), ("xf", c, h)])
                    if P2D_OPS & 2:
                        S.op("vector", lambda v, g=g, i2=i2: v.tensor_single_scalar(
                            sb[f"btb{i2}"][:, :], sb["btg"][:, g * 128:(g + 1) * 128], sb["expfac"][:, 0:1], ALU.mult),
                            key=("bt", c, h), deps=[("bte", c, g), ("x2", c, h)])
                    mmadep = [("m2", c, h), ("x1", c, h)]
                    if not (P2D_OPS & 4):
                        mmadep = None
                    if mmadep is not None and h % 2 == 0:
                        mmadep.append(("ye2", c, pp - 5) if pp >= 5 else (("ye2", c - 1, 27 + pp) if c >= 1 else None))
                    if mmadep is not None:
                        S.op("tensor", lambda t, hr=hr, i2=i2, ybank=ybank, ycol=ycol: t.matmul(
                            ps[f"pyA{ybank}"][hr:hr + 64, ycol:ycol + 128], sb[f"x1t{i2}"][:, :], sb[f"m2b{i2}"][:, :],
                            start=True, stop=True),
                            key=("mmA", c, h), deps=mmadep)
                    mmbdep = [("sb0",)] if c == 0 else [("sbc", c - 1, h)]
                    if not (P2D_OPS & 8):
                        mmbdep = None
                    if mmbdep is not None and h % 2 == 0:
                        mmbdep.append(("ye1", c, pp - 5) if pp >= 5 else (("ye1", c - 1, 27 + pp) if c >= 1 else None))
                    if mmbdep is not None:
                        S.op("tensor", lambda t, hr=hr, h=h, g=g, c=c, sbf_r=sbf_r, ybank=ybank, ycol=ycol: t.matmul(
                            ps[f"pyB{ybank}"][hr:hr + 64, ycol:ycol + 128], sbf_r[:, h * 64:(h + 1) * 64],
                            sb["BC"][:, (8 + g) * TOK + c * Q:(8 + g) * TOK + (c + 1) * Q],
                            start=True, stop=True),
                            key=("mmB", c, h), deps=mmbdep)
                    if P2D_OPS & 16:
                        S.op("tensor", lambda t, i2=i2: t.matmul(
                            ps["paux"][:, 128:192], sb[f"btb{i2}"][:, :], sb[f"x2t{i2}"][:, :],
                            start=True, stop=True),
                            key=("mmD", c, h), deps=[("bt", c, h), ("x2", c, h)] +
                            ([("sch", c, h - 1)] if h >= 1 else ([("sch", c - 1, 63)] if c >= 1 else [])))
                        S.op("vector", lambda v, h=h: v.scalar_tensor_tensor(
                            sb["Scur"][:, h * 64:(h + 1) * 64], sb["Scur"][:, h * 64:(h + 1) * 64],
                            sb["erep"][:, h:h + 1], ps["paux"][:, 128:192], ALU.mult, ALU.add),
                            key=("sch", c, h), deps=[("mmD", c, h), ("erepx", c)])
                        S.op("scalar", lambda s, h=h, sbf_w=sbf_w: s.copy(
                            sbf_w[:, h * 64:(h + 1) * 64], sb["Scur"][:, h * 64:(h + 1) * 64]),
                            key=("sbc", c, h), deps=[("sch", c, h)])
                    if (P2D_OPS & 32) and h % 2 == 1:
                        ev = sb[f"evt{pp % 2}"]
                        S.op("vector", lambda v, ev=ev, pp=pp, ybank=ybank, ycol=ycol: v.tensor_tensor(
                            ev[:, :], ps[f"pyB{ybank}"][:, ycol:ycol + 128],
                            sb[f"Ep{pp % 2}"][:, :], ALU.mult),
                             key=("ye1", c, pp), deps=[("epx", c, h), ("mmB", c, h),
                                                       ("yev", c, pp - 2) if pp >= 2 else (("yev", c - 1, 30 + pp) if c >= 1 else None)])
                        S.op("vector", lambda v, ev=ev, ybank=ybank, ycol=ycol: v.tensor_tensor(
                            ev[:, :], ev[:, :], ps[f"pyA{ybank}"][:, ycol:ycol + 128], ALU.add),
                             key=("ye2", c, pp), deps=[("ye1", c, pp), ("mmA", c, h)])
                        S.op("vector", lambda v, ev=ev, c=c, pp=pp: v.scalar_tensor_tensor(
                            sb["y"][:, pp * TOK + c * Q:pp * TOK + (c + 1) * Q],
                            sb[f"xc{c % 2}"][:, pp * Q:(pp + 1) * Q], sb["dexp"][:, pp:pp + 1],
                            ev[:, :], ALU.mult, ALU.add),
                            key=("yev", c, pp), deps=[("ye2", c, pp)])

        if UPTO >= 4:
            S.run_phase("p2d", p2d)
        st2d.close()

        st3 = ExitStack()
        sbuf("sj", (128, 4096), F32, st3)
        sbuf("tv", (64, 1), F32, st3)
        sbuf("ejr", (128, 64), F32, st3)
        sbuf("ejrow", (1, 64), F32, st3)
        psum("pyA3", (128, 512), F32, st3)

        def p3():
            S.op("sync", lambda s: s.dma_start(out=dram["ags_in"][:, 0:4096], in_=sb["Scur"][:, :]),
                 key=("agsst",), deps=[("sch", NCH - 1, 63)], step=16, sem="agsio")
            S.op("sync", lambda s: s.dma_start(out=dram["ags_in"][0:64, 4096:4097], in_=sb["atot"][:, :],
                                               allow_slow_non_contiguous=True),
                 key=("agsst2",), deps=[("offu", NCH - 1)], step=16, sem="agsio")
            S.op("sync", lambda s: s.dma_start(out=dram["ags_in"][64:128, 4096:4097], in_=sb["atot"][:, :],
                                               allow_slow_non_contiguous=True),
                 key=("agsst3",), deps=[("offu", NCH - 1)], step=16, sem="agsio")
            S.op("gpsimd", lambda g: g.collective_compute(
                "AllGather", ALU.bypass, replica_groups=[list(range(8))],
                ins=[dram["ags_in"].opt()], outs=[dram["ags_out"].opt()]),
                key=("agscc",), deps=[("agsst",), ("agsst2",), ("agsst3",)], async_=True, sem="ccags")
            S.op("vector", lambda v: v.memset(sb["Sin"][:, :], 0.0), key=("si0",), deps=[("agsst",), ("agsst2",), ("agsst3",)])
            for j in range(7):
                bj = "sj"
                S.op("gpsimd", lambda g, j=j, bj=bj: g.dma_start(
                    out=sb[bj][:, :], in_=dram["ags_out"][j * 128:(j + 1) * 128, 0:4096]),
                    key=("sjl", j), deps=[("agscc",)] + ([("acc2", j - 1)] if j >= 1 else []), step=16, sem="sjl")
                S.op("gpsimd", lambda g, j=j: g.dma_start(
                    out=sb["tv"][:, :], in_=dram["ags_out"][j * 128:j * 128 + 64, 4096:4097],
                    allow_slow_non_contiguous=True),
                    key=("tvl", j), deps=[("agscc",)] + ([("tvt", j - 1)] if j >= 1 else []), step=16, sem="tvl")
                S.op("vector", lambda v, j=j: v.tensor_tensor(sb["tv"][:, :], sb["tv"][:, :],
                                                              sb["s_msel"][:, j:j + 1], ALU.mult),
                     key=("tvm", j), deps=[("tvl", j)])
                S.op("scalar", lambda s: s.activation(sb["tv"][:, :], sb["tv"][:, :], ACT.Exp),
                     key=("tvx", j), deps=[("tvm", j)])
                S.op("tensor", lambda t: t.transpose(ps["ptrf0"][0:1, 0:64], sb["tv"][:, :],
                                                     sb["eyef"][0:64, 0:64]),
                     key=("tvt", j), deps=[("tvx", j)] + ([("tve", j - 1)] if j >= 1 else []))
                S.op("vector", lambda v: v.tensor_copy(sb["ejrow"][:, :], ps["ptrf0"][0:1, 0:64]),
                     key=("tve", j), deps=[("tvt", j)])
                S.op("tensor", lambda t: t.matmul(ps["ptrf1"][:, 0:64], sb["onesrf"][0:1, :],
                                                  sb["ejrow"][0:1, 0:64], start=True, stop=True),
                     key=("ejrt", j), deps=[("tve", j)])
                S.op("vector", lambda v: v.tensor_copy(sb["ejr"][:, :], ps["ptrf1"][:, 0:64]),
                     key=("ejre", j), deps=[("ejrt", j)])
                S.op("vector", lambda v: v.tensor_tensor(
                    sb["Sin"][:, :].rearrange("n (h p) -> n h p", h=64),
                    sb["Sin"][:, :].rearrange("n (h p) -> n h p", h=64),
                    sb["ejr"][:, :].rearrange("n (h p) -> n h p", p=1).broadcast_to((128, 64, 64)),
                    ALU.mult),
                    key=("acc1", j), deps=[("ejre", j), ("si0",) if j == 0 else ("acc2", j - 1)])
                S.op("vector", lambda v, j=j, bj=bj: v.scalar_tensor_tensor(
                    sb["Sin"][:, :], sb[bj][:, :], sb["s_macc"][:, j:j + 1], sb["Sin"][:, :],
                    ALU.mult, ALU.add),
                    key=("acc2", j), deps=[("acc1", j), ("sjl", j)])
            S.op("scalar", lambda s: s.copy(sb["Sinb"][:, :], sb["Sin"][:, :]),
                 key=("sinb",), deps=[("acc2", 6)])
            for c in range(NCH):
                for gg8 in range(8):
                    adep = [("agd",)]
                    pgi = (c * 8 + gg8) - 2
                    if pgi >= 0:
                        adep.append(("agrp", pgi // 8, (pgi % 8) * 2 + 1))
                    S.op("sync", lambda s, c=c, gg8=gg8: s.dma_start(
                        out=sb[f"arow{gg8 % 2}"][:, :].rearrange("o (h t) -> o h t", h=8),
                        in_=dram["a_g_d"][gg8 * 8:(gg8 + 1) * 8, c * Q:(c + 1) * Q]),
                        key=("agw", c, gg8), deps=adep, step=16, sem=f"ar{gg8 % 2}")
                for h in range(H):
                    g = h // 8
                    hr = (h % 2) * 64
                    pp = h // 2
                    cdep = [("sinb",)]
                    if h % 2 == 0:
                        pv = ("cy2", c, pp - 1) if pp >= 1 else (("cy2", c - 1, 31) if c >= 1 else None)
                        cdep.append(pv)
                    S.op("tensor", lambda t, c=c, h=h, g=g, hr=hr: t.matmul(
                        ps["pyA3"][hr:hr + 64, 0:128], sb["Sinb"][:, h * 64:(h + 1) * 64],
                        sb["BC"][:, (8 + g) * TOK + c * Q:(8 + g) * TOK + (c + 1) * Q],
                        start=True, stop=True),
                        key=("cmm", c, h), deps=cdep)
                    if h % 4 == 0:
                        agdep = [("agw", c, h // 8)]
                        if h >= 1:
                            agdep.append(("epx3", c, h - 1))
                        elif c >= 1:
                            agdep.append(("epx3", c - 1, 63))
                        else:
                            agdep.append(("tve", 6))
                        S.op("tensor", lambda t, c=c, h=h: t.matmul(
                            ps["ptrf0"][:, 0:512], sb["onesrf"][0:1, :],
                            sb[f"arow{(h // 8) % 2}"][0:1, (h % 8) * Q:(h % 8 + 4) * Q],
                            start=True, stop=True),
                            key=("agrp", c, h // 4), deps=agdep)
                    epdep = [("agrp", c, h // 4)]
                    if pp >= 2:
                        epdep.append(("cy1", c, pp - 2))
                    elif c >= 1:
                        epdep.append(("cy1", c - 1, 30 + pp))
                    S.op("scalar", lambda s, hr=hr, pp=pp, h=h: s.activation(
                        sb[f"Ep{pp % 2}"][hr:hr + 64, :],
                        ps["ptrf0"][hr:hr + 64, (h % 4) * 128:(h % 4) * 128 + 128], ACT.Exp),
                        key=("epx3", c, h), deps=epdep)
                    if (P2D_OPS & 32) and h % 2 == 1:
                        ev = sb[f"evt{pp % 2}"]
                        S.op("vector", lambda v, ev=ev, pp=pp: v.tensor_tensor(ev[:, :], ps["pyA3"][:, 0:128],
                                                                               sb[f"Ep{pp % 2}"][:, :], ALU.mult),
                             key=("cy1", c, pp), deps=[("cmm", c, h), ("epx3", c, h)])
                        S.op("vector", lambda v, ev=ev, c=c, pp=pp: v.tensor_tensor(
                            sb["y"][:, pp * TOK + c * Q:pp * TOK + (c + 1) * Q],
                            sb["y"][:, pp * TOK + c * Q:pp * TOK + (c + 1) * Q], ev[:, :], ALU.add),
                            key=("cy2", c, pp), deps=[("cy1", c, pp)])

        if UPTO >= 5:
            S.run_phase("p3", p3)
        st3.close()
        st2.close()

        st4 = ExitStack()
        sbuf("wbufa", (128, 32 * 128), BF16, st4)
        sbuf("wbufb", (128, 32 * 128), BF16, st4)
        for i in range(4):
            sbuf(f"gld{i}", (128, TOK), BF16, st4)
        for i in range(2):
            sbuf(f"gsil{i}", (128, TOK), BF16, st4)
            sbuf(f"xres{i}", (128, TOK), F32, st4)
            sbuf(f"rstg{i}", (128, TOK), BF16, st4)
            sbuf(f"xstg{i}", (128, TOK), BF16, st4)
        sbuf("sqb", (128, TOK), BF16, st4)
        sbuf("rsvrow", (1, TOK), F32, st4)
        sbuf("stu", (1, TOK), F32, st4)
        sbuf("stq", (1, TOK), F32, st4)
        sbuf("stt", (1, TOK), F32, st4)
        sbuf("tmpf", (128, 512), F32, st4)
        psum("po0", (128, 512), F32, st4)
        psum("po1", (128, 512), F32, st4)
        psum("prr0", (128, 512), F32, st4)
        psum("prr1", (128, 512), F32, st4)
        psum("pst0", (128, 512), F32, st4)
        psum("pst1", (128, 512), F32, st4)
        psum("pst2", (128, 512), F32, st4)
        psum("pst3", (128, 512), F32, st4)

        def p4():
            for g in range(8):
                for ti in range(4):
                    T = g * 4 + ti
                    S.op("scalar", lambda s, T=T: s.activation(
                        sb["sqb"][:, :], sb["y"][:, T * TOK:(T + 1) * TOK], ACT.Square),
                        key=("rsq", T), deps=[("cy2", NCH - 1, 31)] if T == 0 else [("rst", T - 1, 1)])
                    for half in range(2):
                        S.op("tensor", lambda t, half=half, ti=ti: t.matmul(
                            ps[f"pst{half}"][0:1, :], sb["onescb"][:, :],
                            sb["sqb"][:, half * 512:half * 512 + 512],
                            start=(ti == 0), stop=(ti == 3)),
                            key=("rst", T, half),
                            deps=[("rsq", T)] + ([("rcp", g - 1)] if ti == 0 and g >= 1 else []))
                for half in range(2):
                    S.op("scalar", lambda s, half=half: s.activation(
                        sb["rsvrow"][0:1, half * 512:half * 512 + 512], ps[f"pst{half}"][0:1, :],
                        ACT.Sqrt, bias=EPS, scale=1.0 / 512.0),
                        key=("rsq2", g, half), deps=[("rst", g * 4 + 3, half)])
                S.op("vector", lambda v: v.reciprocal(sb["rsvrow"][:, :], sb["rsvrow"][:, :]),
                     key=("rcp", g), deps=[("rsq2", g, 0), ("rsq2", g, 1)])
                for half in range(2):
                    S.op("tensor", lambda t, g=g, half=half: t.matmul(
                        ps[f"prr{half}"][:, :], sb["onesrf"][0:1, :],
                        sb["rsvrow"][0:1, half * 512:half * 512 + 512], start=True, stop=True),
                        key=("rrp", g, half), deps=[("rcp", g)] +
                        ([("yng", (g - 1) * 4 + 3)] if g >= 1 else []))
                for ti in range(4):
                    T = g * 4 + ti
                    for half in range(2):
                        S.op("vector", lambda v, T=T, half=half: v.scalar_tensor_tensor(
                            sb["y"][:, T * TOK + half * 512:T * TOK + half * 512 + 512],
                            sb["y"][:, T * TOK + half * 512:T * TOK + half * 512 + 512],
                            sb["nw"][:, T:T + 1], ps[f"prr{half}"][:, :], ALU.mult, ALU.mult),
                            key=("ynr", T, half), deps=[("rrp", g, half)])
                    S.op("sync", lambda s, T=T: s.dma_start(out=sb[f"gld{T % 4}"][:, :], in_=dram["gate_spill"][T]),
                         key=("gll", T), deps=[("yng", T - 4)] if T >= 4 else [], step=16, sem=f"gll{T % 4}")
                    S.op("scalar", lambda s, T=T: s.activation(sb[f"gsil{T % 2}"][:, :], sb[f"gld{T % 4}"][:, :], ACT.Silu),
                         key=("gsl", T), deps=[("gll", T)] + ([("yng", T - 2)] if T >= 2 else []))
                    S.op("vector", lambda v, T=T: v.tensor_tensor(
                        sb["y"][:, T * TOK:(T + 1) * TOK], sb["y"][:, T * TOK:(T + 1) * TOK],
                        sb[f"gsil{T % 2}"][:, :], ALU.mult),
                        key=("yng", T), deps=[("gsl", T), ("ynr", T, 0), ("ynr", T, 1)])
            for dt in range(NT):
                wb = sb["wbufa" if dt % 2 == 0 else "wbufb"]
                S.op("gpsimd", lambda g, dt=dt, wb=wb: g.dma_start(
                    out=wb[:, :], in_=dram["w_out_t"][:, dt * 4096:(dt + 1) * 4096]),
                    key=("owl", dt), deps=[("omm", dt - 2, 1)] if dt >= 2 else [], step=16, sem=f"owl{dt % 2}")
                S.op("sync", lambda s, dt=dt: s.dma_start(
                    out=sb[f"xres{dt % 2}"][:, :], in_=dram["xT"][dt * 128:(dt + 1) * 128, :]),
                    key=("xrl", dt), deps=[("oev", dt - 2, 1)] if dt >= 2 else [], step=16, sem=f"xrl{dt % 2}")
                rst_b = sb[f"rstg{dt % 2}"]
                for half in range(2):
                    pso = ps[f"po{half}"]
                    deps0 = [("owl", dt), ("yng", 31)] + ([("oev", dt - 1, half)] if dt >= 1 else [])
                    for k in range(NT):
                        S.op("tensor", lambda t, pso=pso, wb=wb, k=k, half=half: t.matmul(
                            pso[:, :], wb[:, k * 128:(k + 1) * 128],
                            sb["y"][:, k * TOK + half * 512:k * TOK + half * 512 + 512],
                            start=(k == 0), stop=(k == NT - 1)),
                            key=("omm", dt, half) if k == NT - 1 else None,
                            deps=deps0 if k == 0 else [])
                    evdep = [("omm", dt, half), ("xrl", dt)]
                    if dt >= 2:
                        evdep += [("lst", dt - 2, 1, 1), ("rsp", dt - 2)]
                    S.op("vector", lambda v, dt=dt, half=half, pso=pso, rst_b=rst_b: v.tensor_tensor(
                        rst_b[:, half * 512:half * 512 + 512],
                        pso[:, :], sb[f"xres{dt % 2}"][:, half * 512:half * 512 + 512], ALU.add),
                        key=("oev", dt, half), deps=evdep)
                S.op("sync", lambda s, dt=dt, rst_b=rst_b: s.dma_start(
                    out=dram["resid_spill"][dt], in_=rst_b[:, :]),
                    key=("rsp", dt), deps=[("oev", dt, 0), ("oev", dt, 1)], step=16, sem=f"rsp{dt % 2}")
                S.op("scalar", lambda s, dt=dt, rst_b=rst_b: s.activation(
                    sb["sqb"][:, :], rst_b[:, :], ACT.Square),
                    key=("osq", dt), deps=[("oev", dt, 0), ("oev", dt, 1)] +
                    ([("lst", dt - 1, 1, 1)] if dt >= 1 else []))
                for half in range(2):
                    S.op("tensor", lambda t, dt=dt, half=half, rst_b=rst_b: t.matmul(
                        ps[f"pst{half}"][0:1, :], sb["onescb"][:, :],
                        rst_b[:, half * 512:half * 512 + 512],
                        start=(dt == 0), stop=(dt == NT - 1)),
                        key=("lst", dt, half, 0), deps=[("oev", dt, half)] + ([("rsq2", 7, half)] if dt == 0 else []))
                    S.op("tensor", lambda t, dt=dt, half=half: t.matmul(
                        ps[f"pst{2 + half}"][0:1, :], sb["onescb"][:, :],
                        sb["sqb"][:, half * 512:half * 512 + 512],
                        start=(dt == 0), stop=(dt == NT - 1)),
                        key=("lst", dt, half, 1), deps=[("osq", dt)])
            for half in range(2):
                S.op("scalar", lambda s, half=half: s.activation(
                    sb["stu"][:, half * 512:half * 512 + 512], ps[f"pst{half}"][0:1, :], ACT.Copy, scale=1.0 / D),
                    key=("mu", half), deps=[("lst", NT - 1, half, 0)])
                S.op("scalar", lambda s, half=half: s.activation(
                    sb["stq"][:, half * 512:half * 512 + 512], ps[f"pst{2 + half}"][0:1, :],
                    ACT.Copy, scale=1.0 / D),
                    key=("m2", half), deps=[("lst", NT - 1, half, 1)])
            S.op("scalar", lambda s: s.activation(sb["stt"][:, :], sb["stu"][:, :], ACT.Square),
                 key=("musq",), deps=[("mu", 0), ("mu", 1)])
            S.op("vector", lambda v: v.tensor_tensor(sb["stq"][:, :], sb["stq"][:, :], sb["stt"][:, :], ALU.subtract),
                 key=("var",), deps=[("musq",), ("m2", 0), ("m2", 1)])
            S.op("scalar", lambda s: s.activation(sb["stq"][:, :], sb["stq"][:, :], ACT.Sqrt, bias=EPS),
                 key=("rr0",), deps=[("var",)])
            S.op("vector", lambda v: v.reciprocal(sb["stq"][:, :], sb["stq"][:, :]),
                 key=("rr",), deps=[("rr0",)])
            S.op("vector", lambda v: v.tensor_tensor(sb["stu"][:, :], sb["stu"][:, :], sb["stq"][:, :], ALU.mult),
                 key=("m2r",), deps=[("rr",)])
            for half in range(2):
                S.op("tensor", lambda t, half=half: t.matmul(
                    ps[f"po{half}"][:, :], sb["onesrf"][0:1, :],
                    sb["stq"][0:1, half * 512:half * 512 + 512], start=True, stop=True),
                    key=("rrep", half), deps=[("rr",), ("m2r",), ("oev", NT - 1, half)])
                S.op("tensor", lambda t, half=half: t.matmul(
                    ps[f"prr{half}"][:, :], sb["onesrf"][0:1, :],
                    sb["stu"][0:1, half * 512:half * 512 + 512], start=True, stop=True),
                    key=("mrep", half), deps=[("m2r",), ("yng", 31)])
            for half in range(2):
                for dt in range(NT):
                    rst_b = sb[f"rstg{dt % 2}"]
                    xst_b = sb[f"xstg{dt % 2}"]
                    rldep = [("rsp", NT - 2 + (dt % 2)), ("lst", NT - 1, 1, 1)]
                    if dt >= 2:
                        rldep.append(("xl0", half, dt - 2))
                    elif half == 1:
                        rldep.append(("xl0", 0, NT - 2 + dt))
                    S.op("gpsimd", lambda g, dt=dt, half=half, rst_b=rst_b: g.dma_start(
                        out=rst_b[:, 0:512], in_=dram["resid_spill"][dt][:, half * 512:half * 512 + 512]),
                        key=("rld", half, dt), deps=rldep, step=16, sem=f"rld{dt % 2}")
                    S.op("vector", lambda v, half=half, rst_b=rst_b: v.tensor_tensor(
                        sb["tmpf"][:, :], rst_b[:, 0:512],
                        ps[f"po{half}"][:, :], ALU.mult),
                        key=("xl0", half, dt), deps=[("rld", half, dt), ("rrep", half),
                                                     ("xl1", half, dt - 1) if dt >= 1 else
                                                     (("xl1", 0, NT - 1) if half == 1 else None)])
                    xldep = [("xl0", half, dt), ("mrep", half)]
                    if dt >= 2:
                        xldep += [("agxs", half, dt - 2, 1), ("xlsp", half, dt - 2)]
                    elif half == 1:
                        xldep += [("agxs", 0, NT - 2 + dt, 1), ("xlsp", 0, NT - 2 + dt)]
                    S.op("vector", lambda v, xst_b=xst_b: v.tensor_tensor(
                        xst_b[:, 0:512], sb["tmpf"][:, :],
                        ps[f"prr{half}"][:, :], ALU.subtract),
                        key=("xl1", half, dt), deps=xldep)
                    for q in range(2):
                        j = half * 2 + q
                        S.op("sync", lambda s, dt=dt, j=j, q=q, xst_b=xst_b: s.dma_start(
                            out=dram[f"agx_in{j}"][:, dt * 256:(dt + 1) * 256],
                            in_=xst_b[:, q * 256:q * 256 + 256]),
                            key=("agxs", half, dt, q), deps=[("xl1", half, dt)], step=16, sem=f"agxs{dt % 2}")
                    S.op("scalar", lambda s, dt=dt, half=half, xst_b=xst_b: s.dma_start(
                        out=dram["xl1_spill"][:, dt * TOK + half * 512:dt * TOK + half * 512 + 512],
                        in_=xst_b[:, 0:512]),
                        key=("xlsp", half, dt), deps=[("xl1", half, dt)], step=16, sem=f"xlsp{dt % 2}")
                for q in range(2):
                    j = half * 2 + q
                    S.op("gpsimd", lambda g, j=j: g.collective_compute(
                        "AllGather", ALU.bypass, replica_groups=[list(range(8))],
                        ins=[dram[f"agx_in{j}"].opt()], outs=[dram[f"agx_out{j}"].opt()]),
                        key=("agx", j), deps=[("agxs", half, NT - 2, 1), ("agxs", half, NT - 1, 1)],
                        async_=True, sem=f"ccagx{j}")

        if UPTO >= 6:
            S.run_phase("p4", p4)
        st4.close()
        stY.close()
        stA.close()

        st5 = ExitStack()
        sbuf("W", (128, 32 * NTC * 128), BF16, st5)
        sbuf("hb0", (128, NTC * 512), BF16, st5)
        sbuf("hb1", (128, NTC * 512), BF16, st5)
        sbuf("qU", (128, 512), I32, st5)
        sbuf("qF", (128, 512), F32, st5)
        sbuf("zq_g", (32, 176), I32, st5)
        sbuf("zst", (32, NTC * 128), F32, st5)
        sbuf("zs_g", (32, NTC * 128), F32, st5)
        sbuf("sc_g", (32, NTC * 128), F32, st5)
        sbuf("onesrb", (1, 128), BF16, st5)
        sbuf("qwk0", (128, 176), I32, st5)
        sb["qwk1"] = sb["qwk0"]
        psum("pf0", (128, 512), F32, st5)
        psum("pf1", (128, 512), F32, st5)
        psum("pzz0", (128, 512), F32, st5)
        psum("pzz1", (128, 512), F32, st5)
        st5x = ExitStack()
        for i in range(2):
            sbuf(f"Xb{i}", (128, NT * 512), BF16, st5x)
            sbuf(f"gstg{i}", (128, 512), BF16, st5x)
        sbuf("srowb0", (1, NTC * 128), BF16, st5x)
        sbuf("zrowb0", (1, NTC * 128), BF16, st5x)
        sb["srowb1"] = sb["srowb0"]
        sb["zrowb1"] = sb["zrowb0"]

        FBLK = [(0, 512), (512, 512), (1024, 384)]

        def prev_blk(k, bi, nb):
            idx = k * nb + bi
            return None if idx == 0 else ((idx - 1) // nb, (idx - 1) % nb)

        def dequant_gu(tag, qname, sname, zname):
            xdep = [("g", "zs")] if tag == "u" else []
            S.op("gpsimd", lambda g: g.dma_start(out=sb["zq_g"][:, :], in_=dram[zname]),
                 key=(tag, "zld"), deps=xdep, step=16, sem="fzio")
            S.op("gpsimd", lambda g: g.dma_start(out=sb["sc_g"][:, :], in_=dram[sname]),
                 key=(tag, "sld"), deps=xdep, step=16, sem="fzio")
            S.op("gpsimd", lambda g: g.memset(sb["onesrb"][:, :], 1.0), key=(tag, "ob"))
            for i in range(8):
                S.op("vector", lambda v, i=i: v.tensor_scalar(
                    sb["zst"][:, :].rearrange("p (a b) -> p b a", b=8)[:, i, :].bitcast(I32),
                    sb["zq_g"][:, :], 4 * i, 15, ALU.logical_shift_right, ALU.bitwise_and),
                    key=(tag, "zu", i), deps=[(tag, "zld"), (tag, "sld")])
            S.op("vector", lambda v: v.tensor_copy(sb["zs_g"][:, :], sb["zst"][:, :].bitcast(I32)),
                 key=(tag, "zcv"), deps=[(tag, "zu", i) for i in range(8)])
            S.op("vector", lambda v: v.tensor_tensor(sb["zs_g"][:, :], sb["zs_g"][:, :], sb["sc_g"][:, :], ALU.mult),
                 key=(tag, "zs"), deps=[(tag, "zcv"), (tag, "sld")])
            S.op("sync", lambda s: s.dma_start(out=dram["zs_gu_d"], in_=sb["zs_g"][:, :]),
                 key=(tag, "zspl"), deps=[(tag, "zs")] + ([("g", "zrl", 31)] if tag == "u" else []),
                 step=16, sem="zspl")
            for k in range(32):
                qb = sb[f"qwk{k % 2}"]
                qdep = [(tag, "u", k - 1, 2, 7)] if k >= 1 else (
                    [("g", "u", 31, 2, 7), ("g", "wB", (31, 2))] if tag == "u" else [])
                S.op("gpsimd", lambda g, qb=qb, k=k: g.dma_start(
                    out=qb[:, :], in_=dram[qname][:, k * 176:(k + 1) * 176]),
                    key=(tag, "qld", k), deps=qdep, step=16, sem="qld")
                S.op("gpsimd", lambda g, k=k: g.dma_start(
                    out=sb[f"srowb{k % 2}"][:, :], in_=dram[sname][k:k + 1, :]),
                    key=(tag, "srl", k), deps=([(tag, "sr", k - 1, 2)] if k >= 1 else
                                               ([("g", "sr", 31, 2)] if tag == "u" else [])),
                    step=16, sem="srl")
                S.op("gpsimd", lambda g, k=k: g.dma_start(
                    out=sb[f"zrowb{k % 2}"][:, :], in_=dram["zs_gu_d"][k:k + 1, :]),
                    key=(tag, "zrl", k), deps=[(tag, "zspl")] + ([(tag, "zr", k - 1, 2)] if k >= 1 else
                                               ([("g", "zr", 31, 2)] if tag == "u" else [])),
                    step=16, sem="zrl")
                for bi, (off, width) in enumerate(FBLK):
                    woff, wwidth = off // 8, width // 8
                    pb = prev_blk(k, bi, 3)
                    for i in range(8):
                        S.op("vector", lambda v, qb=qb, woff=woff, wwidth=wwidth, width=width, i=i: v.tensor_scalar(
                            sb["qU"][:, 0:width].rearrange("p (a b) -> p b a", b=8)[:, i, :],
                            qb[:, woff:woff + wwidth], 4 * i, 15,
                            ALU.logical_shift_right, ALU.bitwise_and),
                            key=(tag, "u", k, bi, i), deps=[(tag, "qld", k)] +
                            ([(tag, "wA", pb)] if pb is not None and i == 0 else []))
                    S.op("vector", lambda v, width=width: v.tensor_copy(sb["qF"][:, 0:width], sb["qU"][:, 0:width]),
                         key=(tag, "cv", k, bi), deps=[(tag, "u", k, bi, i) for i in range(8)] +
                         ([(tag, "wB", pb)] if pb is not None else []))
                    S.op("tensor", lambda t, k=k, off=off, width=width: t.matmul(
                        ps["pzz0"][:, 0:width], sb["onesrb"][0:1, :],
                        sb[f"srowb{k % 2}"][0:1, off:off + width],
                        start=True, stop=True),
                        key=(tag, "sr", k, bi), deps=[(tag, "srl", k), (tag, "ob")] +
                        ([(tag, "wA", pb)] if pb is not None else
                         ([("g", "mm", 15, NTC - 1)] if tag == "u" else [])))
                    S.op("tensor", lambda t, k=k, off=off, width=width: t.matmul(
                        ps["pzz1"][:, 0:width], sb["onesrb"][0:1, :],
                        sb[f"zrowb{k % 2}"][0:1, off:off + width],
                        start=True, stop=True),
                        key=(tag, "zr", k, bi), deps=[(tag, "zrl", k)] +
                        ([(tag, "wB", pb)] if pb is not None else []))
                    S.op("vector", lambda v, width=width: v.tensor_tensor(
                        sb["qF"][:, 0:width], sb["qF"][:, 0:width], ps["pzz0"][:, 0:width], ALU.mult),
                        key=(tag, "wA", (k, bi)), deps=[(tag, "cv", k, bi), (tag, "sr", k, bi)])
                    S.op("vector", lambda v, k=k, off=off, width=width: v.tensor_tensor(
                        sb["W"][:, k * NTC * 128 + off:k * NTC * 128 + off + width],
                        sb["qF"][:, 0:width], ps["pzz1"][:, 0:width], ALU.subtract),
                        key=(tag, "wB", (k, bi)), deps=[(tag, "wA", (k, bi)), (tag, "zr", k, bi)])

        def x_load(tag, tb, buf):
            r = tb // 2
            for jq in range(2):
                j = (tb % 2) * 2 + jq
                dep = [("agx", j)]
                if tb >= 2:
                    dep.append((tag, "mm", tb - 2, NTC - 1))
                elif tag == "u":
                    dep.append(("g", "mm", 14 + tb, NTC - 1))
                S.op("sync", lambda s, buf=buf, r=r, j=j, jq=jq: s.dma_start(
                    out=buf[:, :].rearrange("p (k q t) -> p k q t", k=NT, q=2)[:, :, jq, :],
                    in_=dram[f"agx_out{j}"][r * 128:(r + 1) * 128, :].rearrange("p (k t) -> p k t", k=NT)),
                    key=(tag, "xl", tb, jq), deps=dep, step=16, sem=f"xb{tb % 2}")

        def p5a():
            dequant_gu("g", "gq", "gsc", "gzq")
            for tb in range(16):
                buf = sb[f"Xb{tb % 2}"]
                x_load("g", tb, buf)
                for nt in range(NTC):
                    gg = tb * NTC + nt
                    pf = ps[f"pf{gg % 2}"]
                    deps0 = [("g", "xl", tb, 0), ("g", "xl", tb, 1), ("g", "wB", (31, 2))]
                    if gg >= 2:
                        deps0.append(("g", "sil", (gg - 2) // NTC, (gg - 2) % NTC))
                    for k in range(NT):
                        S.op("tensor", lambda t, pf=pf, k=k, nt=nt, buf=buf: t.matmul(
                            pf[:, :], sb["W"][:, k * NTC * 128 + nt * 128:k * NTC * 128 + (nt + 1) * 128],
                            buf[:, k * 512:(k + 1) * 512], start=(k == 0), stop=(k == NT - 1)),
                            key=("g", "mm", tb, nt) if k == NT - 1 else None,
                            deps=deps0 if k == 0 else [])
                    sdep = [("g", "mm", tb, nt)]
                    if gg >= 2:
                        sdep.append(("g", "sp", (gg - 2) // NTC, (gg - 2) % NTC))
                    S.op("scalar", lambda s, nt=nt, pf=pf, gg=gg: s.activation(
                        sb[f"gstg{gg % 2}"][:, :], pf[:, :], ACT.Silu),
                        key=("g", "sil", tb, nt), deps=sdep)
                    S.op("scalar", lambda s, tb=tb, nt=nt, gg=gg: s.dma_start(
                        out=dram["g_spill"][tb][:, nt * 512:(nt + 1) * 512], in_=sb[f"gstg{gg % 2}"][:, :]),
                        key=("g", "sp", tb, nt), deps=[("g", "sil", tb, nt)], step=16, sem=f"gsl{gg % 2}")

        def p5b():
            dequant_gu("u", "uq", "usc", "uzq")
            for tb in range(16):
                buf = sb[f"Xb{tb % 2}"]
                hbuf = sb[f"hb{tb % 2}"]
                x_load("u", tb, buf)
                S.op("gpsimd", lambda g, tb=tb, hbuf=hbuf: g.dma_start(out=hbuf[:, :], in_=dram["g_spill"][tb]),
                     key=("u", "gl", tb), deps=[("g", "sp", tb, NTC - 1)] + ([("u", "hsp", tb - 2)] if tb >= 2 else []),
                     step=16, sem=f"hgl{tb % 2}")
                for nt in range(NTC):
                    gg = tb * NTC + nt
                    pf = ps[f"pf{gg % 2}"]
                    deps0 = [("u", "xl", tb, 0), ("u", "xl", tb, 1), ("u", "wB", (31, 2))]
                    if gg >= 2:
                        deps0.append(("u", "hh", (gg - 2) // NTC, (gg - 2) % NTC))
                    for k in range(NT):
                        S.op("tensor", lambda t, pf=pf, k=k, nt=nt, buf=buf: t.matmul(
                            pf[:, :], sb["W"][:, k * NTC * 128 + nt * 128:k * NTC * 128 + (nt + 1) * 128],
                            buf[:, k * 512:(k + 1) * 512], start=(k == 0), stop=(k == NT - 1)),
                            key=("u", "mm", tb, nt) if k == NT - 1 else None,
                            deps=deps0 if k == 0 else [])
                    S.op("vector", lambda v, nt=nt, pf=pf, hbuf=hbuf: v.tensor_tensor(
                        hbuf[:, nt * 512:(nt + 1) * 512], hbuf[:, nt * 512:(nt + 1) * 512],
                        pf[:, :], ALU.mult),
                        key=("u", "hh", tb, nt), deps=[("u", "mm", tb, nt), ("u", "gl", tb)])
                S.op("scalar", lambda s, tb=tb, hbuf=hbuf: s.dma_start(out=dram["h_spill"][tb], in_=hbuf[:, :]),
                     key=("u", "hsp", tb), deps=[("u", "hh", tb, nt) for nt in range(NTC)], step=16, sem=f"hsp{tb % 2}")

        def p5c():
            S.op("gpsimd", lambda g: g.dma_start(out=sb["zq_d"][:, :], in_=dram["dzq"]),
                 key=("d", "zld"), step=16, sem="fzio")
            S.op("gpsimd", lambda g: g.dma_start(out=sb["sc_d"][:, :], in_=dram["dsc"]),
                 key=("d", "sld"), step=16, sem="fzio")
            for i in range(8):
                S.op("vector", lambda v, i=i: v.tensor_scalar(
                    sb["zstd"][:, :].rearrange("p (a b) -> p b a", b=8)[:, i, :].bitcast(I32),
                    sb["zq_d"][:, :], 4 * i, 15, ALU.logical_shift_right, ALU.bitwise_and),
                    key=("d", "zu", i), deps=[("d", "zld"), ("d", "sld")])
            S.op("vector", lambda v: v.tensor_copy(sb["zs_d"][:, :], sb["zstd"][:, :].bitcast(I32)),
                 key=("d", "zcv"), deps=[("d", "zu", i) for i in range(8)])
            S.op("vector", lambda v: v.tensor_tensor(sb["zs_d"][:, :], sb["zs_d"][:, :], sb["sc_d"][:, :], ALU.mult),
                 key=("d", "zs"), deps=[("d", "zcv"), ("d", "sld")])
            S.op("sync", lambda s: s.dma_start(out=dram["zs_d_d"], in_=sb["zs_d"][:, :]),
                 key=("d", "zspl"), deps=[("d", "zs")], step=16, sem="zspl")
            for k in range(NTC):
                qb = sb[f"qdk{k % 2}"]
                S.op("gpsimd", lambda g, qb=qb, k=k: g.dma_start(
                    out=qb[:, :], in_=dram["dq"][:, k * 512:(k + 1) * 512]),
                    key=("d", "qld", k), deps=[("d", "u", k - 2, 7, 7)] if k >= 2 else [], step=16, sem=f"qdk{k % 2}")
                S.op("gpsimd", lambda g, k=k: g.dma_start(
                    out=sb[f"srowd{k % 2}"][:, :], in_=dram["dsc"][k:k + 1, :]),
                    key=("d", "srl", k), deps=([("d", "sr", k - 1, 7)] if k >= 1 else []), step=16, sem="srl")
                S.op("gpsimd", lambda g, k=k: g.dma_start(
                    out=sb[f"zrowd{k % 2}"][:, :], in_=dram["zs_d_d"][k:k + 1, :]),
                    key=("d", "zrl", k), deps=[("d", "zspl")] + ([("d", "zr", k - 1, 7)] if k >= 1 else []), step=16, sem="zrl")
                for bi in range(8):
                    off = bi * 512
                    pb = prev_blk(k, bi, 8)
                    for i in range(8):
                        S.op("vector", lambda v, qb=qb, off=off, i=i: v.tensor_scalar(
                            sb["qU"][:, 0:512].rearrange("p (a b) -> p b a", b=8)[:, i, :],
                            qb[:, off // 8:off // 8 + 64], 4 * i, 15,
                            ALU.logical_shift_right, ALU.bitwise_and),
                            key=("d", "u", k, bi, i), deps=[("d", "qld", k)] +
                            ([("d", "wA", pb)] if pb is not None and i == 0 else []))
                    S.op("vector", lambda v: v.tensor_copy(sb["qF"][:, 0:512], sb["qU"][:, 0:512]),
                         key=("d", "cv", k, bi), deps=[("d", "u", k, bi, i) for i in range(8)] +
                         ([("d", "wB", pb)] if pb is not None else []))
                    S.op("tensor", lambda t, k=k, off=off: t.matmul(
                        ps["pzz0"][:, :], sb["onesrb"][0:1, :],
                        sb[f"srowd{k % 2}"][0:1, off:off + 512],
                        start=True, stop=True),
                        key=("d", "sr", k, bi), deps=[("d", "srl", k)] + ([("d", "wA", pb)] if pb is not None else []))
                    S.op("tensor", lambda t, k=k, off=off: t.matmul(
                        ps["pzz1"][:, :], sb["onesrb"][0:1, :],
                        sb[f"zrowd{k % 2}"][0:1, off:off + 512],
                        start=True, stop=True),
                        key=("d", "zr", k, bi), deps=[("d", "zrl", k)] + ([("d", "wB", pb)] if pb is not None else []))
                    S.op("vector", lambda v: v.tensor_tensor(sb["qF"][:, 0:512], sb["qF"][:, 0:512],
                                                             ps["pzz0"][:, :], ALU.mult),
                         key=("d", "wA", (k, bi)), deps=[("d", "cv", k, bi), ("d", "sr", k, bi)])
                    S.op("vector", lambda v, k=k, off=off: v.tensor_tensor(
                        sb["W"][:, k * 4096 + off:k * 4096 + off + 512], sb["qF"][:, 0:512],
                        ps["pzz1"][:, :], ALU.subtract),
                        key=("d", "wB", (k, bi)), deps=[("d", "wA", (k, bi)), ("d", "zr", k, bi)])
            for dh in range(4):
                for tb in range(16):
                    hbuf = sb[f"hb{tb % 2}"]
                    hldep = [("u", "hsp", tb)]
                    if tb >= 2:
                        hldep.append(("d", "mm", dh, tb - 2, 7))
                    elif dh >= 1:
                        hldep.append(("d", "mm", dh - 1, 14 + tb, 7))
                    S.op("gpsimd", lambda g, tb=tb, hbuf=hbuf, dh=dh: g.dma_start(
                        out=hbuf[:, :], in_=dram["h_spill"][tb]),
                        key=("d", "hl", dh, tb), deps=hldep, step=16, sem=f"hl{tb % 2}")
                    for dtl in range(8):
                        dto = dh * 8 + dtl
                        pf = ps[f"pf{dtl % 2}"]
                        deps0 = [("d", "hl", dh, tb), ("d", "wB", (NTC - 1, 7))]
                        pv = (dh, tb, dtl - 2) if dtl >= 2 else ((dh, tb - 1, 6 + dtl) if tb >= 1 else ((dh - 1, 15, 6 + dtl) if dh >= 1 else None))
                        if pv is not None:
                            deps0.append(("d", "pe", pv[0], pv[1], pv[2]))
                        for k in range(NTC):
                            S.op("tensor", lambda t, pf=pf, k=k, dto=dto, hbuf=hbuf: t.matmul(
                                pf[:, :], sb["W"][:, k * 4096 + dto * 128:k * 4096 + (dto + 1) * 128],
                                hbuf[:, k * 512:(k + 1) * 512], start=(k == 0), stop=(k == NTC - 1)),
                                key=("d", "mm", dh, tb, dtl) if k == NTC - 1 else None,
                                deps=deps0 if k == 0 else [])
                        pedep = [("d", "mm", dh, tb, dtl)]
                        pv1 = (dh, tb, dtl - 1) if dtl >= 1 else ((dh, tb - 1, 7) if tb >= 1 else ((dh - 1, 15, 7) if dh >= 1 else None))
                        if pv1 is not None:
                            pedep.append(("d", "pev", pv1[0], pv1[1], pv1[2]))
                        S.op("vector", lambda v, pf=pf: v.tensor_copy(
                            sb["pt"][:, :], pf[:, :]),
                            key=("d", "pe", dh, tb, dtl), deps=pedep)
                        S.op("sync", lambda s, dh=dh, tb=tb, dtl=dtl: s.dma_start(
                            out=dram[f"rs_in{dh}"][(tb // 2) * 1024 + dtl * 128:(tb // 2) * 1024 + (dtl + 1) * 128,
                                                   (tb % 2) * 512:(tb % 2) * 512 + 512],
                            in_=sb["pt"][:, :]),
                            key=("d", "pev", dh, tb, dtl), deps=[("d", "pe", dh, tb, dtl)], step=16, sem="pev")
                S.op("gpsimd", lambda g, dh=dh: g.collective_compute(
                    "ReduceScatter", ALU.add, replica_groups=[list(range(8))],
                    ins=[dram[f"rs_in{dh}"].opt()], outs=[dram[f"rs_out{dh}"].opt()]),
                    key=("rs", dh),
                    deps=[("d", "pev", dh, 14, 7), ("d", "pev", dh, 15, 7)], async_=True, sem=f"ccrs{dh}")

        if UPTO >= 7:
            S.run_phase("p5a", p5a)
        if UPTO >= 8:
            S.run_phase("p5b", p5b)
        st5x.close()
        st5c = ExitStack()
        sbuf("qdk0", (128, 512), I32, st5c)
        sbuf("qdk1", (128, 512), I32, st5c)
        sbuf("srowd0", (1, D), BF16, st5c)
        sbuf("zrowd0", (1, D), BF16, st5c)
        sb["srowd1"] = sb["srowd0"]
        sb["zrowd1"] = sb["zrowd0"]
        sbuf("sc_d", (NTC, D), F32, st5c)
        sbuf("zs_d", (NTC, D), F32, st5c)
        sbuf("zstd", (NTC, D), F32, st5c)
        sbuf("zq_d", (NTC, 512), I32, st5c)
        sbuf("pt", (128, 512), BF16, st5c)
        if UPTO >= 9:
            S.run_phase("p5c", p5c)
        st5c.close()
        st5.close()

        st6 = ExitStack()
        sbuf("r2", (128, NT * TOK), BF16, st6)
        sbuf("sq6", (128, TOK), BF16, st6)
        sbuf("stu6", (1, TOK), F32, st6)
        sbuf("stq6", (1, TOK), F32, st6)
        sbuf("stt6", (1, TOK), F32, st6)
        sbuf("tf6", (128, 512), F32, st6)
        for i in range(2):
            sbuf(f"rsl{i}", (128, TOK), BF16, st6)
            sbuf(f"xo{i}", (128, TOK), BF16, st6)
            sbuf(f"of{i}", (128, TOK), F32, st6)
        psum("p60", (128, 512), F32, st6)
        psum("p61", (128, 512), F32, st6)
        psum("p62", (128, 512), F32, st6)
        psum("p63", (128, 512), F32, st6)
        psum("p6r0", (128, 512), F32, st6)
        psum("p6r1", (128, 512), F32, st6)
        psum("p6m0", (128, 512), F32, st6)
        psum("p6m1", (128, 512), F32, st6)

        def p6():
            for k in range(NT):
                dh = k // 8
                row = (k % 8) * 128
                S.op("gpsimd", lambda g, k=k, dh=dh, row=row: g.dma_start(
                    out=sb[f"rsl{k % 2}"][:, :], in_=dram[f"rs_out{dh}"][row:row + 128, :]),
                    key=("rsl", k), deps=[("rs", dh)] + ([("r2", k - 2)] if k >= 2 else []), step=16, sem=f"rsl{k % 2}")
                S.op("sync", lambda s, k=k: s.dma_start(
                    out=sb[f"xo{k % 2}"][:, :], in_=dram["xl1_spill"][:, k * TOK:(k + 1) * TOK]),
                    key=("xol", k), deps=[("xlsp", 1, k)] + ([("r2", k - 2)] if k >= 2 else []), step=16, sem=f"xol{k % 2}")
                S.op("vector", lambda v, k=k: v.tensor_tensor(
                    sb["r2"][:, k * TOK:(k + 1) * TOK], sb[f"rsl{k % 2}"][:, :], sb[f"xo{k % 2}"][:, :], ALU.add),
                    key=("r2", k), deps=[("rsl", k), ("xol", k)])
                S.op("scalar", lambda s, k=k: s.activation(sb["sq6"][:, :], sb["r2"][:, k * TOK:(k + 1) * TOK], ACT.Square),
                     key=("sq6", k), deps=[("r2", k)] + ([("st6", k - 1, 1, 1)] if k >= 1 else []))
                for half in range(2):
                    S.op("tensor", lambda t, k=k, half=half: t.matmul(
                        ps[f"p6{half}"][0:1, :], sb["onescb"][:, :],
                        sb["r2"][:, k * TOK + half * 512:k * TOK + half * 512 + 512],
                        start=(k == 0), stop=(k == NT - 1)),
                        key=("st6", k, half, 0), deps=[("r2", k)])
                    S.op("tensor", lambda t, k=k, half=half: t.matmul(
                        ps[f"p6{2 + half}"][0:1, :], sb["onescb"][:, :],
                        sb["sq6"][:, half * 512:half * 512 + 512],
                        start=(k == 0), stop=(k == NT - 1)),
                        key=("st6", k, half, 1), deps=[("sq6", k)])
            for half in range(2):
                S.op("scalar", lambda s, half=half: s.activation(
                    sb["stu6"][:, half * 512:half * 512 + 512], ps[f"p6{half}"][0:1, :], ACT.Copy, scale=1.0 / D),
                    key=("mu6", half), deps=[("st6", NT - 1, half, 0)])
                S.op("scalar", lambda s, half=half: s.activation(
                    sb["stq6"][:, half * 512:half * 512 + 512], ps[f"p6{2 + half}"][0:1, :],
                    ACT.Copy, scale=1.0 / D),
                    key=("m26", half), deps=[("st6", NT - 1, half, 1)])
            S.op("scalar", lambda s: s.activation(sb["stt6"][:, :], sb["stu6"][:, :], ACT.Square),
                 key=("musq6",), deps=[("mu6", 0), ("mu6", 1)])
            S.op("vector", lambda v: v.tensor_tensor(sb["stq6"][:, :], sb["stq6"][:, :], sb["stt6"][:, :], ALU.subtract),
                 key=("var6",), deps=[("musq6",), ("m26", 0), ("m26", 1)])
            S.op("scalar", lambda s: s.activation(sb["stq6"][:, :], sb["stq6"][:, :], ACT.Sqrt, bias=EPS),
                 key=("rr60",), deps=[("var6",)])
            S.op("vector", lambda v: v.reciprocal(sb["stq6"][:, :], sb["stq6"][:, :]),
                 key=("rr6",), deps=[("rr60",)])
            S.op("vector", lambda v: v.tensor_tensor(sb["stu6"][:, :], sb["stu6"][:, :], sb["stq6"][:, :], ALU.mult),
                 key=("m2r6",), deps=[("rr6",)])
            for half in range(2):
                S.op("tensor", lambda t, half=half: t.matmul(
                    ps[f"p6r{half}"][:, :], sb["onesrf"][0:1, :],
                    sb["stq6"][0:1, half * 512:half * 512 + 512], start=True, stop=True),
                    key=("rrep6", half), deps=[("rr6",), ("m2r6",)])
                S.op("tensor", lambda t, half=half: t.matmul(
                    ps[f"p6m{half}"][:, :], sb["onesrf"][0:1, :],
                    sb["stu6"][0:1, half * 512:half * 512 + 512], start=True, stop=True),
                    key=("mrep6", half), deps=[("m2r6",)])
            for k in range(NT):
                for half in range(2):
                    S.op("vector", lambda v, k=k, half=half: v.tensor_tensor(
                        sb["tf6"][:, :], sb["r2"][:, k * TOK + half * 512:k * TOK + half * 512 + 512],
                        ps[f"p6r{half}"][:, :], ALU.mult),
                        key=("o0", k, half), deps=[("rrep6", half), ("o1", k, 0) if half == 1 else (("o1", k - 1, 1) if k >= 1 else None)])
                    S.op("vector", lambda v, k=k, half=half: v.tensor_tensor(
                        sb[f"of{k % 2}"][:, half * 512:half * 512 + 512], sb["tf6"][:, :],
                        ps[f"p6m{half}"][:, :], ALU.subtract),
                        key=("o1", k, half), deps=[("o0", k, half), ("mrep6", half)] +
                        ([("ow", k - 2)] if k >= 2 else []))
                S.op("sync", lambda s, k=k: s.dma_start(out=dram["y_out"][k * 128:(k + 1) * 128, :],
                                                        in_=sb[f"of{k % 2}"][:, :]),
                     key=("ow", k), deps=[("o1", k, 0), ("o1", k, 1)], step=16, sem=f"ow{k % 2}")
            S.op("sync", lambda s: None, key=None, deps=[("ow", NT - 2), ("ow", NT - 1)])

        if UPTO >= 10:
            S.run_phase("p6", p6)
        st6.close()
        if UPTO < 10:
            def ptail():
                for k in range(NT):
                    if UPTO in (4, 5) and "y" in sb:
                        S.op("gpsimd", lambda g, k=k: g.dma_start(
                            out=dram["y_out"][k * 128:(k + 1) * 128, :], in_=sb["y"][:, k * TOK:(k + 1) * TOK]),
                            key=("dow", k), step=16, sem="dow")
                    elif UPTO == 6:
                        S.op("gpsimd", lambda g, k=k: g.dma_start(
                            out=dram["y_out"][k * 128:(k + 1) * 128, :],
                            in_=dram["xl1_spill"][:, k * TOK:(k + 1) * TOK]),
                            key=("dow", k), step=16, sem="dow")
                    elif UPTO in (7, 8):
                        src = dram["g_spill" if UPTO == 7 else "h_spill"]
                        S.op("gpsimd", lambda g, k=k, src=src: g.dma_start(
                            out=dram["y_out"][k * 128:(k + 1) * 128, :],
                            in_=src[k % 16][:, (k // 16) * 1024:(k // 16) * 1024 + 1024]),
                            key=("dow", k), step=16, sem="dow")
                    elif UPTO == 9:
                        S.op("gpsimd", lambda g, k=k: g.dma_start(
                            out=dram["y_out"][k * 128:(k + 1) * 128, :],
                            in_=dram[f"rs_out{k // 16}"][(k % 16) * 128:(k % 16) * 128 + 128, :]),
                            key=("dow", k), step=16, sem="dow")
                    else:
                        S.op("sync", lambda s, k=k: s.dma_start(
                            out=dram["y_out"][k * 128:(k + 1) * 128, :], in_=dram["xT"][k * 128:(k + 1) * 128, :]),
                            key=("dow", k), step=16, sem="dow")
                eng = "sync" if UPTO not in (4, 5, 6, 7, 8, 9) else "gpsimd"
                S.op(eng, lambda s: None, key=None, deps=[("dow", k) for k in range(NT)])
            S.run_phase("ptail", ptail)
    return nc



def _tile_lhsT(W):
    O, Din = W.shape
    kt = Din // 128
    blocks = []
    for o0 in range(0, O, 128):
        ow = min(128, O - o0)
        blk = W[o0:o0 + ow, :].reshape(ow, kt, 128).transpose(2, 1, 0).reshape(128, kt * ow)
        blocks.append(blk)
    return np.ascontiguousarray(np.concatenate(blocks, axis=1))


_NC_CACHE = {}


def kernel(**inp):
    if ("nc", UPTO, P2D_CHUNKS, P2D_HEADS, P2D_PRE, P2D_OPS) not in _NC_CACHE:
        _NC_CACHE[("nc", UPTO, P2D_CHUNKS, P2D_HEADS, P2D_PRE, P2D_OPS)] = build_nc()
    nc = _NC_CACHE[("nc", UPTO, P2D_CHUNKS, P2D_HEADS, P2D_PRE, P2D_OPS)]

    x = np.asarray(inp["L0_attn_hidden_states"], dtype=np.float32)
    w_in = np.asarray(inp["L0_attn_in_proj_w"], dtype=np.float32)
    w_out = np.asarray(inp["L0_attn_out_proj_w"], dtype=np.float32)
    conv_w = np.asarray(inp["L0_attn_conv_weight"], dtype=np.float32)
    conv_b = np.asarray(inp["L0_attn_conv_bias"], dtype=np.float32)
    dt_bias = np.asarray(inp["L0_attn_dt_bias"], dtype=np.float32)
    a_log = np.asarray(inp["L0_attn_A_log"], dtype=np.float32)
    Dv = np.asarray(inp["L0_attn_D"], dtype=np.float32)
    nw = np.asarray(inp["L0_attn_norm_weight"], dtype=np.float32)

    w_in_t = _tile_lhsT(w_in)
    w_out_t = _tile_lhsT(w_out)
    conv_w_t = np.ascontiguousarray(conv_w.reshape(48, 128, 4).transpose(1, 0, 2).reshape(128, 192))
    conv_b_t = np.ascontiguousarray(conv_b.reshape(48, 128).T)
    d_exp = np.ascontiguousarray(np.repeat(Dv, PH).reshape(32, 128).T)
    nw_t = np.ascontiguousarray(nw.reshape(32, 128).T)

    def pad_cols(a, w):
        out = np.zeros((a.shape[0], w), dtype=a.dtype)
        out[:, :a.shape[1]] = a
        return out
    gq_full = pad_cols(np.asarray(inp["L0_ffn_gate_qweight"]), FIP // 8)
    uq_full = pad_cols(np.asarray(inp["L0_ffn_up_qweight"]), FIP // 8)
    gs_full = pad_cols(np.asarray(inp["L0_ffn_gate_scales"], dtype=np.float32), FIP)
    us_full = pad_cols(np.asarray(inp["L0_ffn_up_scales"], dtype=np.float32), FIP)
    gz_full = pad_cols(np.asarray(inp["L0_ffn_gate_qzeros"]), FIP // 8)
    uz_full = pad_cols(np.asarray(inp["L0_ffn_up_qzeros"]), FIP // 8)
    dq_in = np.asarray(inp["L0_ffn_down_qweight"])
    dq_full = np.zeros((FIP, 512), dtype=dq_in.dtype)
    dq_full[:FI] = dq_in
    ds_in = np.asarray(inp["L0_ffn_down_scales"], dtype=np.float32)
    ds_full = np.zeros((88, 4096), dtype=np.float32)
    ds_full[:86] = ds_in
    dz_in = np.asarray(inp["L0_ffn_down_qzeros"])
    dz_full = np.zeros((88, 512), dtype=dz_in.dtype)
    dz_full[:86] = dz_in

    in_maps = []
    for c in range(8):
        b = c // 4
        t0 = (c % 4) * TOK
        xT = np.ascontiguousarray(x[b, t0:t0 + TOK, :].T)
        m_prev = np.full((128, 1), 0.0 if c % 4 == 0 else 1.0, np.float32)
        prevsel = np.zeros((128, 8), np.float32)
        if c % 4 != 0:
            prevsel[:, c - 1] = 1.0
        msel = np.zeros((64, 8), np.float32)
        macc = np.zeros((128, 8), np.float32)
        for j in range(8):
            if j // 4 == c // 4 and j < c:
                msel[:, j] = 1.0
                macc[:, j] = 1.0
        w0, w1 = 176 * c, 176 * (c + 1)
        gq = np.ascontiguousarray(gq_full[:, w0:w1].reshape(32, 128, 176).transpose(1, 0, 2).reshape(128, 32 * 176))
        uq = np.ascontiguousarray(uq_full[:, w0:w1].reshape(32, 128, 176).transpose(1, 0, 2).reshape(128, 32 * 176))
        gsc = np.ascontiguousarray(gs_full[:, 1408 * c:1408 * (c + 1)])
        usc = np.ascontiguousarray(us_full[:, 1408 * c:1408 * (c + 1)])
        gzq = np.ascontiguousarray(gz_full[:, w0:w1])
        uzq = np.ascontiguousarray(uz_full[:, w0:w1])
        dq = np.ascontiguousarray(dq_full[1408 * c:1408 * (c + 1), :].reshape(NTC, 128, 512)
                                  .transpose(1, 0, 2).reshape(128, NTC * 512))
        dsc = np.ascontiguousarray(ds_full[11 * c:11 * (c + 1), :])
        dzq = np.ascontiguousarray(dz_full[11 * c:11 * (c + 1), :])
        in_maps.append(dict(
            xT=xT, w_in_t=w_in_t, w_out_t=w_out_t, conv_w=conv_w_t, conv_b=conv_b_t,
            dt_bias=dt_bias.reshape(64, 1), a_log=a_log.reshape(64, 1), d_exp=d_exp,
            norm_w=nw_t, m_prev=m_prev, prevsel=prevsel, msel=msel, macc=macc,
            gq=gq, uq=uq, gsc=gsc, usc=usc, gzq=gzq, uzq=uzq, dq=dq, dsc=dsc, dzq=dzq))

    res = run_bass_kernel_spmd(nc, in_maps, core_ids=list(range(8)))
    out = np.empty((B, L, D), dtype=np.float32)
    for c in range(8):
        out[c // 4, (c % 4) * TOK:(c % 4 + 1) * TOK, :] = res.results[c]["y_out"].T
    return out


# version 12
# speedup vs baseline: 1.0194x; 1.0194x over previous
import numpy as np
from contextlib import ExitStack

import concourse.bass as bass
import concourse.mybir as mybir
from concourse.bass_utils import run_bass_kernel_spmd

F32 = mybir.dt.float32
BF16 = mybir.dt.bfloat16
I32 = mybir.dt.int32
ALU = mybir.AluOpType
ACT = mybir.ActivationFunctionType

B, L, D = 2, 4096, 4096
H, PH, N, G, KC = 64, 64, 128, 8, 4
I = H * PH
PROJ = 10304
TOK = 1024
NT = 32
Q = 128
NCH = 8
FI = 11008
FIP = 11264
NTC = 11
EPS = 1e-5
NEG = -1e30

ENGINES = ("tensor", "vector", "scalar", "gpsimd", "sync")


class Sched:
    def __init__(self, nc, ctx):
        self.nc = nc
        self.ctx = ctx
        self.plan = True
        self.eng = None
        self.h = None
        self.cnt = {}
        self.marks = {}
        self.last_wait = {}
        self.used = set()
        self.sems = {}
        self.sem_eng = {}

    def init_sems(self, ctx):
        pass

    def _sem(self, name):
        if name not in self.sems:
            self.sems[name] = self.ctx.enter_context(self.nc.semaphore(f"s_{name}"))
            self.cnt[name] = 0
        return self.sems[name]

    def drain(self, snap):
        for sname, eng2 in list(self.sem_eng.items()):
            if eng2 != self.eng:
                continue
            v = self.cnt.get(sname, 0)
            if v > snap.get(sname, 0):
                lw = self.last_wait.get((self.eng, sname), 0)
                if v > lw:
                    self.h.wait_ge(self.sems[sname], v)
                    self.last_wait[(self.eng, sname)] = v

    def op(self, eng, fn, key=None, deps=(), step=1, async_=False, sem=None):
        isasync = async_ or step == 16
        semname = sem if sem is not None else eng
        if step == 16 or async_:
            assert sem is not None, (key, "DMA/collective ops must name a sem")
        if self.plan:
            self.used.add(eng)
            if key is not None or isasync:
                self._sem(semname)
                self.cnt[semname] += step
                if isasync:
                    self.sem_eng[semname] = eng
                if key is not None:
                    assert key not in self.marks, key
                    self.marks[key] = (eng, semname, self.cnt[semname], isasync)
            return
        if eng != self.eng:
            return
        want = {}
        for d in deps:
            if d is None or d not in self.marks:
                continue
            peng, psem, pval, pasync = self.marks[d]
            if pval > want.get(psem, 0):
                want[psem] = pval
        for psem, pval in want.items():
            lw = self.last_wait.get((eng, psem), 0)
            if pval > lw:
                self.h.wait_ge(self.sems[psem], pval)
                self.last_wait[(eng, psem)] = pval
        inst = fn(self.h)
        if (key is not None or isasync) and inst is not None:
            inst.then_inc(self.sems[semname], step)

    def run_phase(self, name, body):
        self.plan = True
        self.used = set()
        snap = dict(self.cnt)
        body()
        self.plan = False
        used = set(self.used)
        with self.nc.Block(name) as blk:
            for e in ENGINES:
                if e not in used:
                    continue

                def make(e2):
                    def f(h):
                        self.eng, self.h = e2, h
                        body()
                        self.drain(snap)
                    return f
                getattr(blk, e)(make(e))


UPTO = 99
P2D_CHUNKS = NCH
P2D_HEADS = H
P2D_PRE = 15
P2D_OPS = 1023


def build_nc():
    nc = bass.Bass("TRN2", target_bir_lowering=False, debug=False)
    dram, sb, ps = {}, {}, {}

    def din(name, shape, dt):
        dram[name] = nc.dram_tensor(name, list(shape), dt, kind="ExternalInput").ap()

    def dint(name, shape, dt):
        dram[name] = nc.dram_tensor(name, list(shape), dt).ap()

    din("xT", (D, TOK), F32)
    WIN_W = [128] * 80 + [64]
    din("w_in_t", (128, 32 * PROJ), F32)
    din("w_out_t", (128, 32 * D), F32)
    din("conv_w", (128, 48 * 4), F32)
    din("conv_b", (128, 48), F32)
    din("dt_bias", (64, 1), F32)
    din("a_log", (64, 1), F32)
    din("d_exp", (128, 32), F32)
    din("norm_w", (128, 32), F32)
    din("m_prev", (128, 1), F32)
    din("prevsel", (128, 8), F32)
    din("msel", (64, 8), F32)
    din("macc", (128, 8), F32)
    din("gq", (128, 32 * 176), I32)
    din("uq", (128, 32 * 176), I32)
    din("gsc", (32, NTC * 128), F32)
    din("usc", (32, NTC * 128), F32)
    din("gzq", (32, 176), I32)
    din("uzq", (32, 176), I32)
    din("dq", (128, NTC * 512), I32)
    din("dsc", (NTC, D), F32)
    din("dzq", (NTC, 512), I32)
    dram["y_out"] = nc.dram_tensor("y_out", [D, TOK], F32, kind="ExternalOutput").ap()
    dint("gate_spill", (32, 128, TOK), BF16)
    dint("x_spill", (32, NCH, 128, Q), BF16)
    dint("bd_in", (128, 144), F32)
    dint("bd_out", (8 * 128, 144), F32)
    dint("a_c_d", (64, TOK), F32)
    dint("a_g_d", (64, TOK), F32)
    dint("ags_in", (128, 4097), F32)
    dint("ags_out", (8 * 128, 4097), F32)
    for j in range(4):
        dint(f"agx_in{j}", (128, NT * 256), BF16)
        dint(f"agx_out{j}", (8 * 128, NT * 256), BF16)
    dint("xl1_spill", (128, NT * TOK), BF16)
    dint("resid_spill", (32, 128, TOK), BF16)
    dint("zs_gu_d", (32, NTC * 128), F32)
    dint("zs_d_d", (NTC, D), F32)
    dint("g_spill", (16, 128, NTC * 512), BF16)
    dint("h_spill", (16, 128, NTC * 512), BF16)
    for j in range(2):
        dint(f"rs_in{j}", (8 * 2048, TOK), BF16)
        dint(f"rs_out{j}", (2048, TOK), BF16)

    ctx = ExitStack()
    S = Sched(nc, ctx)

    def sbuf(name, shape, dt, st=None):
        sb[name] = (st or ctx).enter_context(nc.sbuf_tensor(name, list(shape), dt))
        return sb[name]

    def psum(name, shape, dt, st=None):
        ps[name] = (st or ctx).enter_context(nc.psum_tensor(name, list(shape), dt))
        return ps[name]

    with ctx:
        S.init_sems(ctx)

        sbuf("eyef", (128, 128), F32)
        sbuf("eyeb", (128, 128), BF16)
        sbuf("trineg", (128, 128), F32)
        sbuf("onesrf", (1, 128), F32)
        sbuf("onescb", (128, 1), BF16)
        sbuf("ones64", (64, 128), F32)
        sbuf("cw", (128, 192), F32)
        sbuf("cb", (128, 48), F32)
        sbuf("dtb", (64, 1), F32)
        sbuf("alog", (64, 1), F32)
        sbuf("aneg", (64, 1), F32)
        sbuf("dexp", (128, 32), F32)
        sbuf("nw", (128, 32), F32)
        sbuf("mprev", (128, 1), F32)
        sbuf("s_prevsel", (128, 8), F32)
        sbuf("s_msel", (64, 8), F32)
        sbuf("s_macc", (128, 8), F32)
        sbuf("epsb", (128, 1), F32)
        nc.const_aps.register(F32, EPS, sb["epsb"][:, 0:1])

        stA = ExitStack()
        sbuf("BC", (128, 16 * TOK), BF16, stA)
        sbuf("dtp", (64, TOK), F32, stA)
        sbuf("a_c", (64, TOK), F32, stA)
        sbuf("a_g", (64, TOK), F32, stA)
        sbuf("expa", (64, TOK), F32, stA)
        sbuf("aT", (128, NCH * 64), F32, stA)
        sbuf("dtT", (128, NCH * 64), F32, stA)
        sbuf("hbchead", (128, 48 * 4), F32, stA)
        sbuf("hbctail", (128, 48 * 3), F32, stA)
        sbuf("atot", (64, 1), F32, stA)
        sbuf("arow0", (1, 8 * Q), F32, stA)
        sbuf("arow1", (1, 8 * Q), F32, stA)
        sb["dtraw"] = sb["dtp"]
        sb["expag"] = sb["a_g"]

        def p0():
            for nm, src in (("cw", "conv_w"), ("cb", "conv_b"), ("dtb", "dt_bias"),
                            ("alog", "a_log"), ("dexp", "d_exp"), ("nw", "norm_w"),
                            ("mprev", "m_prev"), ("s_prevsel", "prevsel"),
                            ("s_msel", "msel"), ("s_macc", "macc")):
                S.op("gpsimd", lambda g, nm=nm, src=src: g.dma_start(out=sb[nm][:, :], in_=dram[src]),
                     key=("ld", nm), step=16, sem="c0")
            S.op("gpsimd", lambda g: g.iota(sb["eyef"][:, :], [[-1, 128]], base=0,
                                            channel_multiplier=1, allow_small_or_imprecise_dtypes=True),
                 key=("iota", 0))
            S.op("gpsimd", lambda g: g.iota(sb["trineg"][:, :], [[-1, 128]], base=0,
                                            channel_multiplier=1, allow_small_or_imprecise_dtypes=True),
                 key=("iota", 1))
            S.op("gpsimd", lambda g: g.memset(sb["onesrf"][:, :], 1.0), key=("ms", 0))
            S.op("gpsimd", lambda g: g.memset(sb["onescb"][:, :], 1.0), key=("ms", 1))
            S.op("gpsimd", lambda g: g.memset(sb["ones64"][:, :], 1.0), key=("ms", 2))
            S.op("gpsimd", lambda g: g.memset(sb["epsb"][:, :], EPS), key=("ms", 7))
            S.op("vector", lambda v: v.tensor_single_scalar(sb["eyef"][:, :], sb["eyef"][:, :], 0.0, ALU.is_equal),
                 key=("eyef",), deps=[("iota", 0)])
            S.op("vector", lambda v: v.tensor_copy(sb["eyeb"][:, :], sb["eyef"][:, :]),
                 key=("eyeb",), deps=[("eyef",)])
            S.op("vector", lambda v: v.tensor_single_scalar(sb["trineg"][:, :], sb["trineg"][:, :], 0.0, ALU.is_le),
                 key=("tri0",), deps=[("iota", 1)])
            S.op("vector", lambda v: v.tensor_scalar(sb["trineg"][:, :], sb["trineg"][:, :], -NEG, -NEG,
                                                     ALU.mult, ALU.subtract),
                 key=("tri1",), deps=[("tri0",)])
            S.op("scalar", lambda s: s.activation(sb["aneg"][:, :], sb["alog"][:, :], ACT.Exp),
                 key=("an0",), deps=[("ld", "s_macc")])
            S.op("vector", lambda v: v.tensor_single_scalar(sb["aneg"][:, :], sb["aneg"][:, :], -1.0, ALU.mult),
                 key=("an1",), deps=[("an0",)])

        if UPTO >= 0:
            S.run_phase("p0", p0)

        st1 = ExitStack()
        sbuf("xTb", (128, NT * TOK), BF16, st1)
        sbuf("wbuf0", (128, 32 * 128), BF16, st1)
        sbuf("wbuf1", (128, 32 * 128), BF16, st1)
        for i in range(2):
            sbuf(f"hbcf{i}", (128, TOK), F32, st1)
            sbuf(f"ctmp{i}", (128, TOK), F32, st1)
            sbuf(f"gstage{i}", (128, TOK), BF16, st1)
        for i in range(2):
            sbuf(f"xstage{i}", (128, TOK), BF16, st1)
        psum("pin0", (128, 512), F32, st1)
        psum("pin1", (128, 512), F32, st1)

        win_off = np.concatenate([[0], np.cumsum([32 * w for w in WIN_W])]).astype(int)
        p1_order = list(range(32, 80)) + [80] + list(range(0, 32))

        def p1():
            for k in range(NT):
                S.op("gpsimd", lambda g, k=k: g.dma_start(
                    out=sb["xTb"][:, k * TOK:(k + 1) * TOK],
                    in_=dram["xT"][k * 128:(k + 1) * 128, :]), key=("xld", k), step=16, sem="xld")
            for idx, pi in enumerate(p1_order):
                pw = WIN_W[pi]
                buf = sb[f"wbuf{idx % 2}"]
                dep = [("p1mm", p1_order[idx - 2], 1)] if idx >= 2 else []
                S.op("gpsimd", lambda g, pi=pi, buf=buf: g.dma_start(
                    out=buf[:, 0:32 * WIN_W[pi]],
                    in_=dram["w_in_t"][:, int(win_off[pi]):int(win_off[pi + 1])]),
                    key=("wld", pi), deps=dep, step=16, sem=f"wld{idx % 2}")
                for half in range(2):
                    pst = ps[f"pin{half}"]
                    mmdeps = [("wld", pi)] + ([("xld", k) for k in range(NT)] if idx == 0 else [])
                    if idx >= 1:
                        mmdeps.append(("p1ev", p1_order[idx - 1], half))
                    for k in range(NT):
                        last = k == NT - 1
                        S.op("tensor", lambda t, pst=pst, buf=buf, k=k, pw=pw, half=half: t.matmul(
                            pst[0:pw, :], buf[:, k * pw:(k + 1) * pw],
                            sb["xTb"][:, k * TOK + half * 512: k * TOK + half * 512 + 512],
                            start=(k == 0), stop=last),
                            key=(("p1mm", pi, half) if last else None),
                            deps=(mmdeps if k == 0 else []))
                if 32 <= pi < 80:
                    T = pi - 32
                    hf = sb[f"hbcf{T % 2}"]
                    tmp = sb[f"ctmp{T % 2}"]
                    guard = [("p1silu", T - 2)] if T >= 2 else []
                    for half in range(2):
                        S.op("scalar", lambda s, hf=hf, half=half: s.copy(
                            hf[:, half * 512:half * 512 + 512], ps[f"pin{half}"][:, :]),
                            key=("p1ev", pi, half), deps=[("p1mm", pi, half)] + guard)
                    S.op("vector", lambda v, hf=hf, T=T: v.tensor_copy(sb["hbchead"][:, T * 4:T * 4 + 4], hf[:, 0:4]),
                         key=("p1hd", T), deps=[("p1ev", pi, 0)])
                    S.op("vector", lambda v, hf=hf, T=T: v.tensor_copy(sb["hbctail"][:, T * 3:T * 3 + 3], hf[:, 1021:1024]),
                         key=("p1tl", T), deps=[("p1ev", pi, 1)])
                    S.op("vector", lambda v, tmp=tmp, hf=hf, T=T: v.scalar_tensor_tensor(
                        tmp[:, :], hf[:, :], sb["cw"][:, T * 4 + 3:T * 4 + 4],
                        sb["cb"][:, T:T + 1].broadcast_to((128, TOK)), ALU.mult, ALU.add),
                        key=("p1c0", T), deps=[("p1ev", pi, 0), ("p1ev", pi, 1)])
                    for kk, wi in ((1, 2), (2, 1), (3, 0)):
                        S.op("vector", lambda v, tmp=tmp, hf=hf, T=T, kk=kk, wi=wi: v.scalar_tensor_tensor(
                            tmp[:, kk:TOK], hf[:, 0:TOK - kk], sb["cw"][:, T * 4 + wi:T * 4 + wi + 1],
                            tmp[:, kk:TOK], ALU.mult, ALU.add),
                            key=("p1c", T, kk), deps=[("p1c0", T) if kk == 1 else ("p1c", T, kk - 1)])
                    if T < 32:
                        xst = sb[f"xstage{T % 2}"]
                        sdep = [("p1c", T, 3), ("p1hd", T), ("p1tl", T)]
                        if T >= 2:
                            sdep.append(("p1xs", T - 2, NCH - 1))
                        S.op("scalar", lambda s, xst=xst, tmp=tmp: s.activation(xst[:, :], tmp[:, :], ACT.Silu),
                             key=("p1silu", T), deps=sdep)
                        for c in range(NCH):
                            S.op("sync", lambda s, xst=xst, T=T, c=c: s.dma_start(
                                out=dram["x_spill"][T, c], in_=xst[:, c * Q:(c + 1) * Q]),
                                key=("p1xs", T, c), deps=[("p1silu", T)], step=16, sem=f"xsp{T % 2}")
                    else:
                        bc = T - 32
                        S.op("scalar", lambda s, tmp=tmp, bc=bc: s.activation(
                            sb["BC"][:, bc * TOK:(bc + 1) * TOK], tmp[:, :], ACT.Silu),
                            key=("p1silu", T), deps=[("p1c", T, 3), ("p1hd", T), ("p1tl", T)])
                elif pi == 80:
                    for half in range(2):
                        S.op("scalar", lambda s, half=half: s.copy(
                            sb["dtraw"][:, half * 512:half * 512 + 512], ps[f"pin{half}"][0:64, :]),
                            key=("p1ev", pi, half), deps=[("p1mm", pi, half)])
                else:
                    gst = sb[f"gstage{pi % 2}"]
                    guard = [("p1gs", pi - 2)] if pi >= 2 else []
                    for half in range(2):
                        S.op("scalar", lambda s, gst=gst, half=half: s.copy(
                            gst[:, half * 512:half * 512 + 512], ps[f"pin{half}"][:, :]),
                            key=("p1ev", pi, half), deps=[("p1mm", pi, half)] + guard)
                    S.op("sync", lambda s, gst=gst, pi=pi: s.dma_start(
                        out=dram["gate_spill"][pi], in_=gst[:, :]),
                        key=("p1gs", pi), deps=[("p1ev", pi, 0), ("p1ev", pi, 1)], step=16, sem=f"gsp{pi % 2}")

        if UPTO >= 1:
            S.run_phase("p1", p1)
        st1.close()

        stY = ExitStack()
        sbuf("y", (128, NT * TOK), BF16, stY)
        st2 = ExitStack()
        sbuf("Scur", (128, H * 64), F32, st2)
        sbuf("Sbf0", (128, H * 64), BF16, st2)
        sbuf("Sbf1", (128, H * 64), BF16, st2)
        for i in range(2):
            sbuf(f"evt{i}", (128, 128), F32, st2)
            sbuf(f"Ep{i}", (128, 128), F32, st2)
        sbuf("erep", (128, 64), F32, st2)
        sbuf("erow", (1, 64), F32, st2)
        sbuf("aerep", (128, 64), F32, st2)
        sbuf("expfac", (128, 1), F32, st2)
        psum("ptrf0", (128, 512), F32, st2)
        psum("ptrf1", (128, 512), F32, st2)
        sb["Sin"] = sb["Scur"]
        sb["Sinb"] = sb["Sbf0"]
        st2b = ExitStack()
        sbuf("bdall", (128, 8 * 144), F32, st2b)
        sbuf("bdprev", (128, 144), F32, st2b)
        sbuf("fixt", (128, 3), F32, st2b)
        sbuf("fixs", (128, 3), BF16, st2b)

        def p2a():
            S.op("gpsimd", lambda g: g.dma_start(out=dram["bd_in"], in_=sb["hbctail"][:, :]),
                 key=("bdst",), step=16, sem="bdio")
            S.op("gpsimd", lambda g: g.collective_compute(
                "AllGather", ALU.bypass, replica_groups=[list(range(8))],
                ins=[dram["bd_in"].opt()], outs=[dram["bd_out"].opt()]),
                key=("bdcc",), deps=[("bdst",)], async_=True, sem="ccbd")
            S.op("gpsimd", lambda g: g.dma_start(
                out=sb["bdall"][:, :].rearrange("p (r w) -> p r w", r=8),
                in_=dram["bd_out"].rearrange("(r p) w -> p r w", p=128)),
                key=("bdld",), deps=[("bdcc",)], step=16, sem="bdio")
            S.op("vector", lambda v: v.memset(sb["bdprev"][:, :], 0.0), key=("bd0",))
            for j in range(8):
                S.op("vector", lambda v, j=j: v.scalar_tensor_tensor(
                    sb["bdprev"][:, :], sb["bdall"][:, j * 144:(j + 1) * 144],
                    sb["s_prevsel"][:, j:j + 1], sb["bdprev"][:, :], ALU.mult, ALU.add),
                    key=("bdacc", j), deps=[("bdld",), ("bd0",) if j == 0 else ("bdacc", j - 1)])
            S.op("vector", lambda v: v.tensor_single_scalar(sb["bdprev"][:, :], sb["bdprev"][:, :],
                                                            sb["mprev"][:, 0:1], ALU.mult),
                 key=("bdm",), deps=[("bdacc", 7)])

        if UPTO >= 2:
            S.run_phase("p2a", p2a)

        def p2b():
            for T in range(48):
                bd = (sb["bdprev"], T * 3)
                hh = (sb["hbchead"], T * 4)
                S.op("vector", lambda v, T=T, hh=hh: v.scalar_tensor_tensor(
                    sb["fixt"][:, :], hh[0][:, hh[1]:hh[1] + 3], sb["cw"][:, T * 4 + 3:T * 4 + 4],
                    sb["cb"][:, T:T + 1].broadcast_to((128, 3)), ALU.mult, ALU.add),
                    key=("f0", T), deps=([("bdm",)] if T == 0 else [("fxw", T - 1)]))
                steps = [
                    (2, [(bd, 2, 0, 1), (hh, 0, 1, 2)]),
                    (1, [(bd, 1, 0, 2), (hh, 0, 2, 1)]),
                    (0, [(bd, 0, 0, 3)]),
                ]
                pk = ("f0", T)
                for si, (wi, parts) in enumerate(steps):
                    for pj, (src, so, do, cnt) in enumerate(parts):
                        S.op("vector", lambda v, T=T, wi=wi, src=src, so=so, do=do, cnt=cnt: v.scalar_tensor_tensor(
                            sb["fixt"][:, do:do + cnt], src[0][:, src[1] + so:src[1] + so + cnt],
                            sb["cw"][:, T * 4 + wi:T * 4 + wi + 1], sb["fixt"][:, do:do + cnt],
                            ALU.mult, ALU.add),
                            key=("fx", T, si, pj), deps=[pk])
                        pk = ("fx", T, si, pj)
                S.op("scalar", lambda s: s.activation(sb["fixs"][:, :], sb["fixt"][:, :], ACT.Silu),
                     key=("fsl", T), deps=[pk])
                if T < 32:
                    S.op("sync", lambda s, T=T: s.dma_start(out=dram["x_spill"][T, 0][:, 0:3],
                                                            in_=sb["fixs"][:, :],
                                                            allow_slow_non_contiguous=True),
                         key=("fxw", T), deps=[("fsl", T)], step=16, sem="fxio")
                else:
                    S.op("vector", lambda v, T=T: v.tensor_copy(
                        sb["BC"][:, (T - 32) * TOK:(T - 32) * TOK + 3], sb["fixs"][:, :]),
                        key=("fxw", T), deps=[("fsl", T)])
            S.op("scalar", lambda s: s.activation(sb["dtp"][:, :], sb["dtraw"][:, :], ACT.Exp,
                                                  bias=sb["dtb"][:, 0:1]), key=("dtp0",))
            S.op("vector", lambda v: v.tensor_single_scalar(sb["dtp"][:, :], sb["dtp"][:, :], 1.0, ALU.add),
                 key=("dtp1",), deps=[("dtp0",)])
            S.op("scalar", lambda s: s.activation(sb["dtp"][:, :], sb["dtp"][:, :], ACT.Ln),
                 key=("dtp",), deps=[("dtp1",)])
            S.op("vector", lambda v: v.tensor_single_scalar(sb["a_c"][:, :], sb["dtp"][:, :],
                                                            sb["aneg"][:, 0:1], ALU.mult),
                 key=("dta",), deps=[("dtp",)])
            for c in range(NCH):
                S.op("vector", lambda v, c=c: v.tensor_tensor_scan(
                    sb["a_c"][:, c * Q:(c + 1) * Q], sb["ones64"][:, :], sb["a_c"][:, c * Q:(c + 1) * Q],
                    0.0, ALU.mult, ALU.add), key=("scan", c), deps=[("dta",)])
            S.op("vector", lambda v: v.memset(sb["atot"][:, :], 0.0), key=("off0",))
            for c in range(NCH):
                S.op("vector", lambda v, c=c: v.tensor_tensor(
                    sb["a_g"][:, c * Q:(c + 1) * Q], sb["a_c"][:, c * Q:(c + 1) * Q],
                    sb["atot"][:, 0:1].broadcast_to((64, Q)), ALU.add),
                    key=("ago", c), deps=[("scan", c), ("off0",) if c == 0 else ("offu", c - 1)])
                S.op("vector", lambda v, c=c: v.tensor_tensor(
                    sb["atot"][:, 0:1], sb["atot"][:, 0:1], sb["a_c"][:, (c + 1) * Q - 1:(c + 1) * Q], ALU.add),
                    key=("offu", c), deps=[("ago", c)])
            S.op("sync", lambda s: s.dma_start(out=dram["a_c_d"], in_=sb["a_c"][:, :]),
                 key=("acd",), deps=[("scan", NCH - 1)], step=16, sem="acd")
            S.op("sync", lambda s: s.dma_start(out=dram["a_g_d"], in_=sb["a_g"][:, :]),
                 key=("agd",), deps=[("ago", NCH - 1)], step=16, sem="agd")
            S.op("scalar", lambda s: s.activation(sb["expa"][:, :], sb["a_c"][:, :], ACT.Exp),
                 key=("expa",), deps=[("scan", NCH - 1)])
            S.op("scalar", lambda s: s.activation(sb["expag"][:, :], sb["a_g"][:, :], ACT.Exp),
                 key=("expag",), deps=[("ago", NCH - 1), ("agd",)])
            for c in range(NCH):
                pt = ps[f"ptrf{c % 2}"]
                dep = [("scan", c)] + ([("aTe", c - 2), ("dtTe", c - 2)] if c >= 2 else [])
                S.op("tensor", lambda t, c=c, pt=pt: t.transpose(
                    pt[:, 0:64], sb["a_c"][:, c * Q:(c + 1) * Q], sb["eyef"][0:64, 0:64]),
                    key=("aTt", c), deps=dep)
                S.op("vector", lambda v, c=c, pt=pt: v.tensor_copy(sb["aT"][:, c * 64:(c + 1) * 64], pt[:, 0:64]),
                     key=("aTe", c), deps=[("aTt", c)])
                S.op("tensor", lambda t, c=c, pt=pt: t.transpose(
                    pt[:, 64:128], sb["dtp"][:, c * Q:(c + 1) * Q], sb["eyef"][0:64, 0:64]),
                    key=("dtTt", c), deps=[("dtp",), ("aTe", c)])
                S.op("vector", lambda v, c=c, pt=pt: v.tensor_copy(sb["dtT"][:, c * 64:(c + 1) * 64], pt[:, 64:128]),
                     key=("dtTe", c), deps=[("dtTt", c)])

        if UPTO >= 3:
            S.run_phase("p2b", p2b)

        st2b.close()
        st2d = ExitStack()
        for i in range(2):
            sbuf(f"xc{i}", (128, 32 * Q), BF16, st2d)
            sbuf(f"dbuf{i}", (128, 128), F32, st2d)
            sbuf(f"m2b{i}", (128, 128), BF16, st2d)
            sbuf(f"x1t{i}", (128, 64), BF16, st2d)
            sbuf(f"x2t{i}", (128, 64), BF16, st2d)
            sbuf(f"btb{i}", (128, 128), BF16, st2d)
        sbuf("btg", (128, 8 * 128), BF16, st2d)
        sbuf("negaT", (128, 64), F32, st2d)
        sbuf("arm0", (128, 512), F32, st2d)
        sbuf("arm1", (128, 512), F32, st2d)
        psum("pg2", (128, 512), F32, st2d)
        psum("pyA", (128, 512), F32, st2d)
        psum("pyB", (128, 512), F32, st2d)
        psum("pdS", (128, 512), F32, st2d)
        psum("ptb", (128, 1024), BF16, st2d)

        def p2d():
            S.op("vector", lambda v: v.memset(sb["Scur"][:, :], 0.0), key=("sc0",))
            S.op("scalar", lambda s: s.copy(sb["Sbf0"][:, :], sb["Scur"][:, :]), key=("sb0",), deps=[("sc0",)])
            for c in range(P2D_CHUNKS):
                if not (P2D_PRE & 1):
                    pass
                for T in (range(32) if (P2D_PRE & 1) else []):
                    dep = [("p1xs", T, c)] + ([("fxw", 31)] if c == 0 else [])
                    if c >= 2:
                        dep.append(("yev", c - 2, 31))
                    S.op("sync", lambda s, T=T, c=c: s.dma_start(
                        out=sb[f"xc{c % 2}"][:, T * Q:(T + 1) * Q], in_=dram["x_spill"][T, c]),
                        key=("xcl", c, T), deps=dep, step=16, sem=f"xc{c % 2}")
                for gg8 in (range(8) if (P2D_PRE & 2) else []):
                    adep = [("acd",)]
                    pgi = (c * 8 + gg8) - 2
                    if pgi >= 0:
                        adep.append(("arp", pgi // 8, (pgi % 8) * 2 + 1))
                    S.op("sync", lambda s, c=c, gg8=gg8: s.dma_start(
                        out=sb[f"arow{gg8 % 2}"][:, :].rearrange("o (h t) -> o h t", h=8),
                        in_=dram["a_c_d"][gg8 * 8:(gg8 + 1) * 8, c * Q:(c + 1) * Q]),
                        key=("arw", c, gg8), deps=adep, step=16, sem=f"ar{gg8 % 2}")
                if not (P2D_PRE & 4):
                    continue
                S.op("tensor", lambda t, c=c: t.transpose(
                    ps["ptrf0"][0:1, 0:64], sb["a_c"][:, (c + 1) * Q - 1:(c + 1) * Q],
                    sb["eyef"][0:64, 0:64]),
                    key=("aet", c), deps=[("scan", c)] + ([("epx", c - 1, 63)] if c >= 1 else []))
                S.op("vector", lambda v: v.tensor_copy(sb["erow"][:, :], ps["ptrf0"][0:1, 0:64]),
                     key=("aee", c), deps=[("aet", c)])
                S.op("tensor", lambda t: t.matmul(
                    ps["ptrf1"][:, 0:64], sb["onesrf"][0:1, :], sb["erow"][0:1, 0:64],
                    start=True, stop=True),
                    key=("aert", c), deps=[("aee", c)])
                S.op("vector", lambda v: v.tensor_copy(sb["aerep"][:, :], ps["ptrf1"][:, 0:64]),
                     key=("aerE", c), deps=[("aert", c)] + ([("xf", c - 1, 63)] if c >= 1 else []))
                S.op("scalar", lambda s: s.activation(sb["erep"][:, :], sb["aerep"][:, :], ACT.Exp),
                     key=("erepx", c), deps=[("aerE", c)])
                S.op("vector", lambda v, c=c: v.tensor_single_scalar(
                    sb["negaT"][:, :], sb["aT"][:, c * 64:(c + 1) * 64], -1.0, ALU.mult),
                    key=("nga", c), deps=[("aTe", c)] + ([("xpd", c - 1, 63)] if c >= 1 else []))
                for g in (range(8) if (P2D_PRE & 8) else []):
                    btdep = [("bte", c, g - 1)] if g >= 1 else ([("bte", c - 1, 7), ("x1", c - 1, 63)] if c >= 1 else [])
                    if c == 0:
                        btdep += [("fxw", 32 + g)]
                    S.op("tensor", lambda t, c=c, g=g: t.transpose(
                        ps["ptb"][:, 0:128], sb["BC"][:, g * TOK + c * Q: g * TOK + (c + 1) * Q],
                        sb["eyeb"][:, :]),
                        key=("btt", c, g), deps=btdep)
                    S.op("scalar", lambda s, g=g: s.copy(sb["btg"][:, g * 128:(g + 1) * 128], ps["ptb"][:, 0:128]),
                         key=("bte", c, g), deps=[("btt", c, g)])
                for h in range(P2D_HEADS):
                    g = h // 8
                    pp = h // 2
                    hr = (h % 2) * 64
                    i2 = h % 2
                    sbf_r = sb[f"Sbf{c % 2}"]
                    sbf_w = sb[f"Sbf{(c + 1) % 2}"]
                    if (P2D_OPS & 64) and h % 8 == 0:
                        g2dep = []
                        if c == 0:
                            g2dep += [("fxw", 32 + g), ("fxw", 40 + g)]
                        if g >= 1:
                            g2dep.append(("m2", c, g * 8 - 1))
                        elif c >= 1:
                            g2dep.append(("m2", c - 1, 63))
                        S.op("tensor", lambda t, c=c, g=g: t.matmul(
                            ps["pg2"][:, 0:128], sb["BC"][:, g * TOK + c * Q:g * TOK + (c + 1) * Q],
                            sb["BC"][:, (8 + g) * TOK + c * Q:(8 + g) * TOK + (c + 1) * Q],
                            start=True, stop=True),
                            key=("g2", c, g), deps=g2dep)
                    if (P2D_OPS & 1) and h % 4 == 0:
                        S.op("tensor", lambda t, c=c, h=h: t.matmul(
                            ps["ptrf0"][:, 0:512], sb["onesrf"][0:1, :],
                            sb[f"arow{(h // 8) % 2}"][0:1, (h % 8) * Q:(h % 8 + 4) * Q],
                            start=True, stop=True),
                            key=("arp", c, h // 4), deps=[("arw", c, h // 8)] +
                            ([("epx", c, h - 1)] if h >= 1 else
                             ([("epx", c - 1, 63)] if c >= 1 else [])))
                    if (P2D_OPS & 1) and h % 4 == 0:
                        g4 = h // 4
                        armb = sb[f"arm{g4 % 2}"]
                        S.op("vector", lambda v, armb=armb: v.tensor_tensor(
                            armb[:, :].rearrange("p (a b) -> p a b", b=128),
                            ps["ptrf0"][:, 0:512].rearrange("p (a b) -> p a b", b=128),
                            sb["trineg"][:, :].unsqueeze(1).broadcast_to((128, 4, 128)),
                            ALU.add),
                            key=("arm", c, g4), deps=[("arp", c, g4),
                                                      ("xpd", c, (g4 - 2) * 4 + 3) if g4 >= 2 else
                                                      (("xpd", c - 1, (14 + g4) * 4 + 3) if c >= 1 else None)]),
                    if P2D_OPS & 512:
                        epdep = [("arp", c, h // 4), ("arm", c, h // 4)]
                        if pp >= 2:
                            epdep.append(("ye1", c, pp - 2))
                        elif c >= 1:
                            epdep.append(("ye1", c - 1, 30 + pp))
                        S.op("scalar", lambda s, hr=hr, pp=pp, h=h: s.activation(
                            sb[f"Ep{pp % 2}"][hr:hr + 64, :],
                            ps["ptrf0"][hr:hr + 64, (h % 4) * 128:(h % 4) * 128 + 128], ACT.Exp),
                            key=("epx", c, h), deps=epdep)
                    if P2D_OPS & 128:
                        S.op("scalar", lambda s, c=c, h=h, i2=i2: s.activation(
                            sb[f"dbuf{i2}"][:, :], sb[f"arm{(h // 4) % 2}"][:, (h % 4) * 128:(h % 4) * 128 + 128],
                            ACT.Exp, bias=sb["negaT"][:, h:h + 1]),
                            key=("xpd", c, h), deps=[("arm", c, h // 4), ("nga", c),
                                                     ("m2", c, h - 2) if h >= 2 else None])
                    if P2D_OPS & 256:
                        S.op("vector", lambda v, i2=i2: v.tensor_tensor(
                            sb[f"m2b{i2}"][:, :], sb[f"dbuf{i2}"][:, :], ps["pg2"][:, 0:128], ALU.mult),
                            key=("m2", c, h), deps=[("xpd", c, h), ("g2", c, g), ("mmA", c, h - 2) if h >= 2 else None])
                    if P2D_OPS & 2:
                        S.op("tensor", lambda t, c=c, pp=pp, hr=hr: t.transpose(
                            ps["ptb"][:, 128:192], sb[f"xc{c % 2}"][hr:hr + 64, pp * Q:(pp + 1) * Q],
                            sb["eyeb"][hr:hr + 64, hr:hr + 64]),
                            key=("xtt", c, h), deps=[("xcl", c, 31), ("bte", c, 7)] +
                            ([("x2", c, h - 1)] if h >= 1 else ([("x2", c - 1, 63)] if c >= 1 else [])))
                    if P2D_OPS & 2:
                        S.op("vector", lambda v, c=c, h=h, i2=i2: v.tensor_single_scalar(
                            sb[f"x1t{i2}"][:, :], ps["ptb"][:, 128:192], sb["dtT"][:, c * 64 + h:c * 64 + h + 1], ALU.mult),
                            key=("x1", c, h), deps=[("xtt", c, h), ("dtTe", c)])
                    if P2D_OPS & 2:
                        S.op("scalar", lambda s, c=c, h=h: s.activation(
                            sb["expfac"][:, :], sb["aT"][:, c * 64 + h:c * 64 + h + 1], ACT.Exp,
                            bias=sb["aerep"][:, h:h + 1], scale=-1.0),
                            key=("xf", c, h), deps=[("aerE", c)] + ([("bt", c, h - 1)] if h >= 1 else [])),
                    if P2D_OPS & 2:
                        S.op("vector", lambda v, i2=i2: v.tensor_single_scalar(
                            sb[f"x2t{i2}"][:, :], sb[f"x1t{i2}"][:, :], sb["expfac"][:, 0:1], ALU.mult),
                            key=("x2", c, h), deps=[("x1", c, h), ("xf", c, h)])
                    if P2D_OPS & 2:
                        S.op("vector", lambda v, g=g, i2=i2: v.tensor_single_scalar(
                            sb[f"btb{i2}"][:, :], sb["btg"][:, g * 128:(g + 1) * 128], sb["expfac"][:, 0:1], ALU.mult),
                            key=("bt", c, h), deps=[("bte", c, g), ("x2", c, h)])
                    mmadep = [("m2", c, h), ("x1", c, h)]
                    if not (P2D_OPS & 4):
                        mmadep = None
                    if mmadep is not None and h % 2 == 0:
                        mmadep.append(("ye2", c, pp - 1) if pp >= 1 else (("ye2", c - 1, 31) if c >= 1 else None))
                    if mmadep is not None:
                        S.op("tensor", lambda t, hr=hr, i2=i2: t.matmul(
                            ps["pyA"][hr:hr + 64, 0:128], sb[f"x1t{i2}"][:, :], sb[f"m2b{i2}"][:, :],
                            start=True, stop=True),
                            key=("mmA", c, h), deps=mmadep)
                    mmbdep = [("sb0",)] if c == 0 else [("sbc", c - 1, h)]
                    if not (P2D_OPS & 8):
                        mmbdep = None
                    if mmbdep is not None and h % 2 == 0:
                        mmbdep.append(("ye1", c, pp - 1) if pp >= 1 else (("ye1", c - 1, 31) if c >= 1 else None))
                    if mmbdep is not None:
                        S.op("tensor", lambda t, hr=hr, h=h, g=g, c=c, sbf_r=sbf_r: t.matmul(
                            ps["pyB"][hr:hr + 64, 0:128], sbf_r[:, h * 64:(h + 1) * 64],
                            sb["BC"][:, (8 + g) * TOK + c * Q:(8 + g) * TOK + (c + 1) * Q],
                            start=True, stop=True),
                            key=("mmB", c, h), deps=mmbdep)
                    if P2D_OPS & 16:
                        S.op("tensor", lambda t, i2=i2: t.matmul(
                            ps["pdS"][:, 0:64], sb[f"btb{i2}"][:, :], sb[f"x2t{i2}"][:, :],
                            start=True, stop=True),
                            key=("mmD", c, h), deps=[("bt", c, h), ("x2", c, h)] +
                            ([("sch", c, h - 1)] if h >= 1 else ([("sch", c - 1, 63)] if c >= 1 else [])))
                        S.op("vector", lambda v, h=h: v.scalar_tensor_tensor(
                            sb["Scur"][:, h * 64:(h + 1) * 64], sb["Scur"][:, h * 64:(h + 1) * 64],
                            sb["erep"][:, h:h + 1], ps["pdS"][:, 0:64], ALU.mult, ALU.add),
                            key=("sch", c, h), deps=[("mmD", c, h), ("erepx", c)])
                        S.op("scalar", lambda s, h=h, sbf_w=sbf_w: s.copy(
                            sbf_w[:, h * 64:(h + 1) * 64], sb["Scur"][:, h * 64:(h + 1) * 64]),
                            key=("sbc", c, h), deps=[("sch", c, h)])
                    if (P2D_OPS & 32) and h % 2 == 1:
                        ev = sb[f"evt{pp % 2}"]
                        S.op("vector", lambda v, ev=ev, pp=pp: v.tensor_tensor(ev[:, :], ps["pyB"][:, 0:128],
                                                                               sb[f"Ep{pp % 2}"][:, :], ALU.mult),
                             key=("ye1", c, pp), deps=[("epx", c, h), ("mmB", c, h),
                                                       ("yev", c, pp - 2) if pp >= 2 else (("yev", c - 1, 30 + pp) if c >= 1 else None)])
                        S.op("vector", lambda v, ev=ev: v.tensor_tensor(ev[:, :], ev[:, :], ps["pyA"][:, 0:128], ALU.add),
                             key=("ye2", c, pp), deps=[("ye1", c, pp), ("mmA", c, h)])
                        S.op("vector", lambda v, ev=ev, c=c, pp=pp: v.scalar_tensor_tensor(
                            sb["y"][:, pp * TOK + c * Q:pp * TOK + (c + 1) * Q],
                            sb[f"xc{c % 2}"][:, pp * Q:(pp + 1) * Q], sb["dexp"][:, pp:pp + 1],
                            ev[:, :], ALU.mult, ALU.add),
                            key=("yev", c, pp), deps=[("ye2", c, pp)])

        if UPTO >= 4:
            S.run_phase("p2d", p2d)
        st2d.close()

        st3 = ExitStack()
        sbuf("sj", (128, 4096), F32, st3)
        sbuf("tv", (64, 1), F32, st3)
        sbuf("ejr", (128, 64), F32, st3)
        sbuf("ejrow", (1, 64), F32, st3)
        psum("pyA3", (128, 512), F32, st3)

        def p3():
            S.op("sync", lambda s: s.dma_start(out=dram["ags_in"][:, 0:4096], in_=sb["Scur"][:, :]),
                 key=("agsst",), deps=[("sch", NCH - 1, 63)], step=16, sem="agsio")
            S.op("sync", lambda s: s.dma_start(out=dram["ags_in"][0:64, 4096:4097], in_=sb["atot"][:, :],
                                               allow_slow_non_contiguous=True),
                 key=("agsst2",), deps=[("offu", NCH - 1)], step=16, sem="agsio")
            S.op("sync", lambda s: s.dma_start(out=dram["ags_in"][64:128, 4096:4097], in_=sb["atot"][:, :],
                                               allow_slow_non_contiguous=True),
                 key=("agsst3",), deps=[("offu", NCH - 1)], step=16, sem="agsio")
            S.op("gpsimd", lambda g: g.collective_compute(
                "AllGather", ALU.bypass, replica_groups=[list(range(8))],
                ins=[dram["ags_in"].opt()], outs=[dram["ags_out"].opt()]),
                key=("agscc",), deps=[("agsst",), ("agsst2",), ("agsst3",)], async_=True, sem="ccags")
            S.op("vector", lambda v: v.memset(sb["Sin"][:, :], 0.0), key=("si0",), deps=[("agsst",), ("agsst2",), ("agsst3",)])
            for j in range(7):
                bj = "sj"
                S.op("gpsimd", lambda g, j=j, bj=bj: g.dma_start(
                    out=sb[bj][:, :], in_=dram["ags_out"][j * 128:(j + 1) * 128, 0:4096]),
                    key=("sjl", j), deps=[("agscc",)] + ([("acc2", j - 1)] if j >= 1 else []), step=16, sem="sjl")
                S.op("gpsimd", lambda g, j=j: g.dma_start(
                    out=sb["tv"][:, :], in_=dram["ags_out"][j * 128:j * 128 + 64, 4096:4097],
                    allow_slow_non_contiguous=True),
                    key=("tvl", j), deps=[("agscc",)] + ([("tvt", j - 1)] if j >= 1 else []), step=16, sem="tvl")
                S.op("vector", lambda v, j=j: v.tensor_tensor(sb["tv"][:, :], sb["tv"][:, :],
                                                              sb["s_msel"][:, j:j + 1], ALU.mult),
                     key=("tvm", j), deps=[("tvl", j)])
                S.op("scalar", lambda s: s.activation(sb["tv"][:, :], sb["tv"][:, :], ACT.Exp),
                     key=("tvx", j), deps=[("tvm", j)])
                S.op("tensor", lambda t: t.transpose(ps["ptrf0"][0:1, 0:64], sb["tv"][:, :],
                                                     sb["eyef"][0:64, 0:64]),
                     key=("tvt", j), deps=[("tvx", j)] + ([("tve", j - 1)] if j >= 1 else []))
                S.op("vector", lambda v: v.tensor_copy(sb["ejrow"][:, :], ps["ptrf0"][0:1, 0:64]),
                     key=("tve", j), deps=[("tvt", j)])
                S.op("tensor", lambda t: t.matmul(ps["ptrf1"][:, 0:64], sb["onesrf"][0:1, :],
                                                  sb["ejrow"][0:1, 0:64], start=True, stop=True),
                     key=("ejrt", j), deps=[("tve", j)])
                S.op("vector", lambda v: v.tensor_copy(sb["ejr"][:, :], ps["ptrf1"][:, 0:64]),
                     key=("ejre", j), deps=[("ejrt", j)])
                S.op("vector", lambda v: v.tensor_tensor(
                    sb["Sin"][:, :].rearrange("n (h p) -> n h p", h=64),
                    sb["Sin"][:, :].rearrange("n (h p) -> n h p", h=64),
                    sb["ejr"][:, :].rearrange("n (h p) -> n h p", p=1).broadcast_to((128, 64, 64)),
                    ALU.mult),
                    key=("acc1", j), deps=[("ejre", j), ("si0",) if j == 0 else ("acc2", j - 1)])
                S.op("vector", lambda v, j=j, bj=bj: v.scalar_tensor_tensor(
                    sb["Sin"][:, :], sb[bj][:, :], sb["s_macc"][:, j:j + 1], sb["Sin"][:, :],
                    ALU.mult, ALU.add),
                    key=("acc2", j), deps=[("acc1", j), ("sjl", j)])
            S.op("scalar", lambda s: s.copy(sb["Sinb"][:, :], sb["Sin"][:, :]),
                 key=("sinb",), deps=[("acc2", 6)])
            for c in range(NCH):
                for gg8 in range(8):
                    adep = [("agd",)]
                    pgi = (c * 8 + gg8) - 2
                    if pgi >= 0:
                        adep.append(("agrp", pgi // 8, (pgi % 8) * 2 + 1))
                    S.op("sync", lambda s, c=c, gg8=gg8: s.dma_start(
                        out=sb[f"arow{gg8 % 2}"][:, :].rearrange("o (h t) -> o h t", h=8),
                        in_=dram["a_g_d"][gg8 * 8:(gg8 + 1) * 8, c * Q:(c + 1) * Q]),
                        key=("agw", c, gg8), deps=adep, step=16, sem=f"ar{gg8 % 2}")
                for h in range(H):
                    g = h // 8
                    hr = (h % 2) * 64
                    pp = h // 2
                    cdep = [("sinb",)]
                    if h % 2 == 0:
                        pv = ("cy2", c, pp - 1) if pp >= 1 else (("cy2", c - 1, 31) if c >= 1 else None)
                        cdep.append(pv)
                    S.op("tensor", lambda t, c=c, h=h, g=g, hr=hr: t.matmul(
                        ps["pyA3"][hr:hr + 64, 0:128], sb["Sinb"][:, h * 64:(h + 1) * 64],
                        sb["BC"][:, (8 + g) * TOK + c * Q:(8 + g) * TOK + (c + 1) * Q],
                        start=True, stop=True),
                        key=("cmm", c, h), deps=cdep)
                    if h % 4 == 0:
                        agdep = [("agw", c, h // 8)]
                        if h >= 1:
                            agdep.append(("epx3", c, h - 1))
                        elif c >= 1:
                            agdep.append(("epx3", c - 1, 63))
                        else:
                            agdep.append(("tve", 6))
                        S.op("tensor", lambda t, c=c, h=h: t.matmul(
                            ps["ptrf0"][:, 0:512], sb["onesrf"][0:1, :],
                            sb[f"arow{(h // 8) % 2}"][0:1, (h % 8) * Q:(h % 8 + 4) * Q],
                            start=True, stop=True),
                            key=("agrp", c, h // 4), deps=agdep)
                    epdep = [("agrp", c, h // 4)]
                    if pp >= 2:
                        epdep.append(("cy1", c, pp - 2))
                    elif c >= 1:
                        epdep.append(("cy1", c - 1, 30 + pp))
                    S.op("scalar", lambda s, hr=hr, pp=pp, h=h: s.activation(
                        sb[f"Ep{pp % 2}"][hr:hr + 64, :],
                        ps["ptrf0"][hr:hr + 64, (h % 4) * 128:(h % 4) * 128 + 128], ACT.Exp),
                        key=("epx3", c, h), deps=epdep)
                    if (P2D_OPS & 32) and h % 2 == 1:
                        ev = sb[f"evt{pp % 2}"]
                        S.op("vector", lambda v, ev=ev, pp=pp: v.tensor_tensor(ev[:, :], ps["pyA3"][:, 0:128],
                                                                               sb[f"Ep{pp % 2}"][:, :], ALU.mult),
                             key=("cy1", c, pp), deps=[("cmm", c, h), ("epx3", c, h)])
                        S.op("vector", lambda v, ev=ev, c=c, pp=pp: v.tensor_tensor(
                            sb["y"][:, pp * TOK + c * Q:pp * TOK + (c + 1) * Q],
                            sb["y"][:, pp * TOK + c * Q:pp * TOK + (c + 1) * Q], ev[:, :], ALU.add),
                            key=("cy2", c, pp), deps=[("cy1", c, pp)])

        if UPTO >= 5:
            S.run_phase("p3", p3)
        st3.close()
        st2.close()

        st4 = ExitStack()
        sbuf("wbufa", (128, 32 * 128), BF16, st4)
        sbuf("wbufb", (128, 32 * 128), BF16, st4)
        for i in range(4):
            sbuf(f"gld{i}", (128, TOK), BF16, st4)
        for i in range(2):
            sbuf(f"gsil{i}", (128, TOK), BF16, st4)
            sbuf(f"xres{i}", (128, TOK), F32, st4)
            sbuf(f"rstg{i}", (128, TOK), BF16, st4)
            sbuf(f"xstg{i}", (128, TOK), BF16, st4)
        sbuf("sqb", (128, TOK), BF16, st4)
        sbuf("rsvrow", (1, TOK), F32, st4)
        sbuf("stu", (1, TOK), F32, st4)
        sbuf("stq", (1, TOK), F32, st4)
        sbuf("stt", (1, TOK), F32, st4)
        sbuf("tmpf", (128, 512), F32, st4)
        psum("po0", (128, 512), F32, st4)
        psum("po1", (128, 512), F32, st4)
        psum("prr0", (128, 512), F32, st4)
        psum("prr1", (128, 512), F32, st4)
        psum("pst0", (128, 512), F32, st4)
        psum("pst1", (128, 512), F32, st4)
        psum("pst2", (128, 512), F32, st4)
        psum("pst3", (128, 512), F32, st4)

        def p4():
            for g in range(8):
                for ti in range(4):
                    T = g * 4 + ti
                    S.op("scalar", lambda s, T=T: s.activation(
                        sb["sqb"][:, :], sb["y"][:, T * TOK:(T + 1) * TOK], ACT.Square),
                        key=("rsq", T), deps=[("cy2", NCH - 1, 31)] if T == 0 else [("rst", T - 1, 1)])
                    for half in range(2):
                        S.op("tensor", lambda t, half=half, ti=ti: t.matmul(
                            ps[f"pst{half}"][0:1, :], sb["onescb"][:, :],
                            sb["sqb"][:, half * 512:half * 512 + 512],
                            start=(ti == 0), stop=(ti == 3)),
                            key=("rst", T, half),
                            deps=[("rsq", T)] + ([("rcp", g - 1)] if ti == 0 and g >= 1 else []))
                for half in range(2):
                    S.op("scalar", lambda s, half=half: s.activation(
                        sb["rsvrow"][0:1, half * 512:half * 512 + 512], ps[f"pst{half}"][0:1, :],
                        ACT.Sqrt, bias=EPS, scale=1.0 / 512.0),
                        key=("rsq2", g, half), deps=[("rst", g * 4 + 3, half)])
                S.op("vector", lambda v: v.reciprocal(sb["rsvrow"][:, :], sb["rsvrow"][:, :]),
                     key=("rcp", g), deps=[("rsq2", g, 0), ("rsq2", g, 1)])
                for half in range(2):
                    S.op("tensor", lambda t, g=g, half=half: t.matmul(
                        ps[f"prr{half}"][:, :], sb["onesrf"][0:1, :],
                        sb["rsvrow"][0:1, half * 512:half * 512 + 512], start=True, stop=True),
                        key=("rrp", g, half), deps=[("rcp", g)] +
                        ([("yng", (g - 1) * 4 + 3)] if g >= 1 else []))
                for ti in range(4):
                    T = g * 4 + ti
                    for half in range(2):
                        S.op("vector", lambda v, T=T, half=half: v.scalar_tensor_tensor(
                            sb["y"][:, T * TOK + half * 512:T * TOK + half * 512 + 512],
                            sb["y"][:, T * TOK + half * 512:T * TOK + half * 512 + 512],
                            sb["nw"][:, T:T + 1], ps[f"prr{half}"][:, :], ALU.mult, ALU.mult),
                            key=("ynr", T, half), deps=[("rrp", g, half)])
                    S.op("sync", lambda s, T=T: s.dma_start(out=sb[f"gld{T % 4}"][:, :], in_=dram["gate_spill"][T]),
                         key=("gll", T), deps=[("yng", T - 4)] if T >= 4 else [], step=16, sem=f"gll{T % 4}")
                    S.op("scalar", lambda s, T=T: s.activation(sb[f"gsil{T % 2}"][:, :], sb[f"gld{T % 4}"][:, :], ACT.Silu),
                         key=("gsl", T), deps=[("gll", T)] + ([("yng", T - 2)] if T >= 2 else []))
                    S.op("vector", lambda v, T=T: v.tensor_tensor(
                        sb["y"][:, T * TOK:(T + 1) * TOK], sb["y"][:, T * TOK:(T + 1) * TOK],
                        sb[f"gsil{T % 2}"][:, :], ALU.mult),
                        key=("yng", T), deps=[("gsl", T), ("ynr", T, 0), ("ynr", T, 1)])
            for dt in range(NT):
                wb = sb["wbufa" if dt % 2 == 0 else "wbufb"]
                S.op("gpsimd", lambda g, dt=dt, wb=wb: g.dma_start(
                    out=wb[:, :], in_=dram["w_out_t"][:, dt * 4096:(dt + 1) * 4096]),
                    key=("owl", dt), deps=[("omm", dt - 2, 1)] if dt >= 2 else [], step=16, sem=f"owl{dt % 2}")
                S.op("sync", lambda s, dt=dt: s.dma_start(
                    out=sb[f"xres{dt % 2}"][:, :], in_=dram["xT"][dt * 128:(dt + 1) * 128, :]),
                    key=("xrl", dt), deps=[("oev", dt - 2, 1)] if dt >= 2 else [], step=16, sem=f"xrl{dt % 2}")
                rst_b = sb[f"rstg{dt % 2}"]
                for half in range(2):
                    pso = ps[f"po{half}"]
                    deps0 = [("owl", dt), ("yng", 31)] + ([("oev", dt - 1, half)] if dt >= 1 else [])
                    for k in range(NT):
                        S.op("tensor", lambda t, pso=pso, wb=wb, k=k, half=half: t.matmul(
                            pso[:, :], wb[:, k * 128:(k + 1) * 128],
                            sb["y"][:, k * TOK + half * 512:k * TOK + half * 512 + 512],
                            start=(k == 0), stop=(k == NT - 1)),
                            key=("omm", dt, half) if k == NT - 1 else None,
                            deps=deps0 if k == 0 else [])
                    evdep = [("omm", dt, half), ("xrl", dt)]
                    if dt >= 2:
                        evdep += [("lst", dt - 2, 1, 1), ("rsp", dt - 2)]
                    S.op("vector", lambda v, dt=dt, half=half, pso=pso, rst_b=rst_b: v.tensor_tensor(
                        rst_b[:, half * 512:half * 512 + 512],
                        pso[:, :], sb[f"xres{dt % 2}"][:, half * 512:half * 512 + 512], ALU.add),
                        key=("oev", dt, half), deps=evdep)
                S.op("sync", lambda s, dt=dt, rst_b=rst_b: s.dma_start(
                    out=dram["resid_spill"][dt], in_=rst_b[:, :]),
                    key=("rsp", dt), deps=[("oev", dt, 0), ("oev", dt, 1)], step=16, sem=f"rsp{dt % 2}")
                S.op("scalar", lambda s, dt=dt, rst_b=rst_b: s.activation(
                    sb["sqb"][:, :], rst_b[:, :], ACT.Square),
                    key=("osq", dt), deps=[("oev", dt, 0), ("oev", dt, 1)] +
                    ([("lst", dt - 1, 1, 1)] if dt >= 1 else []))
                for half in range(2):
                    S.op("tensor", lambda t, dt=dt, half=half, rst_b=rst_b: t.matmul(
                        ps[f"pst{half}"][0:1, :], sb["onescb"][:, :],
                        rst_b[:, half * 512:half * 512 + 512],
                        start=(dt == 0), stop=(dt == NT - 1)),
                        key=("lst", dt, half, 0), deps=[("oev", dt, half)] + ([("rsq2", 7, half)] if dt == 0 else []))
                    S.op("tensor", lambda t, dt=dt, half=half: t.matmul(
                        ps[f"pst{2 + half}"][0:1, :], sb["onescb"][:, :],
                        sb["sqb"][:, half * 512:half * 512 + 512],
                        start=(dt == 0), stop=(dt == NT - 1)),
                        key=("lst", dt, half, 1), deps=[("osq", dt)])
            for half in range(2):
                S.op("scalar", lambda s, half=half: s.activation(
                    sb["stu"][:, half * 512:half * 512 + 512], ps[f"pst{half}"][0:1, :], ACT.Copy, scale=1.0 / D),
                    key=("mu", half), deps=[("lst", NT - 1, half, 0)])
                S.op("scalar", lambda s, half=half: s.activation(
                    sb["stq"][:, half * 512:half * 512 + 512], ps[f"pst{2 + half}"][0:1, :],
                    ACT.Copy, scale=1.0 / D),
                    key=("m2", half), deps=[("lst", NT - 1, half, 1)])
            S.op("scalar", lambda s: s.activation(sb["stt"][:, :], sb["stu"][:, :], ACT.Square),
                 key=("musq",), deps=[("mu", 0), ("mu", 1)])
            S.op("vector", lambda v: v.tensor_tensor(sb["stq"][:, :], sb["stq"][:, :], sb["stt"][:, :], ALU.subtract),
                 key=("var",), deps=[("musq",), ("m2", 0), ("m2", 1)])
            S.op("scalar", lambda s: s.activation(sb["stq"][:, :], sb["stq"][:, :], ACT.Sqrt, bias=EPS),
                 key=("rr0",), deps=[("var",)])
            S.op("vector", lambda v: v.reciprocal(sb["stq"][:, :], sb["stq"][:, :]),
                 key=("rr",), deps=[("rr0",)])
            S.op("vector", lambda v: v.tensor_tensor(sb["stu"][:, :], sb["stu"][:, :], sb["stq"][:, :], ALU.mult),
                 key=("m2r",), deps=[("rr",)])
            for half in range(2):
                S.op("tensor", lambda t, half=half: t.matmul(
                    ps[f"po{half}"][:, :], sb["onesrf"][0:1, :],
                    sb["stq"][0:1, half * 512:half * 512 + 512], start=True, stop=True),
                    key=("rrep", half), deps=[("rr",), ("m2r",), ("oev", NT - 1, half)])
                S.op("tensor", lambda t, half=half: t.matmul(
                    ps[f"prr{half}"][:, :], sb["onesrf"][0:1, :],
                    sb["stu"][0:1, half * 512:half * 512 + 512], start=True, stop=True),
                    key=("mrep", half), deps=[("m2r",), ("yng", 31)])
            for dt in range(NT):
                rst_b = sb[f"rstg{dt % 2}"]
                xst_b = sb[f"xstg{dt % 2}"]
                rldep = [("rsp", NT - 2 + (dt % 2)), ("lst", NT - 1, 1, 1)]
                if dt >= 2:
                    rldep.append(("xl0", dt - 2, 1))
                S.op("gpsimd", lambda g, dt=dt, rst_b=rst_b: g.dma_start(
                    out=rst_b[:, :], in_=dram["resid_spill"][dt]),
                    key=("rld", dt), deps=rldep, step=16, sem=f"rld{dt % 2}")
                for half in range(2):
                    S.op("vector", lambda v, half=half, rst_b=rst_b: v.tensor_tensor(
                        sb["tmpf"][:, :], rst_b[:, half * 512:half * 512 + 512],
                        ps[f"po{half}"][:, :], ALU.mult),
                        key=("xl0", dt, half), deps=[("rld", dt), ("rrep", half), ("xl1", dt, 0) if half == 1 else (("xl1", dt - 1, 1) if dt >= 1 else None)])
                    xldep = [("xl0", dt, half), ("mrep", half)]
                    if dt >= 2:
                        xldep += [("agxs", dt - 2, 3), ("xlsp", dt - 2)]
                    S.op("vector", lambda v, half=half, xst_b=xst_b: v.tensor_tensor(
                        xst_b[:, half * 512:half * 512 + 512], sb["tmpf"][:, :],
                        ps[f"prr{half}"][:, :], ALU.subtract),
                        key=("xl1", dt, half), deps=xldep)
                for j in range(4):
                    S.op("sync", lambda s, dt=dt, j=j, xst_b=xst_b: s.dma_start(
                        out=dram[f"agx_in{j}"][:, dt * 256:(dt + 1) * 256],
                        in_=xst_b[:, j * 256:j * 256 + 256]),
                        key=("agxs", dt, j), deps=[("xl1", dt, 0), ("xl1", dt, 1)], step=16, sem=f"agxs{dt % 2}")
                S.op("scalar", lambda s, dt=dt, xst_b=xst_b: s.dma_start(
                    out=dram["xl1_spill"][:, dt * TOK:(dt + 1) * TOK], in_=xst_b[:, :]),
                    key=("xlsp", dt), deps=[("xl1", dt, 0), ("xl1", dt, 1)], step=16, sem=f"xlsp{dt % 2}")
            for j in range(4):
                S.op("gpsimd", lambda g, j=j: g.collective_compute(
                    "AllGather", ALU.bypass, replica_groups=[list(range(8))],
                    ins=[dram[f"agx_in{j}"].opt()], outs=[dram[f"agx_out{j}"].opt()]),
                    key=("agx", j), deps=[("agxs", NT - 2, 3), ("agxs", NT - 1, 3)], async_=True, sem=f"ccagx{j}")

        if UPTO >= 6:
            S.run_phase("p4", p4)
        st4.close()
        stY.close()
        stA.close()

        st5 = ExitStack()
        sbuf("W", (128, 32 * NTC * 128), BF16, st5)
        sbuf("hb0", (128, NTC * 512), BF16, st5)
        sbuf("hb1", (128, NTC * 512), BF16, st5)
        sbuf("qU", (128, 512), I32, st5)
        sbuf("qF", (128, 512), F32, st5)
        sbuf("zq_g", (32, 176), I32, st5)
        sbuf("zst", (32, NTC * 128), F32, st5)
        sbuf("zs_g", (32, NTC * 128), F32, st5)
        sbuf("sc_g", (32, NTC * 128), F32, st5)
        sbuf("onesrb", (1, 128), BF16, st5)
        sbuf("qwk0", (128, 176), I32, st5)
        sb["qwk1"] = sb["qwk0"]
        psum("pf0", (128, 512), F32, st5)
        psum("pf1", (128, 512), F32, st5)
        psum("pzz0", (128, 512), F32, st5)
        psum("pzz1", (128, 512), F32, st5)
        st5x = ExitStack()
        for i in range(2):
            sbuf(f"Xb{i}", (128, NT * 512), BF16, st5x)
            sbuf(f"gstg{i}", (128, 512), BF16, st5x)
        sbuf("srowb0", (1, NTC * 128), BF16, st5x)
        sbuf("zrowb0", (1, NTC * 128), BF16, st5x)
        sb["srowb1"] = sb["srowb0"]
        sb["zrowb1"] = sb["zrowb0"]

        FBLK = [(0, 512), (512, 512), (1024, 384)]

        def prev_blk(k, bi, nb):
            idx = k * nb + bi
            return None if idx == 0 else ((idx - 1) // nb, (idx - 1) % nb)

        def dequant_gu(tag, qname, sname, zname):
            xdep = [("g", "zs")] if tag == "u" else []
            S.op("gpsimd", lambda g: g.dma_start(out=sb["zq_g"][:, :], in_=dram[zname]),
                 key=(tag, "zld"), deps=xdep, step=16, sem="fzio")
            S.op("gpsimd", lambda g: g.dma_start(out=sb["sc_g"][:, :], in_=dram[sname]),
                 key=(tag, "sld"), deps=xdep, step=16, sem="fzio")
            S.op("gpsimd", lambda g: g.memset(sb["onesrb"][:, :], 1.0), key=(tag, "ob"))
            for i in range(8):
                S.op("vector", lambda v, i=i: v.tensor_scalar(
                    sb["zst"][:, :].rearrange("p (a b) -> p b a", b=8)[:, i, :].bitcast(I32),
                    sb["zq_g"][:, :], 4 * i, 15, ALU.logical_shift_right, ALU.bitwise_and),
                    key=(tag, "zu", i), deps=[(tag, "zld"), (tag, "sld")])
            S.op("vector", lambda v: v.tensor_copy(sb["zs_g"][:, :], sb["zst"][:, :].bitcast(I32)),
                 key=(tag, "zcv"), deps=[(tag, "zu", i) for i in range(8)])
            S.op("vector", lambda v: v.tensor_tensor(sb["zs_g"][:, :], sb["zs_g"][:, :], sb["sc_g"][:, :], ALU.mult),
                 key=(tag, "zs"), deps=[(tag, "zcv"), (tag, "sld")])
            S.op("sync", lambda s: s.dma_start(out=dram["zs_gu_d"], in_=sb["zs_g"][:, :]),
                 key=(tag, "zspl"), deps=[(tag, "zs")] + ([("g", "zrl", 31)] if tag == "u" else []),
                 step=16, sem="zspl")
            for k in range(32):
                qb = sb[f"qwk{k % 2}"]
                qdep = [(tag, "u", k - 1, 2, 7)] if k >= 1 else (
                    [("g", "u", 31, 2, 7), ("g", "wB", (31, 2))] if tag == "u" else [])
                S.op("gpsimd", lambda g, qb=qb, k=k: g.dma_start(
                    out=qb[:, :], in_=dram[qname][:, k * 176:(k + 1) * 176]),
                    key=(tag, "qld", k), deps=qdep, step=16, sem="qld")
                S.op("gpsimd", lambda g, k=k: g.dma_start(
                    out=sb[f"srowb{k % 2}"][:, :], in_=dram[sname][k:k + 1, :]),
                    key=(tag, "srl", k), deps=([(tag, "sr", k - 1, 2)] if k >= 1 else
                                               ([("g", "sr", 31, 2)] if tag == "u" else [])),
                    step=16, sem="srl")
                S.op("gpsimd", lambda g, k=k: g.dma_start(
                    out=sb[f"zrowb{k % 2}"][:, :], in_=dram["zs_gu_d"][k:k + 1, :]),
                    key=(tag, "zrl", k), deps=[(tag, "zspl")] + ([(tag, "zr", k - 1, 2)] if k >= 1 else
                                               ([("g", "zr", 31, 2)] if tag == "u" else [])),
                    step=16, sem="zrl")
                for bi, (off, width) in enumerate(FBLK):
                    woff, wwidth = off // 8, width // 8
                    pb = prev_blk(k, bi, 3)
                    for i in range(8):
                        S.op("vector", lambda v, qb=qb, woff=woff, wwidth=wwidth, width=width, i=i: v.tensor_scalar(
                            sb["qU"][:, 0:width].rearrange("p (a b) -> p b a", b=8)[:, i, :],
                            qb[:, woff:woff + wwidth], 4 * i, 15,
                            ALU.logical_shift_right, ALU.bitwise_and),
                            key=(tag, "u", k, bi, i), deps=[(tag, "qld", k)] +
                            ([(tag, "wA", pb)] if pb is not None and i == 0 else []))
                    S.op("vector", lambda v, width=width: v.tensor_copy(sb["qF"][:, 0:width], sb["qU"][:, 0:width]),
                         key=(tag, "cv", k, bi), deps=[(tag, "u", k, bi, i) for i in range(8)] +
                         ([(tag, "wB", pb)] if pb is not None else []))
                    S.op("tensor", lambda t, k=k, off=off, width=width: t.matmul(
                        ps["pzz0"][:, 0:width], sb["onesrb"][0:1, :],
                        sb[f"srowb{k % 2}"][0:1, off:off + width],
                        start=True, stop=True),
                        key=(tag, "sr", k, bi), deps=[(tag, "srl", k), (tag, "ob")] +
                        ([(tag, "wA", pb)] if pb is not None else
                         ([("g", "mm", 15, NTC - 1)] if tag == "u" else [])))
                    S.op("tensor", lambda t, k=k, off=off, width=width: t.matmul(
                        ps["pzz1"][:, 0:width], sb["onesrb"][0:1, :],
                        sb[f"zrowb{k % 2}"][0:1, off:off + width],
                        start=True, stop=True),
                        key=(tag, "zr", k, bi), deps=[(tag, "zrl", k)] +
                        ([(tag, "wB", pb)] if pb is not None else []))
                    S.op("vector", lambda v, width=width: v.tensor_tensor(
                        sb["qF"][:, 0:width], sb["qF"][:, 0:width], ps["pzz0"][:, 0:width], ALU.mult),
                        key=(tag, "wA", (k, bi)), deps=[(tag, "cv", k, bi), (tag, "sr", k, bi)])
                    S.op("vector", lambda v, k=k, off=off, width=width: v.tensor_tensor(
                        sb["W"][:, k * NTC * 128 + off:k * NTC * 128 + off + width],
                        sb["qF"][:, 0:width], ps["pzz1"][:, 0:width], ALU.subtract),
                        key=(tag, "wB", (k, bi)), deps=[(tag, "wA", (k, bi)), (tag, "zr", k, bi)])

        def x_load(tag, tb, buf):
            r = tb // 2
            for jq in range(2):
                j = (tb % 2) * 2 + jq
                dep = [("agx", j)]
                if tb >= 2:
                    dep.append((tag, "mm", tb - 2, NTC - 1))
                elif tag == "u":
                    dep.append(("g", "mm", 14 + tb, NTC - 1))
                S.op("sync", lambda s, buf=buf, r=r, j=j, jq=jq: s.dma_start(
                    out=buf[:, :].rearrange("p (k q t) -> p k q t", k=NT, q=2)[:, :, jq, :],
                    in_=dram[f"agx_out{j}"][r * 128:(r + 1) * 128, :].rearrange("p (k t) -> p k t", k=NT)),
                    key=(tag, "xl", tb, jq), deps=dep, step=16, sem=f"xb{tb % 2}")

        def p5a():
            dequant_gu("g", "gq", "gsc", "gzq")
            for tb in range(16):
                buf = sb[f"Xb{tb % 2}"]
                x_load("g", tb, buf)
                for nt in range(NTC):
                    gg = tb * NTC + nt
                    pf = ps[f"pf{gg % 2}"]
                    deps0 = [("g", "xl", tb, 0), ("g", "xl", tb, 1), ("g", "wB", (31, 2))]
                    if gg >= 2:
                        deps0.append(("g", "sil", (gg - 2) // NTC, (gg - 2) % NTC))
                    for k in range(NT):
                        S.op("tensor", lambda t, pf=pf, k=k, nt=nt, buf=buf: t.matmul(
                            pf[:, :], sb["W"][:, k * NTC * 128 + nt * 128:k * NTC * 128 + (nt + 1) * 128],
                            buf[:, k * 512:(k + 1) * 512], start=(k == 0), stop=(k == NT - 1)),
                            key=("g", "mm", tb, nt) if k == NT - 1 else None,
                            deps=deps0 if k == 0 else [])
                    sdep = [("g", "mm", tb, nt)]
                    if gg >= 2:
                        sdep.append(("g", "sp", (gg - 2) // NTC, (gg - 2) % NTC))
                    S.op("scalar", lambda s, nt=nt, pf=pf, gg=gg: s.activation(
                        sb[f"gstg{gg % 2}"][:, :], pf[:, :], ACT.Silu),
                        key=("g", "sil", tb, nt), deps=sdep)
                    S.op("scalar", lambda s, tb=tb, nt=nt, gg=gg: s.dma_start(
                        out=dram["g_spill"][tb][:, nt * 512:(nt + 1) * 512], in_=sb[f"gstg{gg % 2}"][:, :]),
                        key=("g", "sp", tb, nt), deps=[("g", "sil", tb, nt)], step=16, sem=f"gsl{gg % 2}")

        def p5b():
            dequant_gu("u", "uq", "usc", "uzq")
            for tb in range(16):
                buf = sb[f"Xb{tb % 2}"]
                hbuf = sb[f"hb{tb % 2}"]
                x_load("u", tb, buf)
                S.op("gpsimd", lambda g, tb=tb, hbuf=hbuf: g.dma_start(out=hbuf[:, :], in_=dram["g_spill"][tb]),
                     key=("u", "gl", tb), deps=[("g", "sp", tb, NTC - 1)] + ([("u", "hsp", tb - 2)] if tb >= 2 else []),
                     step=16, sem=f"hgl{tb % 2}")
                for nt in range(NTC):
                    gg = tb * NTC + nt
                    pf = ps[f"pf{gg % 2}"]
                    deps0 = [("u", "xl", tb, 0), ("u", "xl", tb, 1), ("u", "wB", (31, 2))]
                    if gg >= 2:
                        deps0.append(("u", "hh", (gg - 2) // NTC, (gg - 2) % NTC))
                    for k in range(NT):
                        S.op("tensor", lambda t, pf=pf, k=k, nt=nt, buf=buf: t.matmul(
                            pf[:, :], sb["W"][:, k * NTC * 128 + nt * 128:k * NTC * 128 + (nt + 1) * 128],
                            buf[:, k * 512:(k + 1) * 512], start=(k == 0), stop=(k == NT - 1)),
                            key=("u", "mm", tb, nt) if k == NT - 1 else None,
                            deps=deps0 if k == 0 else [])
                    S.op("vector", lambda v, nt=nt, pf=pf, hbuf=hbuf: v.tensor_tensor(
                        hbuf[:, nt * 512:(nt + 1) * 512], hbuf[:, nt * 512:(nt + 1) * 512],
                        pf[:, :], ALU.mult),
                        key=("u", "hh", tb, nt), deps=[("u", "mm", tb, nt), ("u", "gl", tb)])
                S.op("scalar", lambda s, tb=tb, hbuf=hbuf: s.dma_start(out=dram["h_spill"][tb], in_=hbuf[:, :]),
                     key=("u", "hsp", tb), deps=[("u", "hh", tb, nt) for nt in range(NTC)], step=16, sem=f"hsp{tb % 2}")

        def p5c():
            S.op("gpsimd", lambda g: g.dma_start(out=sb["zq_d"][:, :], in_=dram["dzq"]),
                 key=("d", "zld"), step=16, sem="fzio")
            S.op("gpsimd", lambda g: g.dma_start(out=sb["sc_d"][:, :], in_=dram["dsc"]),
                 key=("d", "sld"), step=16, sem="fzio")
            for i in range(8):
                S.op("vector", lambda v, i=i: v.tensor_scalar(
                    sb["zstd"][:, :].rearrange("p (a b) -> p b a", b=8)[:, i, :].bitcast(I32),
                    sb["zq_d"][:, :], 4 * i, 15, ALU.logical_shift_right, ALU.bitwise_and),
                    key=("d", "zu", i), deps=[("d", "zld"), ("d", "sld")])
            S.op("vector", lambda v: v.tensor_copy(sb["zs_d"][:, :], sb["zstd"][:, :].bitcast(I32)),
                 key=("d", "zcv"), deps=[("d", "zu", i) for i in range(8)])
            S.op("vector", lambda v: v.tensor_tensor(sb["zs_d"][:, :], sb["zs_d"][:, :], sb["sc_d"][:, :], ALU.mult),
                 key=("d", "zs"), deps=[("d", "zcv"), ("d", "sld")])
            S.op("sync", lambda s: s.dma_start(out=dram["zs_d_d"], in_=sb["zs_d"][:, :]),
                 key=("d", "zspl"), deps=[("d", "zs")], step=16, sem="zspl")
            for k in range(NTC):
                qb = sb[f"qdk{k % 2}"]
                S.op("gpsimd", lambda g, qb=qb, k=k: g.dma_start(
                    out=qb[:, :], in_=dram["dq"][:, k * 512:(k + 1) * 512]),
                    key=("d", "qld", k), deps=[("d", "u", k - 2, 7, 7)] if k >= 2 else [], step=16, sem=f"qdk{k % 2}")
                S.op("gpsimd", lambda g, k=k: g.dma_start(
                    out=sb[f"srowd{k % 2}"][:, :], in_=dram["dsc"][k:k + 1, :]),
                    key=("d", "srl", k), deps=([("d", "sr", k - 1, 7)] if k >= 1 else []), step=16, sem="srl")
                S.op("gpsimd", lambda g, k=k: g.dma_start(
                    out=sb[f"zrowd{k % 2}"][:, :], in_=dram["zs_d_d"][k:k + 1, :]),
                    key=("d", "zrl", k), deps=[("d", "zspl")] + ([("d", "zr", k - 1, 7)] if k >= 1 else []), step=16, sem="zrl")
                for bi in range(8):
                    off = bi * 512
                    pb = prev_blk(k, bi, 8)
                    for i in range(8):
                        S.op("vector", lambda v, qb=qb, off=off, i=i: v.tensor_scalar(
                            sb["qU"][:, 0:512].rearrange("p (a b) -> p b a", b=8)[:, i, :],
                            qb[:, off // 8:off // 8 + 64], 4 * i, 15,
                            ALU.logical_shift_right, ALU.bitwise_and),
                            key=("d", "u", k, bi, i), deps=[("d", "qld", k)] +
                            ([("d", "wA", pb)] if pb is not None and i == 0 else []))
                    S.op("vector", lambda v: v.tensor_copy(sb["qF"][:, 0:512], sb["qU"][:, 0:512]),
                         key=("d", "cv", k, bi), deps=[("d", "u", k, bi, i) for i in range(8)] +
                         ([("d", "wB", pb)] if pb is not None else []))
                    S.op("tensor", lambda t, k=k, off=off: t.matmul(
                        ps["pzz0"][:, :], sb["onesrb"][0:1, :],
                        sb[f"srowd{k % 2}"][0:1, off:off + 512],
                        start=True, stop=True),
                        key=("d", "sr", k, bi), deps=[("d", "srl", k)] + ([("d", "wA", pb)] if pb is not None else []))
                    S.op("tensor", lambda t, k=k, off=off: t.matmul(
                        ps["pzz1"][:, :], sb["onesrb"][0:1, :],
                        sb[f"zrowd{k % 2}"][0:1, off:off + 512],
                        start=True, stop=True),
                        key=("d", "zr", k, bi), deps=[("d", "zrl", k)] + ([("d", "wB", pb)] if pb is not None else []))
                    S.op("vector", lambda v: v.tensor_tensor(sb["qF"][:, 0:512], sb["qF"][:, 0:512],
                                                             ps["pzz0"][:, :], ALU.mult),
                         key=("d", "wA", (k, bi)), deps=[("d", "cv", k, bi), ("d", "sr", k, bi)])
                    S.op("vector", lambda v, k=k, off=off: v.tensor_tensor(
                        sb["W"][:, k * 4096 + off:k * 4096 + off + 512], sb["qF"][:, 0:512],
                        ps["pzz1"][:, :], ALU.subtract),
                        key=("d", "wB", (k, bi)), deps=[("d", "wA", (k, bi)), ("d", "zr", k, bi)])
            for dh in range(2):
                for tb in range(16):
                    hbuf = sb[f"hb{tb % 2}"]
                    hldep = [("u", "hsp", tb)]
                    if tb >= 2:
                        hldep.append(("d", "mm", dh, tb - 2, 15))
                    elif dh >= 1:
                        hldep.append(("d", "mm", dh - 1, 14 + tb, 15))
                    S.op("gpsimd", lambda g, tb=tb, hbuf=hbuf, dh=dh: g.dma_start(
                        out=hbuf[:, :], in_=dram["h_spill"][tb]),
                        key=("d", "hl", dh, tb), deps=hldep, step=16, sem=f"hl{tb % 2}")
                    for dtl in range(16):
                        dto = dh * 16 + dtl
                        pf = ps[f"pf{dtl % 2}"]
                        deps0 = [("d", "hl", dh, tb), ("d", "wB", (NTC - 1, 7))]
                        pv = (dh, tb, dtl - 2) if dtl >= 2 else ((dh, tb - 1, 14 + dtl) if tb >= 1 else ((dh - 1, 15, 14 + dtl) if dh >= 1 else None))
                        if pv is not None:
                            deps0.append(("d", "pe", pv[0], pv[1], pv[2]))
                        for k in range(NTC):
                            S.op("tensor", lambda t, pf=pf, k=k, dto=dto, hbuf=hbuf: t.matmul(
                                pf[:, :], sb["W"][:, k * 4096 + dto * 128:k * 4096 + (dto + 1) * 128],
                                hbuf[:, k * 512:(k + 1) * 512], start=(k == 0), stop=(k == NTC - 1)),
                                key=("d", "mm", dh, tb, dtl) if k == NTC - 1 else None,
                                deps=deps0 if k == 0 else [])
                        pedep = [("d", "mm", dh, tb, dtl)]
                        pv1 = (dh, tb, dtl - 1) if dtl >= 1 else ((dh, tb - 1, 15) if tb >= 1 else ((dh - 1, 15, 15) if dh >= 1 else None))
                        if pv1 is not None:
                            pedep.append(("d", "pev", pv1[0], pv1[1], pv1[2]))
                        S.op("vector", lambda v, pf=pf: v.tensor_copy(
                            sb["pt"][:, :], pf[:, :]),
                            key=("d", "pe", dh, tb, dtl), deps=pedep)
                        S.op("sync", lambda s, dh=dh, tb=tb, dtl=dtl: s.dma_start(
                            out=dram[f"rs_in{dh}"][(tb // 2) * 2048 + dtl * 128:(tb // 2) * 2048 + (dtl + 1) * 128,
                                                   (tb % 2) * 512:(tb % 2) * 512 + 512],
                            in_=sb["pt"][:, :]),
                            key=("d", "pev", dh, tb, dtl), deps=[("d", "pe", dh, tb, dtl)], step=16, sem="pev")
                S.op("gpsimd", lambda g, dh=dh: g.collective_compute(
                    "ReduceScatter", ALU.add, replica_groups=[list(range(8))],
                    ins=[dram[f"rs_in{dh}"].opt()], outs=[dram[f"rs_out{dh}"].opt()]),
                    key=("rs", dh),
                    deps=[("d", "pev", dh, tb, dtl) for tb in range(16) for dtl in range(16)], async_=True, sem=f"ccrs{dh}")

        if UPTO >= 7:
            S.run_phase("p5a", p5a)
        if UPTO >= 8:
            S.run_phase("p5b", p5b)
        st5x.close()
        st5c = ExitStack()
        sbuf("qdk0", (128, 512), I32, st5c)
        sbuf("qdk1", (128, 512), I32, st5c)
        sbuf("srowd0", (1, D), BF16, st5c)
        sbuf("zrowd0", (1, D), BF16, st5c)
        sb["srowd1"] = sb["srowd0"]
        sb["zrowd1"] = sb["zrowd0"]
        sbuf("sc_d", (NTC, D), F32, st5c)
        sbuf("zs_d", (NTC, D), F32, st5c)
        sbuf("zstd", (NTC, D), F32, st5c)
        sbuf("zq_d", (NTC, 512), I32, st5c)
        sbuf("pt", (128, 512), BF16, st5c)
        if UPTO >= 9:
            S.run_phase("p5c", p5c)
        st5c.close()
        st5.close()

        st6 = ExitStack()
        sbuf("r2", (128, NT * TOK), BF16, st6)
        sbuf("sq6", (128, TOK), BF16, st6)
        sbuf("stu6", (1, TOK), F32, st6)
        sbuf("stq6", (1, TOK), F32, st6)
        sbuf("stt6", (1, TOK), F32, st6)
        sbuf("tf6", (128, 512), F32, st6)
        for i in range(2):
            sbuf(f"rsl{i}", (128, TOK), BF16, st6)
            sbuf(f"xo{i}", (128, TOK), BF16, st6)
            sbuf(f"of{i}", (128, TOK), F32, st6)
        psum("p60", (128, 512), F32, st6)
        psum("p61", (128, 512), F32, st6)
        psum("p62", (128, 512), F32, st6)
        psum("p63", (128, 512), F32, st6)
        psum("p6r0", (128, 512), F32, st6)
        psum("p6r1", (128, 512), F32, st6)
        psum("p6m0", (128, 512), F32, st6)
        psum("p6m1", (128, 512), F32, st6)

        def p6():
            for k in range(NT):
                dh = k // 16
                row = (k % 16) * 128
                S.op("gpsimd", lambda g, k=k, dh=dh, row=row: g.dma_start(
                    out=sb[f"rsl{k % 2}"][:, :], in_=dram[f"rs_out{dh}"][row:row + 128, :]),
                    key=("rsl", k), deps=[("rs", dh)] + ([("r2", k - 2)] if k >= 2 else []), step=16, sem=f"rsl{k % 2}")
                S.op("sync", lambda s, k=k: s.dma_start(
                    out=sb[f"xo{k % 2}"][:, :], in_=dram["xl1_spill"][:, k * TOK:(k + 1) * TOK]),
                    key=("xol", k), deps=[("xlsp", k)] + ([("r2", k - 2)] if k >= 2 else []), step=16, sem=f"xol{k % 2}")
                S.op("vector", lambda v, k=k: v.tensor_tensor(
                    sb["r2"][:, k * TOK:(k + 1) * TOK], sb[f"rsl{k % 2}"][:, :], sb[f"xo{k % 2}"][:, :], ALU.add),
                    key=("r2", k), deps=[("rsl", k), ("xol", k)])
                S.op("scalar", lambda s, k=k: s.activation(sb["sq6"][:, :], sb["r2"][:, k * TOK:(k + 1) * TOK], ACT.Square),
                     key=("sq6", k), deps=[("r2", k)] + ([("st6", k - 1, 1, 1)] if k >= 1 else []))
                for half in range(2):
                    S.op("tensor", lambda t, k=k, half=half: t.matmul(
                        ps[f"p6{half}"][0:1, :], sb["onescb"][:, :],
                        sb["r2"][:, k * TOK + half * 512:k * TOK + half * 512 + 512],
                        start=(k == 0), stop=(k == NT - 1)),
                        key=("st6", k, half, 0), deps=[("r2", k)])
                    S.op("tensor", lambda t, k=k, half=half: t.matmul(
                        ps[f"p6{2 + half}"][0:1, :], sb["onescb"][:, :],
                        sb["sq6"][:, half * 512:half * 512 + 512],
                        start=(k == 0), stop=(k == NT - 1)),
                        key=("st6", k, half, 1), deps=[("sq6", k)])
            for half in range(2):
                S.op("scalar", lambda s, half=half: s.activation(
                    sb["stu6"][:, half * 512:half * 512 + 512], ps[f"p6{half}"][0:1, :], ACT.Copy, scale=1.0 / D),
                    key=("mu6", half), deps=[("st6", NT - 1, half, 0)])
                S.op("scalar", lambda s, half=half: s.activation(
                    sb["stq6"][:, half * 512:half * 512 + 512], ps[f"p6{2 + half}"][0:1, :],
                    ACT.Copy, scale=1.0 / D),
                    key=("m26", half), deps=[("st6", NT - 1, half, 1)])
            S.op("scalar", lambda s: s.activation(sb["stt6"][:, :], sb["stu6"][:, :], ACT.Square),
                 key=("musq6",), deps=[("mu6", 0), ("mu6", 1)])
            S.op("vector", lambda v: v.tensor_tensor(sb["stq6"][:, :], sb["stq6"][:, :], sb["stt6"][:, :], ALU.subtract),
                 key=("var6",), deps=[("musq6",), ("m26", 0), ("m26", 1)])
            S.op("scalar", lambda s: s.activation(sb["stq6"][:, :], sb["stq6"][:, :], ACT.Sqrt, bias=EPS),
                 key=("rr60",), deps=[("var6",)])
            S.op("vector", lambda v: v.reciprocal(sb["stq6"][:, :], sb["stq6"][:, :]),
                 key=("rr6",), deps=[("rr60",)])
            S.op("vector", lambda v: v.tensor_tensor(sb["stu6"][:, :], sb["stu6"][:, :], sb["stq6"][:, :], ALU.mult),
                 key=("m2r6",), deps=[("rr6",)])
            for half in range(2):
                S.op("tensor", lambda t, half=half: t.matmul(
                    ps[f"p6r{half}"][:, :], sb["onesrf"][0:1, :],
                    sb["stq6"][0:1, half * 512:half * 512 + 512], start=True, stop=True),
                    key=("rrep6", half), deps=[("rr6",), ("m2r6",)])
                S.op("tensor", lambda t, half=half: t.matmul(
                    ps[f"p6m{half}"][:, :], sb["onesrf"][0:1, :],
                    sb["stu6"][0:1, half * 512:half * 512 + 512], start=True, stop=True),
                    key=("mrep6", half), deps=[("m2r6",)])
            for k in range(NT):
                for half in range(2):
                    S.op("vector", lambda v, k=k, half=half: v.tensor_tensor(
                        sb["tf6"][:, :], sb["r2"][:, k * TOK + half * 512:k * TOK + half * 512 + 512],
                        ps[f"p6r{half}"][:, :], ALU.mult),
                        key=("o0", k, half), deps=[("rrep6", half), ("o1", k, 0) if half == 1 else (("o1", k - 1, 1) if k >= 1 else None)])
                    S.op("vector", lambda v, k=k, half=half: v.tensor_tensor(
                        sb[f"of{k % 2}"][:, half * 512:half * 512 + 512], sb["tf6"][:, :],
                        ps[f"p6m{half}"][:, :], ALU.subtract),
                        key=("o1", k, half), deps=[("o0", k, half), ("mrep6", half)] +
                        ([("ow", k - 2)] if k >= 2 else []))
                S.op("sync", lambda s, k=k: s.dma_start(out=dram["y_out"][k * 128:(k + 1) * 128, :],
                                                        in_=sb[f"of{k % 2}"][:, :]),
                     key=("ow", k), deps=[("o1", k, 0), ("o1", k, 1)], step=16, sem=f"ow{k % 2}")
            S.op("sync", lambda s: None, key=None, deps=[("ow", NT - 2), ("ow", NT - 1)])

        if UPTO >= 10:
            S.run_phase("p6", p6)
        st6.close()
        if UPTO < 10:
            def ptail():
                for k in range(NT):
                    if UPTO in (4, 5) and "y" in sb:
                        S.op("gpsimd", lambda g, k=k: g.dma_start(
                            out=dram["y_out"][k * 128:(k + 1) * 128, :], in_=sb["y"][:, k * TOK:(k + 1) * TOK]),
                            key=("dow", k), step=16, sem="dow")
                    elif UPTO == 6:
                        S.op("gpsimd", lambda g, k=k: g.dma_start(
                            out=dram["y_out"][k * 128:(k + 1) * 128, :],
                            in_=dram["xl1_spill"][:, k * TOK:(k + 1) * TOK]),
                            key=("dow", k), step=16, sem="dow")
                    elif UPTO in (7, 8):
                        src = dram["g_spill" if UPTO == 7 else "h_spill"]
                        S.op("gpsimd", lambda g, k=k, src=src: g.dma_start(
                            out=dram["y_out"][k * 128:(k + 1) * 128, :],
                            in_=src[k % 16][:, (k // 16) * 1024:(k // 16) * 1024 + 1024]),
                            key=("dow", k), step=16, sem="dow")
                    elif UPTO == 9:
                        S.op("gpsimd", lambda g, k=k: g.dma_start(
                            out=dram["y_out"][k * 128:(k + 1) * 128, :],
                            in_=dram[f"rs_out{k // 16}"][(k % 16) * 128:(k % 16) * 128 + 128, :]),
                            key=("dow", k), step=16, sem="dow")
                    else:
                        S.op("sync", lambda s, k=k: s.dma_start(
                            out=dram["y_out"][k * 128:(k + 1) * 128, :], in_=dram["xT"][k * 128:(k + 1) * 128, :]),
                            key=("dow", k), step=16, sem="dow")
                eng = "sync" if UPTO not in (4, 5, 6, 7, 8, 9) else "gpsimd"
                S.op(eng, lambda s: None, key=None, deps=[("dow", k) for k in range(NT)])
            S.run_phase("ptail", ptail)
    return nc



def _tile_lhsT(W):
    O, Din = W.shape
    kt = Din // 128
    blocks = []
    for o0 in range(0, O, 128):
        ow = min(128, O - o0)
        blk = W[o0:o0 + ow, :].reshape(ow, kt, 128).transpose(2, 1, 0).reshape(128, kt * ow)
        blocks.append(blk)
    return np.ascontiguousarray(np.concatenate(blocks, axis=1))


_NC_CACHE = {}


def kernel(**inp):
    if ("nc", UPTO, P2D_CHUNKS, P2D_HEADS, P2D_PRE, P2D_OPS) not in _NC_CACHE:
        _NC_CACHE[("nc", UPTO, P2D_CHUNKS, P2D_HEADS, P2D_PRE, P2D_OPS)] = build_nc()
    nc = _NC_CACHE[("nc", UPTO, P2D_CHUNKS, P2D_HEADS, P2D_PRE, P2D_OPS)]

    x = np.asarray(inp["L0_attn_hidden_states"], dtype=np.float32)
    w_in = np.asarray(inp["L0_attn_in_proj_w"], dtype=np.float32)
    w_out = np.asarray(inp["L0_attn_out_proj_w"], dtype=np.float32)
    conv_w = np.asarray(inp["L0_attn_conv_weight"], dtype=np.float32)
    conv_b = np.asarray(inp["L0_attn_conv_bias"], dtype=np.float32)
    dt_bias = np.asarray(inp["L0_attn_dt_bias"], dtype=np.float32)
    a_log = np.asarray(inp["L0_attn_A_log"], dtype=np.float32)
    Dv = np.asarray(inp["L0_attn_D"], dtype=np.float32)
    nw = np.asarray(inp["L0_attn_norm_weight"], dtype=np.float32)

    w_in_t = _tile_lhsT(w_in)
    w_out_t = _tile_lhsT(w_out)
    conv_w_t = np.ascontiguousarray(conv_w.reshape(48, 128, 4).transpose(1, 0, 2).reshape(128, 192))
    conv_b_t = np.ascontiguousarray(conv_b.reshape(48, 128).T)
    d_exp = np.ascontiguousarray(np.repeat(Dv, PH).reshape(32, 128).T)
    nw_t = np.ascontiguousarray(nw.reshape(32, 128).T)

    def pad_cols(a, w):
        out = np.zeros((a.shape[0], w), dtype=a.dtype)
        out[:, :a.shape[1]] = a
        return out
    gq_full = pad_cols(np.asarray(inp["L0_ffn_gate_qweight"]), FIP // 8)
    uq_full = pad_cols(np.asarray(inp["L0_ffn_up_qweight"]), FIP // 8)
    gs_full = pad_cols(np.asarray(inp["L0_ffn_gate_scales"], dtype=np.float32), FIP)
    us_full = pad_cols(np.asarray(inp["L0_ffn_up_scales"], dtype=np.float32), FIP)
    gz_full = pad_cols(np.asarray(inp["L0_ffn_gate_qzeros"]), FIP // 8)
    uz_full = pad_cols(np.asarray(inp["L0_ffn_up_qzeros"]), FIP // 8)
    dq_in = np.asarray(inp["L0_ffn_down_qweight"])
    dq_full = np.zeros((FIP, 512), dtype=dq_in.dtype)
    dq_full[:FI] = dq_in
    ds_in = np.asarray(inp["L0_ffn_down_scales"], dtype=np.float32)
    ds_full = np.zeros((88, 4096), dtype=np.float32)
    ds_full[:86] = ds_in
    dz_in = np.asarray(inp["L0_ffn_down_qzeros"])
    dz_full = np.zeros((88, 512), dtype=dz_in.dtype)
    dz_full[:86] = dz_in

    in_maps = []
    for c in range(8):
        b = c // 4
        t0 = (c % 4) * TOK
        xT = np.ascontiguousarray(x[b, t0:t0 + TOK, :].T)
        m_prev = np.full((128, 1), 0.0 if c % 4 == 0 else 1.0, np.float32)
        prevsel = np.zeros((128, 8), np.float32)
        if c % 4 != 0:
            prevsel[:, c - 1] = 1.0
        msel = np.zeros((64, 8), np.float32)
        macc = np.zeros((128, 8), np.float32)
        for j in range(8):
            if j // 4 == c // 4 and j < c:
                msel[:, j] = 1.0
                macc[:, j] = 1.0
        w0, w1 = 176 * c, 176 * (c + 1)
        gq = np.ascontiguousarray(gq_full[:, w0:w1].reshape(32, 128, 176).transpose(1, 0, 2).reshape(128, 32 * 176))
        uq = np.ascontiguousarray(uq_full[:, w0:w1].reshape(32, 128, 176).transpose(1, 0, 2).reshape(128, 32 * 176))
        gsc = np.ascontiguousarray(gs_full[:, 1408 * c:1408 * (c + 1)])
        usc = np.ascontiguousarray(us_full[:, 1408 * c:1408 * (c + 1)])
        gzq = np.ascontiguousarray(gz_full[:, w0:w1])
        uzq = np.ascontiguousarray(uz_full[:, w0:w1])
        dq = np.ascontiguousarray(dq_full[1408 * c:1408 * (c + 1), :].reshape(NTC, 128, 512)
                                  .transpose(1, 0, 2).reshape(128, NTC * 512))
        dsc = np.ascontiguousarray(ds_full[11 * c:11 * (c + 1), :])
        dzq = np.ascontiguousarray(dz_full[11 * c:11 * (c + 1), :])
        in_maps.append(dict(
            xT=xT, w_in_t=w_in_t, w_out_t=w_out_t, conv_w=conv_w_t, conv_b=conv_b_t,
            dt_bias=dt_bias.reshape(64, 1), a_log=a_log.reshape(64, 1), d_exp=d_exp,
            norm_w=nw_t, m_prev=m_prev, prevsel=prevsel, msel=msel, macc=macc,
            gq=gq, uq=uq, gsc=gsc, usc=usc, gzq=gzq, uzq=uzq, dq=dq, dsc=dsc, dzq=dzq))

    res = run_bass_kernel_spmd(nc, in_maps, core_ids=list(range(8)))
    out = np.empty((B, L, D), dtype=np.float32)
    for c in range(8):
        out[c // 4, (c % 4) * TOK:(c % 4 + 1) * TOK, :] = res.results[c]["y_out"].T
    return out


# version 13 (final)
# speedup vs baseline: 1.0302x; 1.0106x over previous
import numpy as np
from contextlib import ExitStack

import concourse.bass as bass
import concourse.mybir as mybir
from concourse.bass_utils import run_bass_kernel_spmd

F32 = mybir.dt.float32
BF16 = mybir.dt.bfloat16
I32 = mybir.dt.int32
ALU = mybir.AluOpType
ACT = mybir.ActivationFunctionType

B, L, D = 2, 4096, 4096
H, PH, N, G, KC = 64, 64, 128, 8, 4
I = H * PH
PROJ = 10304
TOK = 1024
NT = 32
Q = 128
NCH = 8
FI = 11008
FIP = 11264
NTC = 11
EPS = 1e-5
NEG = -1e30

ENGINES = ("tensor", "vector", "scalar", "gpsimd", "sync")


class Sched:
    def __init__(self, nc, ctx):
        self.nc = nc
        self.ctx = ctx
        self.plan = True
        self.eng = None
        self.h = None
        self.cnt = {}
        self.marks = {}
        self.last_wait = {}
        self.used = set()
        self.sems = {}
        self.sem_eng = {}

    def init_sems(self, ctx):
        pass

    def _sem(self, name):
        if name not in self.sems:
            self.sems[name] = self.ctx.enter_context(self.nc.semaphore(f"s_{name}"))
            self.cnt[name] = 0
        return self.sems[name]

    def drain(self, snap):
        for sname, eng2 in list(self.sem_eng.items()):
            if eng2 != self.eng:
                continue
            v = self.cnt.get(sname, 0)
            if v > snap.get(sname, 0):
                lw = self.last_wait.get((self.eng, sname), 0)
                if v > lw:
                    self.h.wait_ge(self.sems[sname], v)
                    self.last_wait[(self.eng, sname)] = v

    def op(self, eng, fn, key=None, deps=(), step=1, async_=False, sem=None):
        isasync = async_ or step == 16
        semname = sem if sem is not None else eng
        if step == 16 or async_:
            assert sem is not None, (key, "DMA/collective ops must name a sem")
        if self.plan:
            self.used.add(eng)
            if key is not None or isasync:
                self._sem(semname)
                self.cnt[semname] += step
                if isasync:
                    self.sem_eng[semname] = eng
                if key is not None:
                    assert key not in self.marks, key
                    self.marks[key] = (eng, semname, self.cnt[semname], isasync)
            return
        if eng != self.eng:
            return
        want = {}
        for d in deps:
            if d is None or d not in self.marks:
                continue
            peng, psem, pval, pasync = self.marks[d]
            if pval > want.get(psem, 0):
                want[psem] = pval
        for psem, pval in want.items():
            lw = self.last_wait.get((eng, psem), 0)
            if pval > lw:
                self.h.wait_ge(self.sems[psem], pval)
                self.last_wait[(eng, psem)] = pval
        inst = fn(self.h)
        if (key is not None or isasync) and inst is not None:
            inst.then_inc(self.sems[semname], step)

    def run_phase(self, name, body):
        self.plan = True
        self.used = set()
        snap = dict(self.cnt)
        body()
        self.plan = False
        used = set(self.used)
        with self.nc.Block(name) as blk:
            for e in ENGINES:
                if e not in used:
                    continue

                def make(e2):
                    def f(h):
                        self.eng, self.h = e2, h
                        body()
                        self.drain(snap)
                    return f
                getattr(blk, e)(make(e))


UPTO = 99
P2D_CHUNKS = NCH
P2D_HEADS = H
P2D_PRE = 15
P2D_OPS = 1023


def build_nc():
    nc = bass.Bass("TRN2", target_bir_lowering=False, debug=False)
    dram, sb, ps = {}, {}, {}

    def din(name, shape, dt):
        dram[name] = nc.dram_tensor(name, list(shape), dt, kind="ExternalInput").ap()

    def dint(name, shape, dt):
        dram[name] = nc.dram_tensor(name, list(shape), dt).ap()

    din("xT", (D, TOK), F32)
    WIN_W = [128] * 80 + [64]
    din("w_in_t", (128, 32 * PROJ), F32)
    din("w_out_t", (128, 32 * D), F32)
    din("conv_w", (128, 48 * 4), F32)
    din("conv_b", (128, 48), F32)
    din("dt_bias", (64, 1), F32)
    din("a_log", (64, 1), F32)
    din("d_exp", (128, 32), F32)
    din("norm_w", (128, 32), F32)
    din("m_prev", (128, 1), F32)
    din("prevsel", (128, 8), F32)
    din("msel", (64, 8), F32)
    din("macc", (128, 8), F32)
    din("gq", (128, 32 * 176), I32)
    din("uq", (128, 32 * 176), I32)
    din("gsc", (32, NTC * 128), F32)
    din("usc", (32, NTC * 128), F32)
    din("gzq", (32, 176), I32)
    din("uzq", (32, 176), I32)
    din("dq", (128, NTC * 512), I32)
    din("dsc", (NTC, D), F32)
    din("dzq", (NTC, 512), I32)
    dram["y_out"] = nc.dram_tensor("y_out", [D, TOK], F32, kind="ExternalOutput").ap()
    dint("gate_spill", (32, 128, TOK), BF16)
    dint("x_spill", (32, NCH, 128, Q), BF16)
    dint("bd_in", (128, 144), F32)
    dint("bd_out", (8 * 128, 144), F32)
    dint("a_c_d", (64, TOK), F32)
    dint("a_g_d", (64, TOK), F32)
    dint("ags_in", (128, 4097), F32)
    dint("ags_out", (8 * 128, 4097), F32)
    for j in range(4):
        dint(f"agx_in{j}", (128, NT * 256), BF16)
        dint(f"agx_out{j}", (8 * 128, NT * 256), BF16)
    dint("xl1_spill", (128, NT * TOK), BF16)
    dint("resid_spill", (32, 128, TOK), BF16)
    dint("zs_gu_d", (32, NTC * 128), F32)
    dint("zs_d_d", (NTC, D), F32)
    dint("g_spill", (16, 128, NTC * 512), BF16)
    dint("h_spill", (16, 128, NTC * 512), BF16)
    for j in range(2):
        dint(f"rs_in{j}", (8 * 2048, TOK), BF16)
        dint(f"rs_out{j}", (2048, TOK), BF16)

    ctx = ExitStack()
    S = Sched(nc, ctx)

    def sbuf(name, shape, dt, st=None):
        sb[name] = (st or ctx).enter_context(nc.sbuf_tensor(name, list(shape), dt))
        return sb[name]

    def psum(name, shape, dt, st=None):
        ps[name] = (st or ctx).enter_context(nc.psum_tensor(name, list(shape), dt))
        return ps[name]

    with ctx:
        S.init_sems(ctx)

        sbuf("eyef", (128, 128), F32)
        sbuf("eyeb", (128, 128), BF16)
        sbuf("trineg", (128, 128), F32)
        sbuf("onesrf", (1, 128), F32)
        sbuf("onescb", (128, 1), BF16)
        sbuf("ones64", (64, 128), F32)
        sbuf("cw", (128, 192), F32)
        sbuf("cb", (128, 48), F32)
        sbuf("dtb", (64, 1), F32)
        sbuf("alog", (64, 1), F32)
        sbuf("aneg", (64, 1), F32)
        sbuf("dexp", (128, 32), F32)
        sbuf("nw", (128, 32), F32)
        sbuf("mprev", (128, 1), F32)
        sbuf("s_prevsel", (128, 8), F32)
        sbuf("s_msel", (64, 8), F32)
        sbuf("s_macc", (128, 8), F32)
        sbuf("epsb", (128, 1), F32)
        nc.const_aps.register(F32, EPS, sb["epsb"][:, 0:1])

        stA = ExitStack()
        sbuf("BC", (128, 16 * TOK), BF16, stA)
        sbuf("dtp", (64, TOK), F32, stA)
        sbuf("a_c", (64, TOK), F32, stA)
        sbuf("a_g", (64, TOK), F32, stA)
        sbuf("expa", (64, TOK), F32, stA)
        sbuf("aT", (128, NCH * 64), F32, stA)
        sbuf("dtT", (128, NCH * 64), F32, stA)
        sbuf("hbchead", (128, 48 * 4), F32, stA)
        sbuf("hbctail", (128, 48 * 3), F32, stA)
        sbuf("atot", (64, 1), F32, stA)
        sbuf("arow0", (1, 8 * Q), F32, stA)
        sbuf("arow1", (1, 8 * Q), F32, stA)
        sb["dtraw"] = sb["dtp"]
        sb["expag"] = sb["a_g"]

        def p0():
            for nm, src in (("cw", "conv_w"), ("cb", "conv_b"), ("dtb", "dt_bias"),
                            ("alog", "a_log"), ("dexp", "d_exp"), ("nw", "norm_w"),
                            ("mprev", "m_prev"), ("s_prevsel", "prevsel"),
                            ("s_msel", "msel"), ("s_macc", "macc")):
                S.op("gpsimd", lambda g, nm=nm, src=src: g.dma_start(out=sb[nm][:, :], in_=dram[src]),
                     key=("ld", nm), step=16, sem="c0")
            S.op("gpsimd", lambda g: g.iota(sb["eyef"][:, :], [[-1, 128]], base=0,
                                            channel_multiplier=1, allow_small_or_imprecise_dtypes=True),
                 key=("iota", 0))
            S.op("gpsimd", lambda g: g.iota(sb["trineg"][:, :], [[-1, 128]], base=0,
                                            channel_multiplier=1, allow_small_or_imprecise_dtypes=True),
                 key=("iota", 1))
            S.op("gpsimd", lambda g: g.memset(sb["onesrf"][:, :], 1.0), key=("ms", 0))
            S.op("gpsimd", lambda g: g.memset(sb["onescb"][:, :], 1.0), key=("ms", 1))
            S.op("gpsimd", lambda g: g.memset(sb["ones64"][:, :], 1.0), key=("ms", 2))
            S.op("gpsimd", lambda g: g.memset(sb["epsb"][:, :], EPS), key=("ms", 7))
            S.op("vector", lambda v: v.tensor_single_scalar(sb["eyef"][:, :], sb["eyef"][:, :], 0.0, ALU.is_equal),
                 key=("eyef",), deps=[("iota", 0)])
            S.op("vector", lambda v: v.tensor_copy(sb["eyeb"][:, :], sb["eyef"][:, :]),
                 key=("eyeb",), deps=[("eyef",)])
            S.op("vector", lambda v: v.tensor_single_scalar(sb["trineg"][:, :], sb["trineg"][:, :], 0.0, ALU.is_le),
                 key=("tri0",), deps=[("iota", 1)])
            S.op("vector", lambda v: v.tensor_scalar(sb["trineg"][:, :], sb["trineg"][:, :], -NEG, -NEG,
                                                     ALU.mult, ALU.subtract),
                 key=("tri1",), deps=[("tri0",)])
            S.op("scalar", lambda s: s.activation(sb["aneg"][:, :], sb["alog"][:, :], ACT.Exp),
                 key=("an0",), deps=[("ld", "s_macc")])
            S.op("vector", lambda v: v.tensor_single_scalar(sb["aneg"][:, :], sb["aneg"][:, :], -1.0, ALU.mult),
                 key=("an1",), deps=[("an0",)])

        if UPTO >= 0:
            S.run_phase("p0", p0)

        st1 = ExitStack()
        sbuf("xTb", (128, NT * TOK), BF16, st1)
        sbuf("wbuf0", (128, 32 * 128), BF16, st1)
        sbuf("wbuf1", (128, 32 * 128), BF16, st1)
        for i in range(2):
            sbuf(f"hbcf{i}", (128, TOK), F32, st1)
            sbuf(f"ctmp{i}", (128, TOK), F32, st1)
            sbuf(f"gstage{i}", (128, TOK), BF16, st1)
        for i in range(2):
            sbuf(f"xstage{i}", (128, TOK), BF16, st1)
        psum("pin0", (128, 512), F32, st1)
        psum("pin1", (128, 512), F32, st1)

        win_off = np.concatenate([[0], np.cumsum([32 * w for w in WIN_W])]).astype(int)
        p1_order = list(range(32, 80)) + [80] + list(range(0, 32))

        def p1():
            for k in range(NT):
                S.op("gpsimd", lambda g, k=k: g.dma_start(
                    out=sb["xTb"][:, k * TOK:(k + 1) * TOK],
                    in_=dram["xT"][k * 128:(k + 1) * 128, :]), key=("xld", k), step=16, sem="xld")
            for idx, pi in enumerate(p1_order):
                pw = WIN_W[pi]
                buf = sb[f"wbuf{idx % 2}"]
                dep = [("p1mm", p1_order[idx - 2], 1)] if idx >= 2 else []
                S.op("gpsimd", lambda g, pi=pi, buf=buf: g.dma_start(
                    out=buf[:, 0:32 * WIN_W[pi]],
                    in_=dram["w_in_t"][:, int(win_off[pi]):int(win_off[pi + 1])]),
                    key=("wld", pi), deps=dep, step=16, sem=f"wld{idx % 2}")
                for half in range(2):
                    pst = ps[f"pin{half}"]
                    mmdeps = [("wld", pi)] + ([("xld", k) for k in range(NT)] if idx == 0 else [])
                    if idx >= 1:
                        mmdeps.append(("p1ev", p1_order[idx - 1], half))
                    for k in range(NT):
                        last = k == NT - 1
                        S.op("tensor", lambda t, pst=pst, buf=buf, k=k, pw=pw, half=half: t.matmul(
                            pst[0:pw, :], buf[:, k * pw:(k + 1) * pw],
                            sb["xTb"][:, k * TOK + half * 512: k * TOK + half * 512 + 512],
                            start=(k == 0), stop=last),
                            key=(("p1mm", pi, half) if last else None),
                            deps=(mmdeps if k == 0 else []))
                if 32 <= pi < 80:
                    T = pi - 32
                    hf = sb[f"hbcf{T % 2}"]
                    tmp = sb[f"ctmp{T % 2}"]
                    guard = [("p1silu", T - 2)] if T >= 2 else []
                    for half in range(2):
                        S.op("scalar", lambda s, hf=hf, half=half: s.copy(
                            hf[:, half * 512:half * 512 + 512], ps[f"pin{half}"][:, :]),
                            key=("p1ev", pi, half), deps=[("p1mm", pi, half)] + guard)
                    S.op("vector", lambda v, hf=hf, T=T: v.tensor_copy(sb["hbchead"][:, T * 4:T * 4 + 4], hf[:, 0:4]),
                         key=("p1hd", T), deps=[("p1ev", pi, 0)])
                    S.op("vector", lambda v, hf=hf, T=T: v.tensor_copy(sb["hbctail"][:, T * 3:T * 3 + 3], hf[:, 1021:1024]),
                         key=("p1tl", T), deps=[("p1ev", pi, 1)])
                    S.op("vector", lambda v, tmp=tmp, hf=hf, T=T: v.scalar_tensor_tensor(
                        tmp[:, :], hf[:, :], sb["cw"][:, T * 4 + 3:T * 4 + 4],
                        sb["cb"][:, T:T + 1].broadcast_to((128, TOK)), ALU.mult, ALU.add),
                        key=("p1c0", T), deps=[("p1ev", pi, 0), ("p1ev", pi, 1)])
                    for kk, wi in ((1, 2), (2, 1), (3, 0)):
                        S.op("vector", lambda v, tmp=tmp, hf=hf, T=T, kk=kk, wi=wi: v.scalar_tensor_tensor(
                            tmp[:, kk:TOK], hf[:, 0:TOK - kk], sb["cw"][:, T * 4 + wi:T * 4 + wi + 1],
                            tmp[:, kk:TOK], ALU.mult, ALU.add),
                            key=("p1c", T, kk), deps=[("p1c0", T) if kk == 1 else ("p1c", T, kk - 1)])
                    if T < 32:
                        xst = sb[f"xstage{T % 2}"]
                        sdep = [("p1c", T, 3), ("p1hd", T), ("p1tl", T)]
                        if T >= 2:
                            sdep.append(("p1xs", T - 2, NCH - 1))
                        S.op("scalar", lambda s, xst=xst, tmp=tmp: s.activation(xst[:, :], tmp[:, :], ACT.Silu),
                             key=("p1silu", T), deps=sdep)
                        for c in range(NCH):
                            S.op("sync", lambda s, xst=xst, T=T, c=c: s.dma_start(
                                out=dram["x_spill"][T, c], in_=xst[:, c * Q:(c + 1) * Q]),
                                key=("p1xs", T, c), deps=[("p1silu", T)], step=16, sem=f"xsp{T % 2}")
                    else:
                        bc = T - 32
                        S.op("scalar", lambda s, tmp=tmp, bc=bc: s.activation(
                            sb["BC"][:, bc * TOK:(bc + 1) * TOK], tmp[:, :], ACT.Silu),
                            key=("p1silu", T), deps=[("p1c", T, 3), ("p1hd", T), ("p1tl", T)])
                elif pi == 80:
                    for half in range(2):
                        S.op("scalar", lambda s, half=half: s.copy(
                            sb["dtraw"][:, half * 512:half * 512 + 512], ps[f"pin{half}"][0:64, :]),
                            key=("p1ev", pi, half), deps=[("p1mm", pi, half)])
                else:
                    gst = sb[f"gstage{pi % 2}"]
                    guard = [("p1gs", pi - 2)] if pi >= 2 else []
                    for half in range(2):
                        S.op("scalar", lambda s, gst=gst, half=half: s.copy(
                            gst[:, half * 512:half * 512 + 512], ps[f"pin{half}"][:, :]),
                            key=("p1ev", pi, half), deps=[("p1mm", pi, half)] + guard)
                    S.op("sync", lambda s, gst=gst, pi=pi: s.dma_start(
                        out=dram["gate_spill"][pi], in_=gst[:, :]),
                        key=("p1gs", pi), deps=[("p1ev", pi, 0), ("p1ev", pi, 1)], step=16, sem=f"gsp{pi % 2}")

        if UPTO >= 1:
            S.run_phase("p1", p1)
        st1.close()

        stY = ExitStack()
        sbuf("y", (128, NT * TOK), BF16, stY)
        st2 = ExitStack()
        sbuf("Scur", (128, H * 64), F32, st2)
        sbuf("Sbf0", (128, H * 64), BF16, st2)
        sbuf("Sbf1", (128, H * 64), BF16, st2)
        for i in range(2):
            sbuf(f"evt{i}", (128, 128), F32, st2)
            sbuf(f"Ep{i}", (128, 128), F32, st2)
        sbuf("erep", (128, 64), F32, st2)
        sbuf("erow", (1, 64), F32, st2)
        sbuf("aerep", (128, 64), F32, st2)
        sbuf("expfac", (128, 1), F32, st2)
        psum("ptrf0", (128, 512), F32, st2)
        psum("ptrf1", (128, 512), F32, st2)
        sb["Sin"] = sb["Scur"]
        sb["Sinb"] = sb["Sbf0"]
        st2b = ExitStack()
        sbuf("bdall", (128, 8 * 144), F32, st2b)
        sbuf("bdprev", (128, 144), F32, st2b)
        sbuf("fixt", (128, 3), F32, st2b)
        sbuf("fixs", (128, 3), BF16, st2b)

        def p2a():
            S.op("gpsimd", lambda g: g.dma_start(out=dram["bd_in"], in_=sb["hbctail"][:, :]),
                 key=("bdst",), step=16, sem="bdio")
            S.op("gpsimd", lambda g: g.collective_compute(
                "AllGather", ALU.bypass, replica_groups=[list(range(8))],
                ins=[dram["bd_in"].opt()], outs=[dram["bd_out"].opt()]),
                key=("bdcc",), deps=[("bdst",)], async_=True, sem="ccbd")
            S.op("gpsimd", lambda g: g.dma_start(
                out=sb["bdall"][:, :].rearrange("p (r w) -> p r w", r=8),
                in_=dram["bd_out"].rearrange("(r p) w -> p r w", p=128)),
                key=("bdld",), deps=[("bdcc",)], step=16, sem="bdio")
            S.op("vector", lambda v: v.memset(sb["bdprev"][:, :], 0.0), key=("bd0",))
            for j in range(8):
                S.op("vector", lambda v, j=j: v.scalar_tensor_tensor(
                    sb["bdprev"][:, :], sb["bdall"][:, j * 144:(j + 1) * 144],
                    sb["s_prevsel"][:, j:j + 1], sb["bdprev"][:, :], ALU.mult, ALU.add),
                    key=("bdacc", j), deps=[("bdld",), ("bd0",) if j == 0 else ("bdacc", j - 1)])
            S.op("vector", lambda v: v.tensor_single_scalar(sb["bdprev"][:, :], sb["bdprev"][:, :],
                                                            sb["mprev"][:, 0:1], ALU.mult),
                 key=("bdm",), deps=[("bdacc", 7)])

        if UPTO >= 2:
            S.run_phase("p2a", p2a)

        def p2b():
            for T in range(48):
                bd = (sb["bdprev"], T * 3)
                hh = (sb["hbchead"], T * 4)
                S.op("vector", lambda v, T=T, hh=hh: v.scalar_tensor_tensor(
                    sb["fixt"][:, :], hh[0][:, hh[1]:hh[1] + 3], sb["cw"][:, T * 4 + 3:T * 4 + 4],
                    sb["cb"][:, T:T + 1].broadcast_to((128, 3)), ALU.mult, ALU.add),
                    key=("f0", T), deps=([("bdm",)] if T == 0 else [("fxw", T - 1)]))
                steps = [
                    (2, [(bd, 2, 0, 1), (hh, 0, 1, 2)]),
                    (1, [(bd, 1, 0, 2), (hh, 0, 2, 1)]),
                    (0, [(bd, 0, 0, 3)]),
                ]
                pk = ("f0", T)
                for si, (wi, parts) in enumerate(steps):
                    for pj, (src, so, do, cnt) in enumerate(parts):
                        S.op("vector", lambda v, T=T, wi=wi, src=src, so=so, do=do, cnt=cnt: v.scalar_tensor_tensor(
                            sb["fixt"][:, do:do + cnt], src[0][:, src[1] + so:src[1] + so + cnt],
                            sb["cw"][:, T * 4 + wi:T * 4 + wi + 1], sb["fixt"][:, do:do + cnt],
                            ALU.mult, ALU.add),
                            key=("fx", T, si, pj), deps=[pk])
                        pk = ("fx", T, si, pj)
                S.op("scalar", lambda s: s.activation(sb["fixs"][:, :], sb["fixt"][:, :], ACT.Silu),
                     key=("fsl", T), deps=[pk])
                if T < 32:
                    S.op("sync", lambda s, T=T: s.dma_start(out=dram["x_spill"][T, 0][:, 0:3],
                                                            in_=sb["fixs"][:, :],
                                                            allow_slow_non_contiguous=True),
                         key=("fxw", T), deps=[("fsl", T)], step=16, sem="fxio")
                else:
                    S.op("vector", lambda v, T=T: v.tensor_copy(
                        sb["BC"][:, (T - 32) * TOK:(T - 32) * TOK + 3], sb["fixs"][:, :]),
                        key=("fxw", T), deps=[("fsl", T)])
            S.op("scalar", lambda s: s.activation(sb["dtp"][:, :], sb["dtraw"][:, :], ACT.Exp,
                                                  bias=sb["dtb"][:, 0:1]), key=("dtp0",))
            S.op("vector", lambda v: v.tensor_single_scalar(sb["dtp"][:, :], sb["dtp"][:, :], 1.0, ALU.add),
                 key=("dtp1",), deps=[("dtp0",)])
            S.op("scalar", lambda s: s.activation(sb["dtp"][:, :], sb["dtp"][:, :], ACT.Ln),
                 key=("dtp",), deps=[("dtp1",)])
            S.op("vector", lambda v: v.tensor_single_scalar(sb["a_c"][:, :], sb["dtp"][:, :],
                                                            sb["aneg"][:, 0:1], ALU.mult),
                 key=("dta",), deps=[("dtp",)])
            for c in range(NCH):
                S.op("vector", lambda v, c=c: v.tensor_tensor_scan(
                    sb["a_c"][:, c * Q:(c + 1) * Q], sb["ones64"][:, :], sb["a_c"][:, c * Q:(c + 1) * Q],
                    0.0, ALU.mult, ALU.add), key=("scan", c), deps=[("dta",)])
            S.op("vector", lambda v: v.memset(sb["atot"][:, :], 0.0), key=("off0",))
            for c in range(NCH):
                S.op("vector", lambda v, c=c: v.tensor_tensor(
                    sb["a_g"][:, c * Q:(c + 1) * Q], sb["a_c"][:, c * Q:(c + 1) * Q],
                    sb["atot"][:, 0:1].broadcast_to((64, Q)), ALU.add),
                    key=("ago", c), deps=[("scan", c), ("off0",) if c == 0 else ("offu", c - 1)])
                S.op("vector", lambda v, c=c: v.tensor_tensor(
                    sb["atot"][:, 0:1], sb["atot"][:, 0:1], sb["a_c"][:, (c + 1) * Q - 1:(c + 1) * Q], ALU.add),
                    key=("offu", c), deps=[("ago", c)])
            S.op("sync", lambda s: s.dma_start(out=dram["a_c_d"], in_=sb["a_c"][:, :]),
                 key=("acd",), deps=[("scan", NCH - 1)], step=16, sem="acd")
            S.op("sync", lambda s: s.dma_start(out=dram["a_g_d"], in_=sb["a_g"][:, :]),
                 key=("agd",), deps=[("ago", NCH - 1)], step=16, sem="agd")
            S.op("scalar", lambda s: s.activation(sb["expa"][:, :], sb["a_c"][:, :], ACT.Exp),
                 key=("expa",), deps=[("scan", NCH - 1)])
            S.op("scalar", lambda s: s.activation(sb["expag"][:, :], sb["a_g"][:, :], ACT.Exp),
                 key=("expag",), deps=[("ago", NCH - 1), ("agd",)])
            for c in range(NCH):
                pt = ps[f"ptrf{c % 2}"]
                dep = [("scan", c)] + ([("aTe", c - 2), ("dtTe", c - 2)] if c >= 2 else [])
                S.op("tensor", lambda t, c=c, pt=pt: t.transpose(
                    pt[:, 0:64], sb["a_c"][:, c * Q:(c + 1) * Q], sb["eyef"][0:64, 0:64]),
                    key=("aTt", c), deps=dep)
                S.op("vector", lambda v, c=c, pt=pt: v.tensor_copy(sb["aT"][:, c * 64:(c + 1) * 64], pt[:, 0:64]),
                     key=("aTe", c), deps=[("aTt", c)])
                S.op("tensor", lambda t, c=c, pt=pt: t.transpose(
                    pt[:, 64:128], sb["dtp"][:, c * Q:(c + 1) * Q], sb["eyef"][0:64, 0:64]),
                    key=("dtTt", c), deps=[("dtp",), ("aTe", c)])
                S.op("vector", lambda v, c=c, pt=pt: v.tensor_copy(sb["dtT"][:, c * 64:(c + 1) * 64], pt[:, 64:128]),
                     key=("dtTe", c), deps=[("dtTt", c)])

        if UPTO >= 3:
            S.run_phase("p2b", p2b)

        st2b.close()
        st2d = ExitStack()
        for i in range(2):
            sbuf(f"xc{i}", (128, 32 * Q), BF16, st2d)
            sbuf(f"dbuf{i}", (128, 128), F32, st2d)
            sbuf(f"m2b{i}", (128, 128), BF16, st2d)
            sbuf(f"x1t{i}", (128, 64), BF16, st2d)
            sbuf(f"x2t{i}", (128, 64), BF16, st2d)
            sbuf(f"btb{i}", (128, 128), BF16, st2d)
        sbuf("btg", (128, 8 * 128), BF16, st2d)
        psum("pg2", (128, 512), F32, st2d)
        psum("pyA", (128, 512), F32, st2d)
        psum("pyB", (128, 512), F32, st2d)
        psum("pdS", (128, 512), F32, st2d)
        psum("ptb", (128, 1024), BF16, st2d)

        def p2d():
            S.op("vector", lambda v: v.memset(sb["Scur"][:, :], 0.0), key=("sc0",))
            S.op("scalar", lambda s: s.copy(sb["Sbf0"][:, :], sb["Scur"][:, :]), key=("sb0",), deps=[("sc0",)])
            for c in range(P2D_CHUNKS):
                if not (P2D_PRE & 1):
                    pass
                for T in (range(32) if (P2D_PRE & 1) else []):
                    dep = [("p1xs", T, c)] + ([("fxw", 31)] if c == 0 else [])
                    if c >= 2:
                        dep.append(("yev", c - 2, 31))
                    S.op("sync", lambda s, T=T, c=c: s.dma_start(
                        out=sb[f"xc{c % 2}"][:, T * Q:(T + 1) * Q], in_=dram["x_spill"][T, c]),
                        key=("xcl", c, T), deps=dep, step=16, sem=f"xc{c % 2}")
                for gg8 in (range(8) if (P2D_PRE & 2) else []):
                    adep = [("acd",)]
                    pgi = (c * 8 + gg8) - 2
                    if pgi >= 0:
                        adep.append(("arp", pgi // 8, (pgi % 8) * 2 + 1))
                    S.op("sync", lambda s, c=c, gg8=gg8: s.dma_start(
                        out=sb[f"arow{gg8 % 2}"][:, :].rearrange("o (h t) -> o h t", h=8),
                        in_=dram["a_c_d"][gg8 * 8:(gg8 + 1) * 8, c * Q:(c + 1) * Q]),
                        key=("arw", c, gg8), deps=adep, step=16, sem=f"ar{gg8 % 2}")
                if not (P2D_PRE & 4):
                    continue
                S.op("tensor", lambda t, c=c: t.transpose(
                    ps["ptrf0"][0:1, 0:64], sb["a_c"][:, (c + 1) * Q - 1:(c + 1) * Q],
                    sb["eyef"][0:64, 0:64]),
                    key=("aet", c), deps=[("scan", c)] + ([("dst", c - 1, 63), ("epx", c - 1, 63)] if c >= 1 else []))
                S.op("vector", lambda v: v.tensor_copy(sb["erow"][:, :], ps["ptrf0"][0:1, 0:64]),
                     key=("aee", c), deps=[("aet", c)])
                S.op("tensor", lambda t: t.matmul(
                    ps["ptrf1"][:, 0:64], sb["onesrf"][0:1, :], sb["erow"][0:1, 0:64],
                    start=True, stop=True),
                    key=("aert", c), deps=[("aee", c)])
                S.op("vector", lambda v: v.tensor_copy(sb["aerep"][:, :], ps["ptrf1"][:, 0:64]),
                     key=("aerE", c), deps=[("aert", c)] + ([("xf", c - 1, 63)] if c >= 1 else []))
                S.op("scalar", lambda s: s.activation(sb["erep"][:, :], sb["aerep"][:, :], ACT.Exp),
                     key=("erepx", c), deps=[("aerE", c)])
                for g in (range(8) if (P2D_PRE & 8) else []):
                    btdep = [("bte", c, g - 1)] if g >= 1 else ([("bte", c - 1, 7), ("x1", c - 1, 63)] if c >= 1 else [])
                    if c == 0:
                        btdep += [("fxw", 32 + g)]
                    S.op("tensor", lambda t, c=c, g=g: t.transpose(
                        ps["ptb"][:, 0:128], sb["BC"][:, g * TOK + c * Q: g * TOK + (c + 1) * Q],
                        sb["eyeb"][:, :]),
                        key=("btt", c, g), deps=btdep)
                    S.op("scalar", lambda s, g=g: s.copy(sb["btg"][:, g * 128:(g + 1) * 128], ps["ptb"][:, 0:128]),
                         key=("bte", c, g), deps=[("btt", c, g)])
                for h in range(P2D_HEADS):
                    g = h // 8
                    pp = h // 2
                    hr = (h % 2) * 64
                    i2 = h % 2
                    sbf_r = sb[f"Sbf{c % 2}"]
                    sbf_w = sb[f"Sbf{(c + 1) % 2}"]
                    if (P2D_OPS & 64) and h % 8 == 0:
                        g2dep = []
                        if c == 0:
                            g2dep += [("fxw", 32 + g), ("fxw", 40 + g)]
                        if g >= 1:
                            g2dep.append(("m2", c, g * 8 - 1))
                        elif c >= 1:
                            g2dep.append(("m2", c - 1, 63))
                        S.op("tensor", lambda t, c=c, g=g: t.matmul(
                            ps["pg2"][:, 0:128], sb["BC"][:, g * TOK + c * Q:g * TOK + (c + 1) * Q],
                            sb["BC"][:, (8 + g) * TOK + c * Q:(8 + g) * TOK + (c + 1) * Q],
                            start=True, stop=True),
                            key=("g2", c, g), deps=g2dep)
                    if (P2D_OPS & 1) and h % 4 == 0:
                        S.op("tensor", lambda t, c=c, h=h: t.matmul(
                            ps["ptrf0"][:, 0:512], sb["onesrf"][0:1, :],
                            sb[f"arow{(h // 8) % 2}"][0:1, (h % 8) * Q:(h % 8 + 4) * Q],
                            start=True, stop=True),
                            key=("arp", c, h // 4), deps=[("arw", c, h // 8)] +
                            ([("dst", c, h - 1), ("epx", c, h - 1)] if h >= 1 else []))
                    if P2D_OPS & 1:
                        S.op("vector", lambda v, c=c, h=h, i2=i2: v.scalar_tensor_tensor(
                            sb[f"dbuf{i2}"][:, :], ps["ptrf0"][:, (h % 4) * 128:(h % 4) * 128 + 128],
                            sb["aT"][:, c * 64 + h:c * 64 + h + 1],
                            sb["trineg"][:, :], ALU.subtract, ALU.add),
                            key=("dst", c, h), deps=[("arp", c, h // 4), ("aTe", c),
                                                     ("m2", c, h - 2) if h >= 2 else None,
                                                     ("epx", c, h - 1) if h >= 1 else None]),
                    if P2D_OPS & 512:
                        epdep = [("arp", c, h // 4), ("dst", c, h)]
                        if pp >= 2:
                            epdep.append(("ye1", c, pp - 2))
                        elif c >= 1:
                            epdep.append(("ye1", c - 1, 30 + pp))
                        S.op("scalar", lambda s, hr=hr, pp=pp, h=h: s.activation(
                            sb[f"Ep{pp % 2}"][hr:hr + 64, :],
                            ps["ptrf0"][hr:hr + 64, (h % 4) * 128:(h % 4) * 128 + 128], ACT.Exp),
                            key=("epx", c, h), deps=epdep)
                    if P2D_OPS & 128:
                        S.op("scalar", lambda s, i2=i2: s.activation(sb[f"dbuf{i2}"][:, :], sb[f"dbuf{i2}"][:, :], ACT.Exp),
                             key=("xpd", c, h), deps=[("dst", c, h)])
                    if P2D_OPS & 256:
                        S.op("vector", lambda v, i2=i2: v.tensor_tensor(
                            sb[f"m2b{i2}"][:, :], sb[f"dbuf{i2}"][:, :], ps["pg2"][:, 0:128], ALU.mult),
                            key=("m2", c, h), deps=[("xpd", c, h), ("g2", c, g), ("mmA", c, h - 2) if h >= 2 else None])
                    if P2D_OPS & 2:
                        S.op("tensor", lambda t, c=c, pp=pp, hr=hr: t.transpose(
                            ps["ptb"][:, 128:192], sb[f"xc{c % 2}"][hr:hr + 64, pp * Q:(pp + 1) * Q],
                            sb["eyeb"][hr:hr + 64, hr:hr + 64]),
                            key=("xtt", c, h), deps=[("xcl", c, 31), ("bte", c, 7)] +
                            ([("x2", c, h - 1)] if h >= 1 else ([("x2", c - 1, 63)] if c >= 1 else [])))
                    if P2D_OPS & 2:
                        S.op("vector", lambda v, c=c, h=h, i2=i2: v.tensor_single_scalar(
                            sb[f"x1t{i2}"][:, :], ps["ptb"][:, 128:192], sb["dtT"][:, c * 64 + h:c * 64 + h + 1], ALU.mult),
                            key=("x1", c, h), deps=[("xtt", c, h), ("dtTe", c)])
                    if P2D_OPS & 2:
                        S.op("scalar", lambda s, c=c, h=h: s.activation(
                            sb["expfac"][:, :], sb["aT"][:, c * 64 + h:c * 64 + h + 1], ACT.Exp,
                            bias=sb["aerep"][:, h:h + 1], scale=-1.0),
                            key=("xf", c, h), deps=[("aerE", c)] + ([("bt", c, h - 1)] if h >= 1 else [])),
                    if P2D_OPS & 2:
                        S.op("vector", lambda v, i2=i2: v.tensor_single_scalar(
                            sb[f"x2t{i2}"][:, :], sb[f"x1t{i2}"][:, :], sb["expfac"][:, 0:1], ALU.mult),
                            key=("x2", c, h), deps=[("x1", c, h), ("xf", c, h)])
                    if P2D_OPS & 2:
                        S.op("vector", lambda v, g=g, i2=i2: v.tensor_single_scalar(
                            sb[f"btb{i2}"][:, :], sb["btg"][:, g * 128:(g + 1) * 128], sb["expfac"][:, 0:1], ALU.mult),
                            key=("bt", c, h), deps=[("bte", c, g), ("x2", c, h)])
                    mmadep = [("m2", c, h), ("x1", c, h)]
                    if not (P2D_OPS & 4):
                        mmadep = None
                    if mmadep is not None and h % 2 == 0:
                        mmadep.append(("ye2", c, pp - 1) if pp >= 1 else (("ye2", c - 1, 31) if c >= 1 else None))
                    if mmadep is not None:
                        S.op("tensor", lambda t, hr=hr, i2=i2: t.matmul(
                            ps["pyA"][hr:hr + 64, 0:128], sb[f"x1t{i2}"][:, :], sb[f"m2b{i2}"][:, :],
                            start=True, stop=True),
                            key=("mmA", c, h), deps=mmadep)
                    mmbdep = [("sb0",)] if c == 0 else [("sbc", c - 1, h)]
                    if not (P2D_OPS & 8):
                        mmbdep = None
                    if mmbdep is not None and h % 2 == 0:
                        mmbdep.append(("ye1", c, pp - 1) if pp >= 1 else (("ye1", c - 1, 31) if c >= 1 else None))
                    if mmbdep is not None:
                        S.op("tensor", lambda t, hr=hr, h=h, g=g, c=c, sbf_r=sbf_r: t.matmul(
                            ps["pyB"][hr:hr + 64, 0:128], sbf_r[:, h * 64:(h + 1) * 64],
                            sb["BC"][:, (8 + g) * TOK + c * Q:(8 + g) * TOK + (c + 1) * Q],
                            start=True, stop=True),
                            key=("mmB", c, h), deps=mmbdep)
                    if P2D_OPS & 16:
                        S.op("tensor", lambda t, i2=i2: t.matmul(
                            ps["pdS"][:, 0:64], sb[f"btb{i2}"][:, :], sb[f"x2t{i2}"][:, :],
                            start=True, stop=True),
                            key=("mmD", c, h), deps=[("bt", c, h), ("x2", c, h)] +
                            ([("sch", c, h - 1)] if h >= 1 else ([("sch", c - 1, 63)] if c >= 1 else [])))
                        S.op("vector", lambda v, h=h: v.scalar_tensor_tensor(
                            sb["Scur"][:, h * 64:(h + 1) * 64], sb["Scur"][:, h * 64:(h + 1) * 64],
                            sb["erep"][:, h:h + 1], ps["pdS"][:, 0:64], ALU.mult, ALU.add),
                            key=("sch", c, h), deps=[("mmD", c, h), ("erepx", c)])
                        S.op("scalar", lambda s, h=h, sbf_w=sbf_w: s.copy(
                            sbf_w[:, h * 64:(h + 1) * 64], sb["Scur"][:, h * 64:(h + 1) * 64]),
                            key=("sbc", c, h), deps=[("sch", c, h)])
                    if (P2D_OPS & 32) and h % 2 == 1:
                        ev = sb[f"evt{pp % 2}"]
                        S.op("vector", lambda v, ev=ev, pp=pp: v.tensor_tensor(ev[:, :], ps["pyB"][:, 0:128],
                                                                               sb[f"Ep{pp % 2}"][:, :], ALU.mult),
                             key=("ye1", c, pp), deps=[("epx", c, h), ("mmB", c, h),
                                                       ("yev", c, pp - 2) if pp >= 2 else (("yev", c - 1, 30 + pp) if c >= 1 else None)])
                        S.op("vector", lambda v, ev=ev: v.tensor_tensor(ev[:, :], ev[:, :], ps["pyA"][:, 0:128], ALU.add),
                             key=("ye2", c, pp), deps=[("ye1", c, pp), ("mmA", c, h)])
                        S.op("vector", lambda v, ev=ev, c=c, pp=pp: v.scalar_tensor_tensor(
                            sb["y"][:, pp * TOK + c * Q:pp * TOK + (c + 1) * Q],
                            sb[f"xc{c % 2}"][:, pp * Q:(pp + 1) * Q], sb["dexp"][:, pp:pp + 1],
                            ev[:, :], ALU.mult, ALU.add),
                            key=("yev", c, pp), deps=[("ye2", c, pp)])

        if UPTO >= 4:
            S.run_phase("p2d", p2d)
        st2d.close()

        st3 = ExitStack()
        sbuf("sj", (128, 4096), F32, st3)
        sbuf("tv", (64, 1), F32, st3)
        sbuf("ejr", (128, 64), F32, st3)
        sbuf("ejrow", (1, 64), F32, st3)
        psum("pyA3", (128, 512), F32, st3)

        def p3():
            S.op("sync", lambda s: s.dma_start(out=dram["ags_in"][:, 0:4096], in_=sb["Scur"][:, :]),
                 key=("agsst",), deps=[("sch", NCH - 1, 63)], step=16, sem="agsio")
            S.op("sync", lambda s: s.dma_start(out=dram["ags_in"][0:64, 4096:4097], in_=sb["atot"][:, :],
                                               allow_slow_non_contiguous=True),
                 key=("agsst2",), deps=[("offu", NCH - 1)], step=16, sem="agsio")
            S.op("sync", lambda s: s.dma_start(out=dram["ags_in"][64:128, 4096:4097], in_=sb["atot"][:, :],
                                               allow_slow_non_contiguous=True),
                 key=("agsst3",), deps=[("offu", NCH - 1)], step=16, sem="agsio")
            S.op("gpsimd", lambda g: g.collective_compute(
                "AllGather", ALU.bypass, replica_groups=[list(range(8))],
                ins=[dram["ags_in"].opt()], outs=[dram["ags_out"].opt()]),
                key=("agscc",), deps=[("agsst",), ("agsst2",), ("agsst3",)], async_=True, sem="ccags")
            S.op("vector", lambda v: v.memset(sb["Sin"][:, :], 0.0), key=("si0",), deps=[("agsst",), ("agsst2",), ("agsst3",)])
            for j in range(7):
                bj = "sj"
                S.op("gpsimd", lambda g, j=j, bj=bj: g.dma_start(
                    out=sb[bj][:, :], in_=dram["ags_out"][j * 128:(j + 1) * 128, 0:4096]),
                    key=("sjl", j), deps=[("agscc",)] + ([("acc2", j - 1)] if j >= 1 else []), step=16, sem="sjl")
                S.op("gpsimd", lambda g, j=j: g.dma_start(
                    out=sb["tv"][:, :], in_=dram["ags_out"][j * 128:j * 128 + 64, 4096:4097],
                    allow_slow_non_contiguous=True),
                    key=("tvl", j), deps=[("agscc",)] + ([("tvt", j - 1)] if j >= 1 else []), step=16, sem="tvl")
                S.op("vector", lambda v, j=j: v.tensor_tensor(sb["tv"][:, :], sb["tv"][:, :],
                                                              sb["s_msel"][:, j:j + 1], ALU.mult),
                     key=("tvm", j), deps=[("tvl", j)])
                S.op("scalar", lambda s: s.activation(sb["tv"][:, :], sb["tv"][:, :], ACT.Exp),
                     key=("tvx", j), deps=[("tvm", j)])
                S.op("tensor", lambda t: t.transpose(ps["ptrf0"][0:1, 0:64], sb["tv"][:, :],
                                                     sb["eyef"][0:64, 0:64]),
                     key=("tvt", j), deps=[("tvx", j)] + ([("tve", j - 1)] if j >= 1 else []))
                S.op("vector", lambda v: v.tensor_copy(sb["ejrow"][:, :], ps["ptrf0"][0:1, 0:64]),
                     key=("tve", j), deps=[("tvt", j)])
                S.op("tensor", lambda t: t.matmul(ps["ptrf1"][:, 0:64], sb["onesrf"][0:1, :],
                                                  sb["ejrow"][0:1, 0:64], start=True, stop=True),
                     key=("ejrt", j), deps=[("tve", j)])
                S.op("vector", lambda v: v.tensor_copy(sb["ejr"][:, :], ps["ptrf1"][:, 0:64]),
                     key=("ejre", j), deps=[("ejrt", j)])
                S.op("vector", lambda v: v.tensor_tensor(
                    sb["Sin"][:, :].rearrange("n (h p) -> n h p", h=64),
                    sb["Sin"][:, :].rearrange("n (h p) -> n h p", h=64),
                    sb["ejr"][:, :].rearrange("n (h p) -> n h p", p=1).broadcast_to((128, 64, 64)),
                    ALU.mult),
                    key=("acc1", j), deps=[("ejre", j), ("si0",) if j == 0 else ("acc2", j - 1)])
                S.op("vector", lambda v, j=j, bj=bj: v.scalar_tensor_tensor(
                    sb["Sin"][:, :], sb[bj][:, :], sb["s_macc"][:, j:j + 1], sb["Sin"][:, :],
                    ALU.mult, ALU.add),
                    key=("acc2", j), deps=[("acc1", j), ("sjl", j)])
            S.op("scalar", lambda s: s.copy(sb["Sinb"][:, :], sb["Sin"][:, :]),
                 key=("sinb",), deps=[("acc2", 6)])
            for c in range(NCH):
                for gg8 in range(8):
                    adep = [("agd",)]
                    pgi = (c * 8 + gg8) - 2
                    if pgi >= 0:
                        adep.append(("agrp", pgi // 8, (pgi % 8) * 2 + 1))
                    S.op("sync", lambda s, c=c, gg8=gg8: s.dma_start(
                        out=sb[f"arow{gg8 % 2}"][:, :].rearrange("o (h t) -> o h t", h=8),
                        in_=dram["a_g_d"][gg8 * 8:(gg8 + 1) * 8, c * Q:(c + 1) * Q]),
                        key=("agw", c, gg8), deps=adep, step=16, sem=f"ar{gg8 % 2}")
                for h in range(H):
                    g = h // 8
                    hr = (h % 2) * 64
                    pp = h // 2
                    cdep = [("sinb",)]
                    if h % 2 == 0:
                        pv = ("cy2", c, pp - 1) if pp >= 1 else (("cy2", c - 1, 31) if c >= 1 else None)
                        cdep.append(pv)
                    S.op("tensor", lambda t, c=c, h=h, g=g, hr=hr: t.matmul(
                        ps["pyA3"][hr:hr + 64, 0:128], sb["Sinb"][:, h * 64:(h + 1) * 64],
                        sb["BC"][:, (8 + g) * TOK + c * Q:(8 + g) * TOK + (c + 1) * Q],
                        start=True, stop=True),
                        key=("cmm", c, h), deps=cdep)
                    if h % 4 == 0:
                        agdep = [("agw", c, h // 8)]
                        if h >= 1:
                            agdep.append(("epx3", c, h - 1))
                        elif c >= 1:
                            agdep.append(("epx3", c - 1, 63))
                        else:
                            agdep.append(("tve", 6))
                        S.op("tensor", lambda t, c=c, h=h: t.matmul(
                            ps["ptrf0"][:, 0:512], sb["onesrf"][0:1, :],
                            sb[f"arow{(h // 8) % 2}"][0:1, (h % 8) * Q:(h % 8 + 4) * Q],
                            start=True, stop=True),
                            key=("agrp", c, h // 4), deps=agdep)
                    epdep = [("agrp", c, h // 4)]
                    if pp >= 2:
                        epdep.append(("cy1", c, pp - 2))
                    elif c >= 1:
                        epdep.append(("cy1", c - 1, 30 + pp))
                    S.op("scalar", lambda s, hr=hr, pp=pp, h=h: s.activation(
                        sb[f"Ep{pp % 2}"][hr:hr + 64, :],
                        ps["ptrf0"][hr:hr + 64, (h % 4) * 128:(h % 4) * 128 + 128], ACT.Exp),
                        key=("epx3", c, h), deps=epdep)
                    if (P2D_OPS & 32) and h % 2 == 1:
                        ev = sb[f"evt{pp % 2}"]
                        S.op("vector", lambda v, ev=ev, pp=pp: v.tensor_tensor(ev[:, :], ps["pyA3"][:, 0:128],
                                                                               sb[f"Ep{pp % 2}"][:, :], ALU.mult),
                             key=("cy1", c, pp), deps=[("cmm", c, h), ("epx3", c, h)])
                        S.op("vector", lambda v, ev=ev, c=c, pp=pp: v.tensor_tensor(
                            sb["y"][:, pp * TOK + c * Q:pp * TOK + (c + 1) * Q],
                            sb["y"][:, pp * TOK + c * Q:pp * TOK + (c + 1) * Q], ev[:, :], ALU.add),
                            key=("cy2", c, pp), deps=[("cy1", c, pp)])

        if UPTO >= 5:
            S.run_phase("p3", p3)
        st3.close()
        st2.close()

        st4 = ExitStack()
        sbuf("wbufa", (128, 32 * 128), BF16, st4)
        sbuf("wbufb", (128, 32 * 128), BF16, st4)
        for i in range(4):
            sbuf(f"gld{i}", (128, TOK), BF16, st4)
        for i in range(2):
            sbuf(f"gsil{i}", (128, TOK), BF16, st4)
            sbuf(f"xres{i}", (128, TOK), F32, st4)
            sbuf(f"rstg{i}", (128, TOK), BF16, st4)
            sbuf(f"xstg{i}", (128, TOK), BF16, st4)
        sbuf("sqb", (128, TOK), BF16, st4)
        sbuf("rsvrow", (1, TOK), F32, st4)
        sbuf("stu", (1, TOK), F32, st4)
        sbuf("stq", (1, TOK), F32, st4)
        sbuf("stt", (1, TOK), F32, st4)
        sbuf("tmpf", (128, 512), F32, st4)
        psum("po0", (128, 512), F32, st4)
        psum("po1", (128, 512), F32, st4)
        psum("prr0", (128, 512), F32, st4)
        psum("prr1", (128, 512), F32, st4)
        psum("pst0", (128, 512), F32, st4)
        psum("pst1", (128, 512), F32, st4)
        psum("pst2", (128, 512), F32, st4)
        psum("pst3", (128, 512), F32, st4)

        def p4():
            for g in range(8):
                for ti in range(4):
                    T = g * 4 + ti
                    S.op("scalar", lambda s, T=T: s.activation(
                        sb["sqb"][:, :], sb["y"][:, T * TOK:(T + 1) * TOK], ACT.Square),
                        key=("rsq", T), deps=[("cy2", NCH - 1, 31)] if T == 0 else [("rst", T - 1, 1)])
                    for half in range(2):
                        S.op("tensor", lambda t, half=half, ti=ti: t.matmul(
                            ps[f"pst{half}"][0:1, :], sb["onescb"][:, :],
                            sb["sqb"][:, half * 512:half * 512 + 512],
                            start=(ti == 0), stop=(ti == 3)),
                            key=("rst", T, half),
                            deps=[("rsq", T)] + ([("rcp", g - 1)] if ti == 0 and g >= 1 else []))
                for half in range(2):
                    S.op("scalar", lambda s, half=half: s.activation(
                        sb["rsvrow"][0:1, half * 512:half * 512 + 512], ps[f"pst{half}"][0:1, :],
                        ACT.Sqrt, bias=EPS, scale=1.0 / 512.0),
                        key=("rsq2", g, half), deps=[("rst", g * 4 + 3, half)])
                S.op("vector", lambda v: v.reciprocal(sb["rsvrow"][:, :], sb["rsvrow"][:, :]),
                     key=("rcp", g), deps=[("rsq2", g, 0), ("rsq2", g, 1)])
                for half in range(2):
                    S.op("tensor", lambda t, g=g, half=half: t.matmul(
                        ps[f"prr{half}"][:, :], sb["onesrf"][0:1, :],
                        sb["rsvrow"][0:1, half * 512:half * 512 + 512], start=True, stop=True),
                        key=("rrp", g, half), deps=[("rcp", g)] +
                        ([("yng", (g - 1) * 4 + 3)] if g >= 1 else []))
                for ti in range(4):
                    T = g * 4 + ti
                    for half in range(2):
                        S.op("vector", lambda v, T=T, half=half: v.scalar_tensor_tensor(
                            sb["y"][:, T * TOK + half * 512:T * TOK + half * 512 + 512],
                            sb["y"][:, T * TOK + half * 512:T * TOK + half * 512 + 512],
                            sb["nw"][:, T:T + 1], ps[f"prr{half}"][:, :], ALU.mult, ALU.mult),
                            key=("ynr", T, half), deps=[("rrp", g, half)])
                    S.op("sync", lambda s, T=T: s.dma_start(out=sb[f"gld{T % 4}"][:, :], in_=dram["gate_spill"][T]),
                         key=("gll", T), deps=[("yng", T - 4)] if T >= 4 else [], step=16, sem=f"gll{T % 4}")
                    S.op("scalar", lambda s, T=T: s.activation(sb[f"gsil{T % 2}"][:, :], sb[f"gld{T % 4}"][:, :], ACT.Silu),
                         key=("gsl", T), deps=[("gll", T)] + ([("yng", T - 2)] if T >= 2 else []))
                    S.op("vector", lambda v, T=T: v.tensor_tensor(
                        sb["y"][:, T * TOK:(T + 1) * TOK], sb["y"][:, T * TOK:(T + 1) * TOK],
                        sb[f"gsil{T % 2}"][:, :], ALU.mult),
                        key=("yng", T), deps=[("gsl", T), ("ynr", T, 0), ("ynr", T, 1)])
            for dt in range(NT):
                wb = sb["wbufa" if dt % 2 == 0 else "wbufb"]
                S.op("gpsimd", lambda g, dt=dt, wb=wb: g.dma_start(
                    out=wb[:, :], in_=dram["w_out_t"][:, dt * 4096:(dt + 1) * 4096]),
                    key=("owl", dt), deps=[("omm", dt - 2, 1)] if dt >= 2 else [], step=16, sem=f"owl{dt % 2}")
                S.op("sync", lambda s, dt=dt: s.dma_start(
                    out=sb[f"xres{dt % 2}"][:, :], in_=dram["xT"][dt * 128:(dt + 1) * 128, :]),
                    key=("xrl", dt), deps=[("oev", dt - 2, 1)] if dt >= 2 else [], step=16, sem=f"xrl{dt % 2}")
                rst_b = sb[f"rstg{dt % 2}"]
                for half in range(2):
                    pso = ps[f"po{half}"]
                    deps0 = [("owl", dt), ("yng", 31)] + ([("oev", dt - 1, half)] if dt >= 1 else [])
                    for k in range(NT):
                        S.op("tensor", lambda t, pso=pso, wb=wb, k=k, half=half: t.matmul(
                            pso[:, :], wb[:, k * 128:(k + 1) * 128],
                            sb["y"][:, k * TOK + half * 512:k * TOK + half * 512 + 512],
                            start=(k == 0), stop=(k == NT - 1)),
                            key=("omm", dt, half) if k == NT - 1 else None,
                            deps=deps0 if k == 0 else [])
                    evdep = [("omm", dt, half), ("xrl", dt)]
                    if dt >= 2:
                        evdep += [("lst", dt - 2, 1, 1), ("rsp", dt - 2)]
                    S.op("vector", lambda v, dt=dt, half=half, pso=pso, rst_b=rst_b: v.tensor_tensor(
                        rst_b[:, half * 512:half * 512 + 512],
                        pso[:, :], sb[f"xres{dt % 2}"][:, half * 512:half * 512 + 512], ALU.add),
                        key=("oev", dt, half), deps=evdep)
                S.op("sync", lambda s, dt=dt, rst_b=rst_b: s.dma_start(
                    out=dram["resid_spill"][dt], in_=rst_b[:, :]),
                    key=("rsp", dt), deps=[("oev", dt, 0), ("oev", dt, 1)], step=16, sem=f"rsp{dt % 2}")
                S.op("scalar", lambda s, dt=dt, rst_b=rst_b: s.activation(
                    sb["sqb"][:, :], rst_b[:, :], ACT.Square),
                    key=("osq", dt), deps=[("oev", dt, 0), ("oev", dt, 1)] +
                    ([("lst", dt - 1, 1, 1)] if dt >= 1 else []))
                for half in range(2):
                    S.op("tensor", lambda t, dt=dt, half=half, rst_b=rst_b: t.matmul(
                        ps[f"pst{half}"][0:1, :], sb["onescb"][:, :],
                        rst_b[:, half * 512:half * 512 + 512],
                        start=(dt == 0), stop=(dt == NT - 1)),
                        key=("lst", dt, half, 0), deps=[("oev", dt, half)] + ([("rsq2", 7, half)] if dt == 0 else []))
                    S.op("tensor", lambda t, dt=dt, half=half: t.matmul(
                        ps[f"pst{2 + half}"][0:1, :], sb["onescb"][:, :],
                        sb["sqb"][:, half * 512:half * 512 + 512],
                        start=(dt == 0), stop=(dt == NT - 1)),
                        key=("lst", dt, half, 1), deps=[("osq", dt)])
            for half in range(2):
                S.op("scalar", lambda s, half=half: s.activation(
                    sb["stu"][:, half * 512:half * 512 + 512], ps[f"pst{half}"][0:1, :], ACT.Copy, scale=1.0 / D),
                    key=("mu", half), deps=[("lst", NT - 1, half, 0)])
                S.op("scalar", lambda s, half=half: s.activation(
                    sb["stq"][:, half * 512:half * 512 + 512], ps[f"pst{2 + half}"][0:1, :],
                    ACT.Copy, scale=1.0 / D),
                    key=("m2", half), deps=[("lst", NT - 1, half, 1)])
            S.op("scalar", lambda s: s.activation(sb["stt"][:, :], sb["stu"][:, :], ACT.Square),
                 key=("musq",), deps=[("mu", 0), ("mu", 1)])
            S.op("vector", lambda v: v.tensor_tensor(sb["stq"][:, :], sb["stq"][:, :], sb["stt"][:, :], ALU.subtract),
                 key=("var",), deps=[("musq",), ("m2", 0), ("m2", 1)])
            S.op("scalar", lambda s: s.activation(sb["stq"][:, :], sb["stq"][:, :], ACT.Sqrt, bias=EPS),
                 key=("rr0",), deps=[("var",)])
            S.op("vector", lambda v: v.reciprocal(sb["stq"][:, :], sb["stq"][:, :]),
                 key=("rr",), deps=[("rr0",)])
            S.op("vector", lambda v: v.tensor_tensor(sb["stu"][:, :], sb["stu"][:, :], sb["stq"][:, :], ALU.mult),
                 key=("m2r",), deps=[("rr",)])
            for half in range(2):
                S.op("tensor", lambda t, half=half: t.matmul(
                    ps[f"po{half}"][:, :], sb["onesrf"][0:1, :],
                    sb["stq"][0:1, half * 512:half * 512 + 512], start=True, stop=True),
                    key=("rrep", half), deps=[("rr",), ("m2r",), ("oev", NT - 1, half)])
                S.op("tensor", lambda t, half=half: t.matmul(
                    ps[f"prr{half}"][:, :], sb["onesrf"][0:1, :],
                    sb["stu"][0:1, half * 512:half * 512 + 512], start=True, stop=True),
                    key=("mrep", half), deps=[("m2r",), ("yng", 31)])
            for dt in range(NT):
                rst_b = sb[f"rstg{dt % 2}"]
                xst_b = sb[f"xstg{dt % 2}"]
                rldep = [("rsp", NT - 2 + (dt % 2)), ("lst", NT - 1, 1, 1)]
                if dt >= 2:
                    rldep.append(("xl0", dt - 2, 1))
                S.op("gpsimd", lambda g, dt=dt, rst_b=rst_b: g.dma_start(
                    out=rst_b[:, :], in_=dram["resid_spill"][dt]),
                    key=("rld", dt), deps=rldep, step=16, sem=f"rld{dt % 2}")
                for half in range(2):
                    S.op("vector", lambda v, half=half, rst_b=rst_b: v.tensor_tensor(
                        sb["tmpf"][:, :], rst_b[:, half * 512:half * 512 + 512],
                        ps[f"po{half}"][:, :], ALU.mult),
                        key=("xl0", dt, half), deps=[("rld", dt), ("rrep", half), ("xl1", dt, 0) if half == 1 else (("xl1", dt - 1, 1) if dt >= 1 else None)])
                    xldep = [("xl0", dt, half), ("mrep", half)]
                    if dt >= 2:
                        xldep += [("agxs", dt - 2, 3), ("xlsp", dt - 2)]
                    S.op("vector", lambda v, half=half, xst_b=xst_b: v.tensor_tensor(
                        xst_b[:, half * 512:half * 512 + 512], sb["tmpf"][:, :],
                        ps[f"prr{half}"][:, :], ALU.subtract),
                        key=("xl1", dt, half), deps=xldep)
                for j in range(4):
                    S.op("sync", lambda s, dt=dt, j=j, xst_b=xst_b: s.dma_start(
                        out=dram[f"agx_in{j}"][:, dt * 256:(dt + 1) * 256],
                        in_=xst_b[:, j * 256:j * 256 + 256]),
                        key=("agxs", dt, j), deps=[("xl1", dt, 0), ("xl1", dt, 1)], step=16, sem=f"agxs{dt % 2}")
                S.op("scalar", lambda s, dt=dt, xst_b=xst_b: s.dma_start(
                    out=dram["xl1_spill"][:, dt * TOK:(dt + 1) * TOK], in_=xst_b[:, :]),
                    key=("xlsp", dt), deps=[("xl1", dt, 0), ("xl1", dt, 1)], step=16, sem=f"xlsp{dt % 2}")
            for j in range(4):
                S.op("gpsimd", lambda g, j=j: g.collective_compute(
                    "AllGather", ALU.bypass, replica_groups=[list(range(8))],
                    ins=[dram[f"agx_in{j}"].opt()], outs=[dram[f"agx_out{j}"].opt()]),
                    key=("agx", j), deps=[("agxs", NT - 2, 3), ("agxs", NT - 1, 3)], async_=True, sem=f"ccagx{j}")

        if UPTO >= 6:
            S.run_phase("p4", p4)
        st4.close()
        stY.close()
        stA.close()

        st5 = ExitStack()
        sbuf("W", (128, 32 * NTC * 128), BF16, st5)
        sbuf("hb0", (128, NTC * 512), BF16, st5)
        sbuf("hb1", (128, NTC * 512), BF16, st5)
        sbuf("qU", (128, 512), I32, st5)
        sbuf("qF", (128, 512), F32, st5)
        sbuf("zq_g", (32, 176), I32, st5)
        sbuf("zst", (32, NTC * 128), F32, st5)
        sbuf("zs_g", (32, NTC * 128), F32, st5)
        sbuf("sc_g", (32, NTC * 128), F32, st5)
        sbuf("onesrb", (1, 128), BF16, st5)
        sbuf("qwk0", (128, 176), I32, st5)
        sb["qwk1"] = sb["qwk0"]
        psum("pf0", (128, 512), F32, st5)
        psum("pf1", (128, 512), F32, st5)
        psum("pzz0", (128, 512), F32, st5)
        psum("pzz1", (128, 512), F32, st5)
        st5x = ExitStack()
        for i in range(2):
            sbuf(f"Xb{i}", (128, NT * 512), BF16, st5x)
            sbuf(f"gstg{i}", (128, 512), BF16, st5x)
        sbuf("srowb0", (1, NTC * 128), BF16, st5x)
        sbuf("zrowb0", (1, NTC * 128), BF16, st5x)
        sb["srowb1"] = sb["srowb0"]
        sb["zrowb1"] = sb["zrowb0"]

        FBLK = [(0, 512), (512, 512), (1024, 384)]

        def prev_blk(k, bi, nb):
            idx = k * nb + bi
            return None if idx == 0 else ((idx - 1) // nb, (idx - 1) % nb)

        def dequant_gu(tag, qname, sname, zname):
            xdep = [("g", "zs")] if tag == "u" else []
            S.op("gpsimd", lambda g: g.dma_start(out=sb["zq_g"][:, :], in_=dram[zname]),
                 key=(tag, "zld"), deps=xdep, step=16, sem="fzio")
            S.op("gpsimd", lambda g: g.dma_start(out=sb["sc_g"][:, :], in_=dram[sname]),
                 key=(tag, "sld"), deps=xdep, step=16, sem="fzio")
            S.op("gpsimd", lambda g: g.memset(sb["onesrb"][:, :], 1.0), key=(tag, "ob"))
            for i in range(8):
                S.op("vector", lambda v, i=i: v.tensor_scalar(
                    sb["zst"][:, :].rearrange("p (a b) -> p b a", b=8)[:, i, :].bitcast(I32),
                    sb["zq_g"][:, :], 4 * i, 15, ALU.logical_shift_right, ALU.bitwise_and),
                    key=(tag, "zu", i), deps=[(tag, "zld"), (tag, "sld")])
            S.op("vector", lambda v: v.tensor_copy(sb["zs_g"][:, :], sb["zst"][:, :].bitcast(I32)),
                 key=(tag, "zcv"), deps=[(tag, "zu", i) for i in range(8)])
            S.op("vector", lambda v: v.tensor_tensor(sb["zs_g"][:, :], sb["zs_g"][:, :], sb["sc_g"][:, :], ALU.mult),
                 key=(tag, "zs"), deps=[(tag, "zcv"), (tag, "sld")])
            S.op("sync", lambda s: s.dma_start(out=dram["zs_gu_d"], in_=sb["zs_g"][:, :]),
                 key=(tag, "zspl"), deps=[(tag, "zs")] + ([("g", "zrl", 31)] if tag == "u" else []),
                 step=16, sem="zspl")
            for k in range(32):
                qb = sb[f"qwk{k % 2}"]
                qdep = [(tag, "u", k - 1, 2, 7)] if k >= 1 else (
                    [("g", "u", 31, 2, 7), ("g", "wB", (31, 2))] if tag == "u" else [])
                S.op("gpsimd", lambda g, qb=qb, k=k: g.dma_start(
                    out=qb[:, :], in_=dram[qname][:, k * 176:(k + 1) * 176]),
                    key=(tag, "qld", k), deps=qdep, step=16, sem="qld")
                S.op("gpsimd", lambda g, k=k: g.dma_start(
                    out=sb[f"srowb{k % 2}"][:, :], in_=dram[sname][k:k + 1, :]),
                    key=(tag, "srl", k), deps=([(tag, "sr", k - 1, 2)] if k >= 1 else
                                               ([("g", "sr", 31, 2)] if tag == "u" else [])),
                    step=16, sem="srl")
                S.op("gpsimd", lambda g, k=k: g.dma_start(
                    out=sb[f"zrowb{k % 2}"][:, :], in_=dram["zs_gu_d"][k:k + 1, :]),
                    key=(tag, "zrl", k), deps=[(tag, "zspl")] + ([(tag, "zr", k - 1, 2)] if k >= 1 else
                                               ([("g", "zr", 31, 2)] if tag == "u" else [])),
                    step=16, sem="zrl")
                for bi, (off, width) in enumerate(FBLK):
                    woff, wwidth = off // 8, width // 8
                    pb = prev_blk(k, bi, 3)
                    for i in range(8):
                        S.op("vector", lambda v, qb=qb, woff=woff, wwidth=wwidth, width=width, i=i: v.tensor_scalar(
                            sb["qU"][:, 0:width].rearrange("p (a b) -> p b a", b=8)[:, i, :],
                            qb[:, woff:woff + wwidth], 4 * i, 15,
                            ALU.logical_shift_right, ALU.bitwise_and),
                            key=(tag, "u", k, bi, i), deps=[(tag, "qld", k)] +
                            ([(tag, "wA", pb)] if pb is not None and i == 0 else []))
                    S.op("vector", lambda v, width=width: v.tensor_copy(sb["qF"][:, 0:width], sb["qU"][:, 0:width]),
                         key=(tag, "cv", k, bi), deps=[(tag, "u", k, bi, i) for i in range(8)] +
                         ([(tag, "wB", pb)] if pb is not None else []))
                    S.op("tensor", lambda t, k=k, off=off, width=width: t.matmul(
                        ps["pzz0"][:, 0:width], sb["onesrb"][0:1, :],
                        sb[f"srowb{k % 2}"][0:1, off:off + width],
                        start=True, stop=True),
                        key=(tag, "sr", k, bi), deps=[(tag, "srl", k), (tag, "ob")] +
                        ([(tag, "wA", pb)] if pb is not None else
                         ([("g", "mm", 15, NTC - 1)] if tag == "u" else [])))
                    S.op("tensor", lambda t, k=k, off=off, width=width: t.matmul(
                        ps["pzz1"][:, 0:width], sb["onesrb"][0:1, :],
                        sb[f"zrowb{k % 2}"][0:1, off:off + width],
                        start=True, stop=True),
                        key=(tag, "zr", k, bi), deps=[(tag, "zrl", k)] +
                        ([(tag, "wB", pb)] if pb is not None else []))
                    S.op("vector", lambda v, width=width: v.tensor_tensor(
                        sb["qF"][:, 0:width], sb["qF"][:, 0:width], ps["pzz0"][:, 0:width], ALU.mult),
                        key=(tag, "wA", (k, bi)), deps=[(tag, "cv", k, bi), (tag, "sr", k, bi)])
                    S.op("vector", lambda v, k=k, off=off, width=width: v.tensor_tensor(
                        sb["W"][:, k * NTC * 128 + off:k * NTC * 128 + off + width],
                        sb["qF"][:, 0:width], ps["pzz1"][:, 0:width], ALU.subtract),
                        key=(tag, "wB", (k, bi)), deps=[(tag, "wA", (k, bi)), (tag, "zr", k, bi)])

        def x_load(tag, tb, buf):
            r = tb // 2
            for jq in range(2):
                j = (tb % 2) * 2 + jq
                dep = [("agx", j)]
                if tb >= 2:
                    dep.append((tag, "mm", tb - 2, NTC - 1))
                elif tag == "u":
                    dep.append(("g", "mm", 14 + tb, NTC - 1))
                S.op("sync", lambda s, buf=buf, r=r, j=j, jq=jq: s.dma_start(
                    out=buf[:, :].rearrange("p (k q t) -> p k q t", k=NT, q=2)[:, :, jq, :],
                    in_=dram[f"agx_out{j}"][r * 128:(r + 1) * 128, :].rearrange("p (k t) -> p k t", k=NT)),
                    key=(tag, "xl", tb, jq), deps=dep, step=16, sem=f"xb{tb % 2}")

        def p5a():
            dequant_gu("g", "gq", "gsc", "gzq")
            for tb in range(16):
                buf = sb[f"Xb{tb % 2}"]
                x_load("g", tb, buf)
                for nt in range(NTC):
                    gg = tb * NTC + nt
                    pf = ps[f"pf{gg % 2}"]
                    deps0 = [("g", "xl", tb, 0), ("g", "xl", tb, 1), ("g", "wB", (31, 2))]
                    if gg >= 2:
                        deps0.append(("g", "sil", (gg - 2) // NTC, (gg - 2) % NTC))
                    for k in range(NT):
                        S.op("tensor", lambda t, pf=pf, k=k, nt=nt, buf=buf: t.matmul(
                            pf[:, :], sb["W"][:, k * NTC * 128 + nt * 128:k * NTC * 128 + (nt + 1) * 128],
                            buf[:, k * 512:(k + 1) * 512], start=(k == 0), stop=(k == NT - 1)),
                            key=("g", "mm", tb, nt) if k == NT - 1 else None,
                            deps=deps0 if k == 0 else [])
                    sdep = [("g", "mm", tb, nt)]
                    if gg >= 2:
                        sdep.append(("g", "sp", (gg - 2) // NTC, (gg - 2) % NTC))
                    S.op("scalar", lambda s, nt=nt, pf=pf, gg=gg: s.activation(
                        sb[f"gstg{gg % 2}"][:, :], pf[:, :], ACT.Silu),
                        key=("g", "sil", tb, nt), deps=sdep)
                    S.op("scalar", lambda s, tb=tb, nt=nt, gg=gg: s.dma_start(
                        out=dram["g_spill"][tb][:, nt * 512:(nt + 1) * 512], in_=sb[f"gstg{gg % 2}"][:, :]),
                        key=("g", "sp", tb, nt), deps=[("g", "sil", tb, nt)], step=16, sem=f"gsl{gg % 2}")

        def p5b():
            dequant_gu("u", "uq", "usc", "uzq")
            for tb in range(16):
                buf = sb[f"Xb{tb % 2}"]
                hbuf = sb[f"hb{tb % 2}"]
                x_load("u", tb, buf)
                S.op("gpsimd", lambda g, tb=tb, hbuf=hbuf: g.dma_start(out=hbuf[:, :], in_=dram["g_spill"][tb]),
                     key=("u", "gl", tb), deps=[("g", "sp", tb, NTC - 1)] + ([("u", "hsp", tb - 2)] if tb >= 2 else []),
                     step=16, sem=f"hgl{tb % 2}")
                for nt in range(NTC):
                    gg = tb * NTC + nt
                    pf = ps[f"pf{gg % 2}"]
                    deps0 = [("u", "xl", tb, 0), ("u", "xl", tb, 1), ("u", "wB", (31, 2))]
                    if gg >= 2:
                        deps0.append(("u", "hh", (gg - 2) // NTC, (gg - 2) % NTC))
                    for k in range(NT):
                        S.op("tensor", lambda t, pf=pf, k=k, nt=nt, buf=buf: t.matmul(
                            pf[:, :], sb["W"][:, k * NTC * 128 + nt * 128:k * NTC * 128 + (nt + 1) * 128],
                            buf[:, k * 512:(k + 1) * 512], start=(k == 0), stop=(k == NT - 1)),
                            key=("u", "mm", tb, nt) if k == NT - 1 else None,
                            deps=deps0 if k == 0 else [])
                    S.op("vector", lambda v, nt=nt, pf=pf, hbuf=hbuf: v.tensor_tensor(
                        hbuf[:, nt * 512:(nt + 1) * 512], hbuf[:, nt * 512:(nt + 1) * 512],
                        pf[:, :], ALU.mult),
                        key=("u", "hh", tb, nt), deps=[("u", "mm", tb, nt), ("u", "gl", tb)])
                S.op("scalar", lambda s, tb=tb, hbuf=hbuf: s.dma_start(out=dram["h_spill"][tb], in_=hbuf[:, :]),
                     key=("u", "hsp", tb), deps=[("u", "hh", tb, nt) for nt in range(NTC)], step=16, sem=f"hsp{tb % 2}")

        def p5c():
            S.op("gpsimd", lambda g: g.dma_start(out=sb["zq_d"][:, :], in_=dram["dzq"]),
                 key=("d", "zld"), step=16, sem="fzio")
            S.op("gpsimd", lambda g: g.dma_start(out=sb["sc_d"][:, :], in_=dram["dsc"]),
                 key=("d", "sld"), step=16, sem="fzio")
            for i in range(8):
                S.op("vector", lambda v, i=i: v.tensor_scalar(
                    sb["zstd"][:, :].rearrange("p (a b) -> p b a", b=8)[:, i, :].bitcast(I32),
                    sb["zq_d"][:, :], 4 * i, 15, ALU.logical_shift_right, ALU.bitwise_and),
                    key=("d", "zu", i), deps=[("d", "zld"), ("d", "sld")])
            S.op("vector", lambda v: v.tensor_copy(sb["zs_d"][:, :], sb["zstd"][:, :].bitcast(I32)),
                 key=("d", "zcv"), deps=[("d", "zu", i) for i in range(8)])
            S.op("vector", lambda v: v.tensor_tensor(sb["zs_d"][:, :], sb["zs_d"][:, :], sb["sc_d"][:, :], ALU.mult),
                 key=("d", "zs"), deps=[("d", "zcv"), ("d", "sld")])
            S.op("sync", lambda s: s.dma_start(out=dram["zs_d_d"], in_=sb["zs_d"][:, :]),
                 key=("d", "zspl"), deps=[("d", "zs")], step=16, sem="zspl")
            for k in range(NTC):
                qb = sb[f"qdk{k % 2}"]
                S.op("gpsimd", lambda g, qb=qb, k=k: g.dma_start(
                    out=qb[:, :], in_=dram["dq"][:, k * 512:(k + 1) * 512]),
                    key=("d", "qld", k), deps=[("d", "u", k - 2, 7, 7)] if k >= 2 else [], step=16, sem=f"qdk{k % 2}")
                S.op("gpsimd", lambda g, k=k: g.dma_start(
                    out=sb[f"srowd{k % 2}"][:, :], in_=dram["dsc"][k:k + 1, :]),
                    key=("d", "srl", k), deps=([("d", "sr", k - 1, 7)] if k >= 1 else []), step=16, sem="srl")
                S.op("gpsimd", lambda g, k=k: g.dma_start(
                    out=sb[f"zrowd{k % 2}"][:, :], in_=dram["zs_d_d"][k:k + 1, :]),
                    key=("d", "zrl", k), deps=[("d", "zspl")] + ([("d", "zr", k - 1, 7)] if k >= 1 else []), step=16, sem="zrl")
                for bi in range(8):
                    off = bi * 512
                    pb = prev_blk(k, bi, 8)
                    for i in range(8):
                        S.op("vector", lambda v, qb=qb, off=off, i=i: v.tensor_scalar(
                            sb["qU"][:, 0:512].rearrange("p (a b) -> p b a", b=8)[:, i, :],
                            qb[:, off // 8:off // 8 + 64], 4 * i, 15,
                            ALU.logical_shift_right, ALU.bitwise_and),
                            key=("d", "u", k, bi, i), deps=[("d", "qld", k)] +
                            ([("d", "wA", pb)] if pb is not None and i == 0 else []))
                    S.op("vector", lambda v: v.tensor_copy(sb["qF"][:, 0:512], sb["qU"][:, 0:512]),
                         key=("d", "cv", k, bi), deps=[("d", "u", k, bi, i) for i in range(8)] +
                         ([("d", "wB", pb)] if pb is not None else []))
                    S.op("tensor", lambda t, k=k, off=off: t.matmul(
                        ps["pzz0"][:, :], sb["onesrb"][0:1, :],
                        sb[f"srowd{k % 2}"][0:1, off:off + 512],
                        start=True, stop=True),
                        key=("d", "sr", k, bi), deps=[("d", "srl", k)] + ([("d", "wA", pb)] if pb is not None else []))
                    S.op("tensor", lambda t, k=k, off=off: t.matmul(
                        ps["pzz1"][:, :], sb["onesrb"][0:1, :],
                        sb[f"zrowd{k % 2}"][0:1, off:off + 512],
                        start=True, stop=True),
                        key=("d", "zr", k, bi), deps=[("d", "zrl", k)] + ([("d", "wB", pb)] if pb is not None else []))
                    S.op("vector", lambda v: v.tensor_tensor(sb["qF"][:, 0:512], sb["qF"][:, 0:512],
                                                             ps["pzz0"][:, :], ALU.mult),
                         key=("d", "wA", (k, bi)), deps=[("d", "cv", k, bi), ("d", "sr", k, bi)])
                    S.op("vector", lambda v, k=k, off=off: v.tensor_tensor(
                        sb["W"][:, k * 4096 + off:k * 4096 + off + 512], sb["qF"][:, 0:512],
                        ps["pzz1"][:, :], ALU.subtract),
                        key=("d", "wB", (k, bi)), deps=[("d", "wA", (k, bi)), ("d", "zr", k, bi)])
            for dh in range(2):
                for tb in range(16):
                    hbuf = sb[f"hb{tb % 2}"]
                    hldep = [("u", "hsp", tb)]
                    if tb >= 2:
                        hldep.append(("d", "mm", dh, tb - 2, 15))
                    elif dh >= 1:
                        hldep.append(("d", "mm", dh - 1, 14 + tb, 15))
                    S.op("scalar", lambda g, tb=tb, hbuf=hbuf, dh=dh: g.dma_start(
                        out=hbuf[:, :], in_=dram["h_spill"][tb]),
                        key=("d", "hl", dh, tb), deps=hldep, step=16, sem=f"hl{tb % 2}")
                    for dtl in range(16):
                        dto = dh * 16 + dtl
                        pf = ps[f"pf{dtl % 2}"]
                        deps0 = [("d", "hl", dh, tb), ("d", "wB", (NTC - 1, 7))]
                        pv = (dh, tb, dtl - 2) if dtl >= 2 else ((dh, tb - 1, 14 + dtl) if tb >= 1 else ((dh - 1, 15, 14 + dtl) if dh >= 1 else None))
                        if pv is not None:
                            deps0.append(("d", "pe", pv[0], pv[1], pv[2]))
                        for k in range(NTC):
                            S.op("tensor", lambda t, pf=pf, k=k, dto=dto, hbuf=hbuf: t.matmul(
                                pf[:, :], sb["W"][:, k * 4096 + dto * 128:k * 4096 + (dto + 1) * 128],
                                hbuf[:, k * 512:(k + 1) * 512], start=(k == 0), stop=(k == NTC - 1)),
                                key=("d", "mm", dh, tb, dtl) if k == NTC - 1 else None,
                                deps=deps0 if k == 0 else [])
                        pedep = [("d", "mm", dh, tb, dtl)]
                        if pv is not None:
                            pedep.append(("d", "pev", pv[0], pv[1], pv[2]))
                        S.op("vector", lambda v, pf=pf, dtl=dtl: v.tensor_copy(
                            sb[f"pt{dtl % 2}"][:, :], pf[:, :]),
                            key=("d", "pe", dh, tb, dtl), deps=pedep)
                        S.op("sync", lambda s, dh=dh, tb=tb, dtl=dtl: s.dma_start(
                            out=dram[f"rs_in{dh}"][(tb // 2) * 2048 + dtl * 128:(tb // 2) * 2048 + (dtl + 1) * 128,
                                                   (tb % 2) * 512:(tb % 2) * 512 + 512],
                            in_=sb[f"pt{dtl % 2}"][:, :]),
                            key=("d", "pev", dh, tb, dtl), deps=[("d", "pe", dh, tb, dtl)], step=16, sem=f"pev{dtl % 2}")
                S.op("gpsimd", lambda g, dh=dh: g.collective_compute(
                    "ReduceScatter", ALU.add, replica_groups=[list(range(8))],
                    ins=[dram[f"rs_in{dh}"].opt()], outs=[dram[f"rs_out{dh}"].opt()]),
                    key=("rs", dh),
                    deps=[("d", "pev", dh, tb, dtl) for tb in range(16) for dtl in range(16)], async_=True, sem=f"ccrs{dh}")

        if UPTO >= 7:
            S.run_phase("p5a", p5a)
        if UPTO >= 8:
            S.run_phase("p5b", p5b)
        st5x.close()
        st5c = ExitStack()
        sbuf("qdk0", (128, 512), I32, st5c)
        sbuf("qdk1", (128, 512), I32, st5c)
        sbuf("srowd0", (1, D), BF16, st5c)
        sbuf("zrowd0", (1, D), BF16, st5c)
        sb["srowd1"] = sb["srowd0"]
        sb["zrowd1"] = sb["zrowd0"]
        sbuf("sc_d", (NTC, D), F32, st5c)
        sbuf("zs_d", (NTC, D), F32, st5c)
        sbuf("zstd", (NTC, D), F32, st5c)
        sbuf("zq_d", (NTC, 512), I32, st5c)
        sbuf("pt0", (128, 512), BF16, st5c)
        sbuf("pt1", (128, 512), BF16, st5c)
        if UPTO >= 9:
            S.run_phase("p5c", p5c)
        st5c.close()
        st5.close()

        st6 = ExitStack()
        sbuf("r2", (128, NT * TOK), BF16, st6)
        sbuf("sq6", (128, TOK), BF16, st6)
        sbuf("stu6", (1, TOK), F32, st6)
        sbuf("stq6", (1, TOK), F32, st6)
        sbuf("stt6", (1, TOK), F32, st6)
        sbuf("tf6", (128, 512), F32, st6)
        for i in range(2):
            sbuf(f"rsl{i}", (128, TOK), BF16, st6)
            sbuf(f"xo{i}", (128, TOK), BF16, st6)
            sbuf(f"of{i}", (128, TOK), F32, st6)
        psum("p60", (128, 512), F32, st6)
        psum("p61", (128, 512), F32, st6)
        psum("p62", (128, 512), F32, st6)
        psum("p63", (128, 512), F32, st6)
        psum("p6r0", (128, 512), F32, st6)
        psum("p6r1", (128, 512), F32, st6)
        psum("p6m0", (128, 512), F32, st6)
        psum("p6m1", (128, 512), F32, st6)

        def p6():
            for k in range(NT):
                dh = k // 16
                row = (k % 16) * 128
                S.op("gpsimd", lambda g, k=k, dh=dh, row=row: g.dma_start(
                    out=sb[f"rsl{k % 2}"][:, :], in_=dram[f"rs_out{dh}"][row:row + 128, :]),
                    key=("rsl", k), deps=[("rs", dh)] + ([("r2", k - 2)] if k >= 2 else []), step=16, sem=f"rsl{k % 2}")
                S.op("sync", lambda s, k=k: s.dma_start(
                    out=sb[f"xo{k % 2}"][:, :], in_=dram["xl1_spill"][:, k * TOK:(k + 1) * TOK]),
                    key=("xol", k), deps=[("xlsp", k)] + ([("r2", k - 2)] if k >= 2 else []), step=16, sem=f"xol{k % 2}")
                S.op("vector", lambda v, k=k: v.tensor_tensor(
                    sb["r2"][:, k * TOK:(k + 1) * TOK], sb[f"rsl{k % 2}"][:, :], sb[f"xo{k % 2}"][:, :], ALU.add),
                    key=("r2", k), deps=[("rsl", k), ("xol", k)])
                S.op("scalar", lambda s, k=k: s.activation(sb["sq6"][:, :], sb["r2"][:, k * TOK:(k + 1) * TOK], ACT.Square),
                     key=("sq6", k), deps=[("r2", k)] + ([("st6", k - 1, 1, 1)] if k >= 1 else []))
                for half in range(2):
                    S.op("tensor", lambda t, k=k, half=half: t.matmul(
                        ps[f"p6{half}"][0:1, :], sb["onescb"][:, :],
                        sb["r2"][:, k * TOK + half * 512:k * TOK + half * 512 + 512],
                        start=(k == 0), stop=(k == NT - 1)),
                        key=("st6", k, half, 0), deps=[("r2", k)])
                    S.op("tensor", lambda t, k=k, half=half: t.matmul(
                        ps[f"p6{2 + half}"][0:1, :], sb["onescb"][:, :],
                        sb["sq6"][:, half * 512:half * 512 + 512],
                        start=(k == 0), stop=(k == NT - 1)),
                        key=("st6", k, half, 1), deps=[("sq6", k)])
            for half in range(2):
                S.op("scalar", lambda s, half=half: s.activation(
                    sb["stu6"][:, half * 512:half * 512 + 512], ps[f"p6{half}"][0:1, :], ACT.Copy, scale=1.0 / D),
                    key=("mu6", half), deps=[("st6", NT - 1, half, 0)])
                S.op("scalar", lambda s, half=half: s.activation(
                    sb["stq6"][:, half * 512:half * 512 + 512], ps[f"p6{2 + half}"][0:1, :],
                    ACT.Copy, scale=1.0 / D),
                    key=("m26", half), deps=[("st6", NT - 1, half, 1)])
            S.op("scalar", lambda s: s.activation(sb["stt6"][:, :], sb["stu6"][:, :], ACT.Square),
                 key=("musq6",), deps=[("mu6", 0), ("mu6", 1)])
            S.op("vector", lambda v: v.tensor_tensor(sb["stq6"][:, :], sb["stq6"][:, :], sb["stt6"][:, :], ALU.subtract),
                 key=("var6",), deps=[("musq6",), ("m26", 0), ("m26", 1)])
            S.op("scalar", lambda s: s.activation(sb["stq6"][:, :], sb["stq6"][:, :], ACT.Sqrt, bias=EPS),
                 key=("rr60",), deps=[("var6",)])
            S.op("vector", lambda v: v.reciprocal(sb["stq6"][:, :], sb["stq6"][:, :]),
                 key=("rr6",), deps=[("rr60",)])
            S.op("vector", lambda v: v.tensor_tensor(sb["stu6"][:, :], sb["stu6"][:, :], sb["stq6"][:, :], ALU.mult),
                 key=("m2r6",), deps=[("rr6",)])
            for half in range(2):
                S.op("tensor", lambda t, half=half: t.matmul(
                    ps[f"p6r{half}"][:, :], sb["onesrf"][0:1, :],
                    sb["stq6"][0:1, half * 512:half * 512 + 512], start=True, stop=True),
                    key=("rrep6", half), deps=[("rr6",), ("m2r6",)])
                S.op("tensor", lambda t, half=half: t.matmul(
                    ps[f"p6m{half}"][:, :], sb["onesrf"][0:1, :],
                    sb["stu6"][0:1, half * 512:half * 512 + 512], start=True, stop=True),
                    key=("mrep6", half), deps=[("m2r6",)])
            for k in range(NT):
                for half in range(2):
                    S.op("vector", lambda v, k=k, half=half: v.tensor_tensor(
                        sb["tf6"][:, :], sb["r2"][:, k * TOK + half * 512:k * TOK + half * 512 + 512],
                        ps[f"p6r{half}"][:, :], ALU.mult),
                        key=("o0", k, half), deps=[("rrep6", half), ("o1", k, 0) if half == 1 else (("o1", k - 1, 1) if k >= 1 else None)])
                    S.op("vector", lambda v, k=k, half=half: v.tensor_tensor(
                        sb[f"of{k % 2}"][:, half * 512:half * 512 + 512], sb["tf6"][:, :],
                        ps[f"p6m{half}"][:, :], ALU.subtract),
                        key=("o1", k, half), deps=[("o0", k, half), ("mrep6", half)] +
                        ([("ow", k - 2)] if k >= 2 else []))
                S.op("sync", lambda s, k=k: s.dma_start(out=dram["y_out"][k * 128:(k + 1) * 128, :],
                                                        in_=sb[f"of{k % 2}"][:, :]),
                     key=("ow", k), deps=[("o1", k, 0), ("o1", k, 1)], step=16, sem=f"ow{k % 2}")
            S.op("sync", lambda s: None, key=None, deps=[("ow", NT - 2), ("ow", NT - 1)])

        if UPTO >= 10:
            S.run_phase("p6", p6)
        st6.close()
        if UPTO < 10:
            def ptail():
                for k in range(NT):
                    if UPTO in (4, 5) and "y" in sb:
                        S.op("gpsimd", lambda g, k=k: g.dma_start(
                            out=dram["y_out"][k * 128:(k + 1) * 128, :], in_=sb["y"][:, k * TOK:(k + 1) * TOK]),
                            key=("dow", k), step=16, sem="dow")
                    elif UPTO == 6:
                        S.op("gpsimd", lambda g, k=k: g.dma_start(
                            out=dram["y_out"][k * 128:(k + 1) * 128, :],
                            in_=dram["xl1_spill"][:, k * TOK:(k + 1) * TOK]),
                            key=("dow", k), step=16, sem="dow")
                    elif UPTO in (7, 8):
                        src = dram["g_spill" if UPTO == 7 else "h_spill"]
                        S.op("gpsimd", lambda g, k=k, src=src: g.dma_start(
                            out=dram["y_out"][k * 128:(k + 1) * 128, :],
                            in_=src[k % 16][:, (k // 16) * 1024:(k // 16) * 1024 + 1024]),
                            key=("dow", k), step=16, sem="dow")
                    elif UPTO == 9:
                        S.op("gpsimd", lambda g, k=k: g.dma_start(
                            out=dram["y_out"][k * 128:(k + 1) * 128, :],
                            in_=dram[f"rs_out{k // 16}"][(k % 16) * 128:(k % 16) * 128 + 128, :]),
                            key=("dow", k), step=16, sem="dow")
                    else:
                        S.op("sync", lambda s, k=k: s.dma_start(
                            out=dram["y_out"][k * 128:(k + 1) * 128, :], in_=dram["xT"][k * 128:(k + 1) * 128, :]),
                            key=("dow", k), step=16, sem="dow")
                eng = "sync" if UPTO not in (4, 5, 6, 7, 8, 9) else "gpsimd"
                S.op(eng, lambda s: None, key=None, deps=[("dow", k) for k in range(NT)])
            S.run_phase("ptail", ptail)
    return nc



def _tile_lhsT(W):
    O, Din = W.shape
    kt = Din // 128
    blocks = []
    for o0 in range(0, O, 128):
        ow = min(128, O - o0)
        blk = W[o0:o0 + ow, :].reshape(ow, kt, 128).transpose(2, 1, 0).reshape(128, kt * ow)
        blocks.append(blk)
    return np.ascontiguousarray(np.concatenate(blocks, axis=1))


_NC_CACHE = {}


def kernel(**inp):
    if ("nc", UPTO, P2D_CHUNKS, P2D_HEADS, P2D_PRE, P2D_OPS) not in _NC_CACHE:
        _NC_CACHE[("nc", UPTO, P2D_CHUNKS, P2D_HEADS, P2D_PRE, P2D_OPS)] = build_nc()
    nc = _NC_CACHE[("nc", UPTO, P2D_CHUNKS, P2D_HEADS, P2D_PRE, P2D_OPS)]

    x = np.asarray(inp["L0_attn_hidden_states"], dtype=np.float32)
    w_in = np.asarray(inp["L0_attn_in_proj_w"], dtype=np.float32)
    w_out = np.asarray(inp["L0_attn_out_proj_w"], dtype=np.float32)
    conv_w = np.asarray(inp["L0_attn_conv_weight"], dtype=np.float32)
    conv_b = np.asarray(inp["L0_attn_conv_bias"], dtype=np.float32)
    dt_bias = np.asarray(inp["L0_attn_dt_bias"], dtype=np.float32)
    a_log = np.asarray(inp["L0_attn_A_log"], dtype=np.float32)
    Dv = np.asarray(inp["L0_attn_D"], dtype=np.float32)
    nw = np.asarray(inp["L0_attn_norm_weight"], dtype=np.float32)

    w_in_t = _tile_lhsT(w_in)
    w_out_t = _tile_lhsT(w_out)
    conv_w_t = np.ascontiguousarray(conv_w.reshape(48, 128, 4).transpose(1, 0, 2).reshape(128, 192))
    conv_b_t = np.ascontiguousarray(conv_b.reshape(48, 128).T)
    d_exp = np.ascontiguousarray(np.repeat(Dv, PH).reshape(32, 128).T)
    nw_t = np.ascontiguousarray(nw.reshape(32, 128).T)

    def pad_cols(a, w):
        out = np.zeros((a.shape[0], w), dtype=a.dtype)
        out[:, :a.shape[1]] = a
        return out
    gq_full = pad_cols(np.asarray(inp["L0_ffn_gate_qweight"]), FIP // 8)
    uq_full = pad_cols(np.asarray(inp["L0_ffn_up_qweight"]), FIP // 8)
    gs_full = pad_cols(np.asarray(inp["L0_ffn_gate_scales"], dtype=np.float32), FIP)
    us_full = pad_cols(np.asarray(inp["L0_ffn_up_scales"], dtype=np.float32), FIP)
    gz_full = pad_cols(np.asarray(inp["L0_ffn_gate_qzeros"]), FIP // 8)
    uz_full = pad_cols(np.asarray(inp["L0_ffn_up_qzeros"]), FIP // 8)
    dq_in = np.asarray(inp["L0_ffn_down_qweight"])
    dq_full = np.zeros((FIP, 512), dtype=dq_in.dtype)
    dq_full[:FI] = dq_in
    ds_in = np.asarray(inp["L0_ffn_down_scales"], dtype=np.float32)
    ds_full = np.zeros((88, 4096), dtype=np.float32)
    ds_full[:86] = ds_in
    dz_in = np.asarray(inp["L0_ffn_down_qzeros"])
    dz_full = np.zeros((88, 512), dtype=dz_in.dtype)
    dz_full[:86] = dz_in

    in_maps = []
    for c in range(8):
        b = c // 4
        t0 = (c % 4) * TOK
        xT = np.ascontiguousarray(x[b, t0:t0 + TOK, :].T)
        m_prev = np.full((128, 1), 0.0 if c % 4 == 0 else 1.0, np.float32)
        prevsel = np.zeros((128, 8), np.float32)
        if c % 4 != 0:
            prevsel[:, c - 1] = 1.0
        msel = np.zeros((64, 8), np.float32)
        macc = np.zeros((128, 8), np.float32)
        for j in range(8):
            if j // 4 == c // 4 and j < c:
                msel[:, j] = 1.0
                macc[:, j] = 1.0
        w0, w1 = 176 * c, 176 * (c + 1)
        gq = np.ascontiguousarray(gq_full[:, w0:w1].reshape(32, 128, 176).transpose(1, 0, 2).reshape(128, 32 * 176))
        uq = np.ascontiguousarray(uq_full[:, w0:w1].reshape(32, 128, 176).transpose(1, 0, 2).reshape(128, 32 * 176))
        gsc = np.ascontiguousarray(gs_full[:, 1408 * c:1408 * (c + 1)])
        usc = np.ascontiguousarray(us_full[:, 1408 * c:1408 * (c + 1)])
        gzq = np.ascontiguousarray(gz_full[:, w0:w1])
        uzq = np.ascontiguousarray(uz_full[:, w0:w1])
        dq = np.ascontiguousarray(dq_full[1408 * c:1408 * (c + 1), :].reshape(NTC, 128, 512)
                                  .transpose(1, 0, 2).reshape(128, NTC * 512))
        dsc = np.ascontiguousarray(ds_full[11 * c:11 * (c + 1), :])
        dzq = np.ascontiguousarray(dz_full[11 * c:11 * (c + 1), :])
        in_maps.append(dict(
            xT=xT, w_in_t=w_in_t, w_out_t=w_out_t, conv_w=conv_w_t, conv_b=conv_b_t,
            dt_bias=dt_bias.reshape(64, 1), a_log=a_log.reshape(64, 1), d_exp=d_exp,
            norm_w=nw_t, m_prev=m_prev, prevsel=prevsel, msel=msel, macc=macc,
            gq=gq, uq=uq, gsc=gsc, usc=usc, gzq=gzq, uzq=uzq, dq=dq, dsc=dsc, dzq=dzq))

    res = run_bass_kernel_spmd(nc, in_maps, core_ids=list(range(8)))
    out = np.empty((B, L, D), dtype=np.float32)
    for c in range(8):
        out[c // 4, (c % 4) * TOK:(c % 4 + 1) * TOK, :] = res.results[c]["y_out"].T
    return out
